# version 07
# baseline (speedup 1.0000x reference)
import contextlib
import math
import os as _os
from types import SimpleNamespace
import numpy as np
import concourse.bass as bass
import concourse.mybir as mybir
from concourse.bass_utils import run_bass_kernel_spmd

F32 = mybir.dt.float32
BF16 = mybir.dt.bfloat16
FP16 = mybir.dt.float16
I32 = mybir.dt.int32
AF = mybir.ActivationFunctionType
ALU = mybir.AluOpType
AX = mybir.AxisListType

D = 1024
HD = 64
NH = 16
E = 8
MH = 4096
L1H = 8
L1D = 128
CD = 768
CTXT = 4096
FH = 4096
QSCALE = HD ** -0.5
ASCALE = 1.5 * (L1D ** -0.5)
LNEPS = 1e-5
LN2 = math.log(2.0)

ENGS = ("tensor", "vector", "scalar", "sync", "gpsimd")


class Ctr:
    def __init__(self, sem):
        self.sem = sem
        self.n = 0


class Op:
    __slots__ = ("eng", "fn", "waits", "ctr", "k", "tok", "idx", "ph")

    def __init__(self, eng, fn, waits, ctr, k, idx, ph=None):
        self.eng, self.fn, self.waits, self.ctr, self.k, self.idx = \
            eng, fn, list(waits), ctr, k, idx
        self.tok = None
        self.ph = ph


class Hard:

    def __init__(self, op):
        self.op = op


class Phase:
    def __init__(self, c, name):
        self.c = c
        self.name = name
        self.ops = []

    def op(self, eng, fn, waits=(), ctr=None, k=1):
        o = Op(eng, fn, waits, ctr, k, len(self.ops), self)
        self.ops.append(o)
        return o

    def dma(self, eng, fn, waits=(), ctr=None):
        return self.op(eng, fn, waits, ctr, 16)

    def emit(self):
        c = self.c
        for o in self.ops:
            if o.ctr is not None:
                o.ctr.n += o.k
                o.tok = o.ctr.n
        for eng_name in ENGS:
            ops = [o for o in self.ops if o.eng == eng_name]
            if not ops:
                continue

            def body(e, ops=ops, eng_name=eng_name):
                hw = {}
                for o in ops:
                    for w in o.waits:
                        if w is None:
                            continue
                        if isinstance(w, Hard):
                            w = w.op
                            ctr, tok = w.ctr, w.tok
                        elif isinstance(w, Op):
                            if w.eng == eng_name:
                                assert w.ph is not self or w.idx < o.idx, \
                                    f"{self.name}: same-engine forward wait"
                                continue
                            ctr, tok = w.ctr, w.tok
                        else:
                            ctr, tok = w
                        if ctr is None or tok is None:
                            continue
                        if hw.get(id(ctr), -1) < tok:
                            e.wait_ge(ctr.sem, tok)
                            hw[id(ctr)] = tok
                    last = o.fn(e)
                    if o.ctr is not None:
                        assert last is not None, f"{self.name}: op fn returned None"
                        last.then_inc(o.ctr.sem, o.k)
            getattr(c.block, eng_name)(body)


def selfsync(c, engname, eng, instr):
    ctr = c.self_ctrs[engname]
    instr.then_inc(ctr.sem, 1)
    ctr.n += 1
    eng.wait_ge(ctr.sem, ctr.n)


def barrier(c, name):
    snap = [(ctr, ctr.n) for ctr in c.ctrs.values() if ctr.n > 0]
    for eng_name in ENGS:
        def body(e, snap=snap):
            for ctr, n in snap:
                e.wait_ge(ctr.sem, n)
        getattr(c.block, eng_name)(body)



def p0_consts(c):
    ph = Phase(c, "p0")
    o_iota = ph.op("gpsimd", lambda g: g.iota(
        c.iotav[:], [[1, 128]], base=0, channel_multiplier=-1,
        allow_small_or_imprecise_dtypes=True), ctr=c.gq)
    o_i2 = ph.op("gpsimd", lambda g: g.iota(
        c.ecap_col[:, 0:1], [[1, 1]], base=-1, channel_multiplier=c.CAP,
        allow_small_or_imprecise_dtypes=True), ctr=c.gq)
    o_i3 = ph.op("gpsimd", lambda g: g.iota(
        c.ecapmax[:, 0:E], [[c.CAP, E]], base=c.CAP - 1, channel_multiplier=0,
        allow_small_or_imprecise_dtypes=True), ctr=c.gq)
    o_i4 = ph.op("gpsimd", lambda g: g.iota(
        c.rsel_t[:], [[1, NH * 128]], base=0, channel_multiplier=-128,
        allow_small_or_imprecise_dtypes=True), ctr=c.gq)
    for w in ("Wq0", "Wk0", "Wv0", "Wb0", "Wa0"):
        ph.dma("gpsimd", lambda g, w=w: g.dma_start(out=c.W16[w], in_=c.ins[w]), ctr=c.ldC)
    ph.dma("gpsimd", lambda g: g.dma_start(out=c.CTX16, in_=c.ins["ctx"]), ctr=c.ldC)
    for ei in range(E):
        ph.dma("gpsimd", lambda g, ei=ei: g.dma_start(out=c.W2S[ei], in_=c.ins["W2"][ei]),
               ctr=c.ldC)
    for i, bn in enumerate(("bk0", "bv0", "bb0", "ba0")):
        ph.dma("gpsimd", lambda g, i=i, bn=bn: g.dma_start(
            out=c.bias_rows[0:1, i * 1024:(i + 1) * 1024], in_=c.ins[bn].unsqueeze(0)),
            ctr=c.ldC)
    ph.dma("gpsimd", lambda g: g.dma_start(out=c.br_row[:], in_=c.ins["br"].unsqueeze(0)),
           ctr=c.ldC)
    ph.dma("gpsimd", lambda g: g.dma_start(out=c.bo1_row[:], in_=c.ins["bo1"].unsqueeze(0)),
           ctr=c.ldC)
    o_bq = ph.dma("sync", lambda s: s.dma_start(
        out=c.bq_cols[:, 0:8], in_=c.ins["bq0"].rearrange("(a p) -> p a", p=128),
        allow_slow_non_contiguous=True), ctr=c.ldA)
    o_bk = ph.dma("sync", lambda s: s.dma_start(
        out=c.bq_cols[:, 8:16], in_=c.ins["bk0"].rearrange("(a p) -> p a", p=128),
        allow_slow_non_contiguous=True), ctr=c.ldA)
    o_sin = ph.dma("sync", lambda s: s.dma_start(
        out=c.sc_sm[0:1, 0:1], in_=c.ins["s_in"].unsqueeze(0)), ctr=c.ldA)
    o_sind = ph.dma("sync", lambda s: s.dma_start(
        out=c.sc_sm[0:1, 1:2], in_=c.ins["s_in_d"].unsqueeze(0)), ctr=c.ldA)
    ph.op("scalar", lambda s: s.mul(out=c.bq_cols[:, 0:16], in_=c.bq_cols[:, 0:16],
                                    mul=QSCALE), waits=[o_sind], ctr=c.sq)

    def mkconsts(v):
        v.tensor_single_scalar(out=c.ident[:], in_=c.iotav[:], scalar=0.0, op=ALU.is_equal)
        v.tensor_single_scalar(out=c.identf[:], in_=c.iotav[:], scalar=0.0, op=ALU.is_equal)
        v.tensor_single_scalar(out=c.uinc[:], in_=c.iotav[:], scalar=0.0, op=ALU.is_ge)
        v.tensor_single_scalar(out=c.uincf[:], in_=c.iotav[:], scalar=0.0, op=ALU.is_ge)
        v.tensor_single_scalar(out=c.ustrf[:], in_=c.iotav[:], scalar=0.0, op=ALU.is_gt)
        v.tensor_scalar(out=c.madd[:], in0=c.iotav[:], scalar1=0.0, scalar2=-1e9,
                        op0=ALU.is_lt, op1=ALU.mult)
        v.memset(c.ones_row[:], 1.0)
        v.memset(c.ones_rowf[:], 1.0)
        v.memset(c.ones_colb[:], 1.0)
        v.memset(c.ones8f[:], 1.0)
        v.memset(c.S_all[:], 0.0)
        v.memset(c.constcols[:, 0:1], LNEPS)
        v.memset(c.constcols[:, 1:2], -9.0 * LN2)
        v.memset(c.constcols[:, 2:3], 9.0 * LN2)
        v.tensor_scalar(out=c.rowsel[:], in0=c.rsel_t[:], scalar1=0.0, scalar2=None,
                        op0=ALU.is_ge)
        v.tensor_single_scalar(out=c.rsel_t[:], in_=c.rsel_t[:], scalar=128.0,
                               op=ALU.is_lt)
        v.tensor_tensor(out=c.rowsel[:], in0=c.rowsel[:], in1=c.rsel_t[:], op=ALU.mult)
        return v.memset(c.MASKS[:], 0.0)

    o_consts = ph.op("vector", mkconsts, waits=[o_iota, o_i4], ctr=c.vq)
    o_gsmm = ph.op("tensor", lambda t: t.matmul(
        c.pb[0][:, 0:2], c.ones_rowf[0:1, 0:128], c.sc_sm[0:1, 0:2],
        start=True, stop=True), waits=[o_sind, o_consts], ctr=c.tq)

    def gs_ev(v):
        v.tensor_scalar_mul(out=c.gs_cols[:, 0:1], in0=c.pb[0][:, 0:1], scalar1=1.0 / 6.0)
        v.reciprocal(out=c.gs_cols[:, 1:2], in_=c.pb[0][:, 0:1])
        v.tensor_scalar_mul(out=c.gs_cols[:, 2:3], in0=c.pb[0][:, 1:2], scalar1=1.0 / 6.0)
        return v.reciprocal(out=c.gs_cols[:, 3:4], in_=c.pb[0][:, 1:2])

    ph.op("vector", gs_ev, waits=[o_gsmm], ctr=c.vq)
    ph.emit()


def emit_layernorm(c, name, x_src, xn_dst, x1_store=None, tile_epilogue=None):
    with contextlib.ExitStack() as px:
        _ln_inner(c, name, x_src, xn_dst, tile_epilogue, px.enter_context)


def _ln_inner(c, name, x_src, xn_dst, tile_epilogue, ep):
    ph = Phase(c, name)
    ldp = [sctr(c, "lnx0"), sctr(c, "lnx1")]
    stp = [sctr(c, "lnst0"), sctr(c, "lnst1")]
    xa = ep(c.nc.sbuf_tensor(name + "_x", [128, 2 * D], F32))
    oa = ep(c.nc.sbuf_tensor(name + "_o", [128, 2 * D], FP16))
    xc = ep(c.nc.sbuf_tensor(name + "_xc", [128, 2 * D], F32))
    sqs = ep(c.nc.sbuf_tensor(name + "_sq", [128, D], F32))
    stt = ep(c.nc.sbuf_tensor(name + "_st", [128, 16], F32))
    v3tok, g1tok = {}, {}
    for ti in range(c.NT):
        b = ti % 2
        xs = xa[:, b * D:(b + 1) * D]
        os_ = oa[:, b * D:(b + 1) * D]
        xcs = xc[:, b * D:(b + 1) * D]
        ss = stt[:, b * 8:(b + 1) * 8]
        w = [v3tok[ti - 2]] if ti >= 2 else []
        o_ld = ph.dma("sync", lambda s, ti=ti, xs=xs: s.dma_start(
            out=xs, in_=x_src[ti * 128:(ti + 1) * 128, :]), waits=w, ctr=ldp[b])

        def v1(v, xs=xs, ss=ss):
            selfsync(c, "vector", v,
                     v.tensor_reduce(out=ss[:, 0:1], in_=xs, axis=AX.X, op=ALU.add))
            return v.tensor_scalar_mul(out=ss[:, 1:2], in0=ss[:, 0:1], scalar1=-1.0 / D)
        o_v1 = ph.op("vector", v1, waits=[o_ld], ctr=c.vq)

        def s1(s, xs=xs, xcs=xcs, ss=ss):
            s.activation(out=xcs, in_=xs, func=AF.Identity, bias=ss[:, 1:2])
            return s.activation(out=sqs[:], in_=xcs, func=AF.Square, accum_out=ss[:, 3:4])
        o_s1 = ph.op("scalar", s1, waits=[o_v1], ctr=c.sq)

        def v2(v, ss=ss):
            selfsync(c, "vector", v,
                     v.tensor_scalar(out=ss[:, 4:5], in0=ss[:, 3:4], scalar1=1.0 / D,
                                     scalar2=LNEPS, op0=ALU.mult, op1=ALU.add))
            return v.reciprocal(out=ss[:, 5:6], in_=ss[:, 4:5])
        o_v2 = ph.op("vector", v2, waits=[o_s1], ctr=c.vq)

        def s2(s, ss=ss):
            return s.activation(out=ss[:, 6:7], in_=ss[:, 5:6], func=AF.Sqrt)
        o_s2 = ph.op("scalar", s2, waits=[o_v2], ctr=c.sq)

        def v3(v, xcs=xcs, ss=ss, os_=os_):
            return v.tensor_scalar(out=os_, in0=xcs, scalar1=ss[:, 6:7],
                                   scalar2=None, op0=ALU.mult)
        w3 = [o_s2] + ([g1tok[ti - 2]] if ti >= 2 else [])
        v3tok[ti] = ph.op("vector", v3, waits=w3, ctr=c.vq)
        g1tok[ti] = ph.dma("gpsimd", lambda g, ti=ti, os_=os_: g.dma_start(
            out=xn_dst[ti * 128:(ti + 1) * 128, :], in_=os_), waits=[v3tok[ti]], ctr=stp[b])
        if tile_epilogue is not None:
            tile_epilogue(ph, ti, v2tok[ti], SimpleNamespace(xn=os_, x=xs, xcs=xcs))
    ph.emit()


def p2_proj(c):
    with contextlib.ExitStack() as px:
        _p2_inner(c, px.enter_context)


def _p2_inner(c, ep):
    ph = Phase(c, "p2")
    nc = c.nc
    WT = {w: ep(nc.sbuf_tensor("WT_" + w, [128, 8 * D], FP16))
          for w in ("Wq0", "Wk0", "Wv0", "Wb0", "Wa0")}
    WrS = ep(nc.sbuf_tensor("WrS", [128, 8 * E], FP16))
    wWT = sctr(c, "wWT")
    xntc = [sctr(c, "xnt0"), sctr(c, "xnt1")]
    evs = [sctr(c, f"evs{i}") for i in range(4)]
    ktmc = [sctr(c, "ktm0"), sctr(c, "ktm1")]
    vtmc = [sctr(c, "vtm0"), sctr(c, "vtm1")]
    vbsc = [sctr(c, "vbs0"), sctr(c, "vbs1")]

    o_w = None
    for w in ("Wq0", "Wk0", "Wv0", "Wb0", "Wa0"):
        for dc in range(8):
            o_w = ph.dma("sync", lambda s, w=w, dc=dc: s.dma_start_transpose(
                WT[w][:, dc * D:(dc + 1) * D], c.W16[w][0:D, dc * 128:(dc + 1) * 128]),
                ctr=wWT)
    wWr = sctr(c, "wWr")
    o_wr = None
    for dc in range(8):
        o_wr = ph.dma("gpsimd", lambda g, dc=dc: g.dma_start(
            out=WrS[:, dc * E:(dc + 1) * E], in_=c.ins["Wr"][dc * 128:(dc + 1) * 128, :]),
            ctr=wWr)
    wload = [o_w, o_wr]

    xnt = ep(nc.sbuf_tensor("p2_xnt", [128, 2 * 8 * 512], FP16))
    evfm = ep(nc.sbuf_tensor("p2_evfm", [128, 4 * 512], FP16))
    evktm = ep(nc.sbuf_tensor("p2_evktm", [128, 2 * 512], FP16))
    vsb = ep(nc.sbuf_tensor("p2_v", [128, 2 * D], FP16))
    vbsb = ep(nc.sbuf_tensor("p2_vb", [128, 2 * D], FP16))
    betas = ep(nc.sbuf_tensor("p2_beta", [128, 2 * D], F32))
    nlss = ep(nc.sbuf_tensor("p2_nls", [128, 2 * D], F32))
    e1s = ep(nc.sbuf_tensor("p2_e1", [128, 512], F32))
    a1s = ep(nc.sbuf_tensor("p2_a1", [128, 2 * 512], F32))
    red16 = ep(nc.sbuf_tensor("p2_red", [128, 16], F32))
    logt = ep(nc.sbuf_tensor("p2_logt", [128, 48], F32))

    last_pb = {}
    last_slot = {}
    vt_hist = {}
    aev_hist = {}
    last_xnt_use = {}

    for tg in range(c.NTG):
        gb = tg % 2
        xg = xnt[:, gb * 4096:(gb + 1) * 4096]
        w = [last_xnt_use.get(gb)]
        o_l = None
        for dc in range(8):
            o_l = ph.dma("sync", lambda s, tg=tg, dc=dc, xg=xg: s.dma_start_transpose(
                xg[:, dc * 512:(dc + 1) * 512],
                c.XN[tg * 512:(tg + 1) * 512, dc * 128:(dc + 1) * 128]),
                waits=w, ctr=xntc[gb])
        lw = [o_l] + (wload if tg == 0 else [])

        for wi, (wname, dst, colofs) in enumerate((("Wq0", c.QT, 0), ("Wk0", c.KT, 8))):
            for oc in range(8):
                bk = oc % 4
                bank = c.pb[bk]

                def mm(t, wname=wname, oc=oc, xg=xg, bank=bank):
                    r = None
                    for dc in range(8):
                        r = t.matmul(bank[:],
                                     WT[wname][:, dc * D + oc * 128:dc * D + (oc + 1) * 128],
                                     xg[:, dc * 512:(dc + 1) * 512],
                                     start=(dc == 0), stop=(dc == 7))
                    return r
                o_mm = ph.op("tensor", mm, waits=lw + [last_pb.get(bk)], ctr=c.tq)
                last_xnt_use[gb] = o_mm
                sl = (wi * 8 + oc) % 4
                eb = evfm[:, sl * 512:(sl + 1) * 512]

                def evac(s, bank=bank, eb=eb, colofs=colofs, oc=oc):
                    return s.activation(out=eb, in_=bank[:], func=AF.Identity,
                                        scale=QSCALE,
                                        bias=c.bq_cols[:, colofs + oc:colofs + oc + 1])
                o_ev = ph.op("scalar", evac, waits=[o_mm, last_slot.get(sl)], ctr=c.sq)
                last_pb[bk] = o_ev
                o_st = ph.dma("gpsimd", lambda g, dst=dst, oc=oc, tg=tg, eb=eb: g.dma_start(
                    out=dst[oc * 128:(oc + 1) * 128, tg * 512:(tg + 1) * 512], in_=eb),
                    waits=[o_ev], ctr=evs[sl])
                last_slot[sl] = o_st

        for tt in range(4):
            ti = tg * 4 + tt
            p = ti % 2
            o_vt_m2 = vt_hist.get(ti - 2)
            ev_ops = []
            for wi, wname in enumerate(("Wk0", "Wv0", "Wb0", "Wa0")):
                for oh in range(2):
                    bk = (wi * 2 + oh) % 4
                    bank = c.pb[bk]

                    def mmtm(t, wname=wname, oh=oh, tt=tt, xg=xg, bank=bank, wi=wi):
                        for dc in range(8):
                            t.matmul(bank[:],
                                     xg[:, dc * 512 + tt * 128:dc * 512 + (tt + 1) * 128],
                                     WT[wname][:, dc * D + oh * 512:dc * D + (oh + 1) * 512],
                                     start=(dc == 0), stop=False)
                        return t.matmul(
                            bank[:], c.ones_row[0:1, 0:128],
                            c.bias_rows[0:1, wi * 1024 + oh * 512:wi * 1024 + (oh + 1) * 512],
                            start=False, stop=True)
                    o_mm = ph.op("tensor", mmtm, waits=lw + [last_pb.get(bk)], ctr=c.tq)
                    last_xnt_use[gb] = o_mm
                    if wname == "Wk0":
                        eb = evktm[:, oh * 512:(oh + 1) * 512]

                        def evk(s, bank=bank, eb=eb):
                            return s.activation(out=eb, in_=bank[:], func=AF.Copy,
                                                scale=QSCALE)
                        o_ev = ph.op("scalar", evk,
                                     waits=[o_mm, last_slot.get(("ktm", oh))], ctr=c.sq)
                        o_st = ph.dma("gpsimd", lambda g, ti=ti, oh=oh, eb=eb: g.dma_start(
                            out=c.KTM[ti * 128:(ti + 1) * 128, oh * 512:(oh + 1) * 512],
                            in_=eb), waits=[o_ev], ctr=ktmc[oh])
                        last_slot[("ktm", oh)] = o_st
                        last_pb[bk] = o_ev
                    elif wname == "Wv0":
                        vb_half = vsb[:, p * D + oh * 512:p * D + (oh + 1) * 512]

                        def evv(s, bank=bank, vb_half=vb_half):
                            return s.activation(out=vb_half, in_=bank[:], func=AF.Copy)
                        o_ev = ph.op("scalar", evv,
                                     waits=[o_mm, o_vt_m2, last_slot.get(("vtm", p))],
                                     ctr=c.sq)
                        o_st = ph.dma("gpsimd",
                                      lambda g, ti=ti, oh=oh, vb_half=vb_half: g.dma_start(
                                          out=c.VTM[ti * 128:(ti + 1) * 128,
                                                    oh * 512:(oh + 1) * 512], in_=vb_half),
                                      waits=[o_ev], ctr=vtmc[p])
                        last_slot[("vtm", p)] = o_st
                        last_pb[bk] = o_ev
                        ev_ops.append(o_ev)
                    elif wname == "Wb0":
                        bslice = betas[:, p * D + oh * 512:p * D + (oh + 1) * 512]

                        def evb(s, bank=bank, bslice=bslice):
                            s.activation(out=e1s[:], in_=bank[:], func=AF.Exp)
                            return s.activation(out=bslice, in_=e1s[:], func=AF.Ln, bias=1.0)
                        o_ev = ph.op("scalar", evb, waits=[o_mm, o_vt_m2], ctr=c.sq)
                        last_pb[bk] = o_ev
                        ev_ops.append(o_ev)
                    else:
                        aslice = nlss[:, p * D + oh * 512:p * D + (oh + 1) * 512]
                        asl = a1s[:, oh * 512:(oh + 1) * 512]

                        def eva_v(v, bank=bank, asl=asl):
                            return v.tensor_scalar(out=asl, in0=bank[:], scalar1=-10.0,
                                                   scalar2=10.0, op0=ALU.max, op1=ALU.min)
                        o_cl = ph.op("vector", eva_v,
                                     waits=[o_mm, aev_hist.get(oh)], ctr=c.vq)
                        last_pb[bk] = o_cl

                        def eva_s(s, aslice=aslice, asl=asl):
                            s.activation(out=e1s[:], in_=asl, func=AF.Exp, scale=-1.0)
                            return s.activation(out=aslice, in_=e1s[:], func=AF.Ln, bias=1.0)
                        o_ev = ph.op("scalar", eva_s, waits=[o_cl, o_vt_m2], ctr=c.sq)
                        aev_hist[oh] = o_ev
                        ev_ops.append(o_ev)

            def mmr(t, tt=tt, xg=xg):
                for dc in range(8):
                    t.matmul(c.pb[4][:, 0:8],
                             xg[:, dc * 512 + tt * 128:dc * 512 + (tt + 1) * 128],
                             WrS[:, dc * E:(dc + 1) * E], start=(dc == 0), stop=False)
                return t.matmul(c.pb[4][:, 0:8], c.ones_row[0:1, 0:128],
                                c.br_row[0:1, 0:8], start=False, stop=True)
            o_mr = ph.op("tensor", mmr, waits=lw + [last_pb.get(4)], ctr=c.tq)
            last_xnt_use[gb] = o_mr

            def vtile(v, ti=ti, p=p):
                bet = betas[:, p * D:(p + 1) * D]
                nls = nlss[:, p * D:(p + 1) * D]
                vv = vsb[:, p * D:(p + 1) * D]
                vb = vbsb[:, p * D:(p + 1) * D]
                selfsync(c, "vector", v, v.tensor_reduce(
                    out=red16[:], in_=bet.rearrange("q (h d) -> q h d", d=HD),
                    axis=AX.X, op=ALU.add))
                v.tensor_copy(out=c.BMS[:, ti * NH:(ti + 1) * NH], in_=red16[:])
                selfsync(c, "vector", v, v.tensor_reduce(
                    out=red16[:], in_=nls.rearrange("q (h d) -> q h d", d=HD),
                    axis=AX.X, op=ALU.add))
                v.tensor_scalar_mul(out=c.MLA[:, ti * NH:(ti + 1) * NH], in0=red16[:],
                                    scalar1=-1.0 / HD)
                v.tensor_tensor(out=vb, in0=vv, in1=bet, op=ALU.mult)
                selfsync(c, "vector", v,
                         v.tensor_reduce(out=logt[:, 8:9], in_=c.pb[4][:, 0:8],
                                         axis=AX.X, op=ALU.max))
                eq = logt[:, 16:24]
                selfsync(c, "vector", v,
                         v.tensor_scalar(out=eq, in0=c.pb[4][:, 0:8],
                                         scalar1=logt[:, 8:9], scalar2=None,
                                         op0=ALU.is_ge))
                cs = logt[:, 24:32]
                t1 = logt[:, 32:40]
                sv = lambda i: selfsync(c, "vector", v, i)
                sv(v.tensor_copy(out=cs, in_=eq))
                sv(v.tensor_copy(out=t1[:, 1:8], in_=cs[:, 0:7]))
                sv(v.tensor_tensor(out=cs[:, 1:8], in0=cs[:, 1:8], in1=t1[:, 1:8], op=ALU.add))
                sv(v.tensor_copy(out=t1[:, 2:8], in_=cs[:, 0:6]))
                sv(v.tensor_tensor(out=cs[:, 2:8], in0=cs[:, 2:8], in1=t1[:, 2:8], op=ALU.add))
                sv(v.tensor_copy(out=t1[:, 4:8], in_=cs[:, 0:4]))
                sv(v.tensor_tensor(out=cs[:, 4:8], in0=cs[:, 4:8], in1=t1[:, 4:8], op=ALU.add))
                first = logt[:, 40:48]
                sv(v.tensor_single_scalar(out=first, in_=cs, scalar=1.0, op=ALU.is_equal))
                mf = c.MASKSF[:, ti * E:(ti + 1) * E]
                sv(v.tensor_tensor(out=mf, in0=eq, in1=first, op=ALU.mult))
                return v.tensor_copy(out=c.MASKS[:, ti * E:(ti + 1) * E], in_=mf)
            o_vt = ph.op("vector", vtile,
                         waits=ev_ops + [o_mr, last_slot.get(("vbs", p))], ctr=c.vq)
            vt_hist[ti] = o_vt
            last_pb[4] = o_vt
            o_vbst = ph.dma("gpsimd", lambda g, ti=ti, p=p: g.dma_start(
                out=c.VBS[ti * 128:(ti + 1) * 128, :], in_=vbsb[:, p * D:(p + 1) * D]),
                waits=[o_vt], ctr=vbsc[p])
            last_slot[("vbs", p)] = o_vbst

            def mcnt(t, ti=ti):
                return t.matmul(c.pb[5][0:8, ti:ti + 1], c.MASKS[:, ti * E:(ti + 1) * E],
                                c.ones_colb[:, 0:1], start=True, stop=True)
            o_cnt_last = ph.op("tensor", mcnt, waits=[o_vt], ctr=c.tq)
    ph.op("vector", lambda v: v.tensor_copy(out=c.cnt_sb[0:8, 0:c.NT],
                                            in_=c.pb[5][0:8, 0:c.NT]),
          waits=[o_cnt_last], ctr=c.vq)
    ph.emit()


def p3_deltanet(c):
    with contextlib.ExitStack() as px:
        _p3_inner(c, px.enter_context)


def _p3_inner(c, ep):
    ph = Phase(c, "p3")
    nc = c.nc
    NCH = c.NCH
    qts = ep(nc.sbuf_tensor("p3_q", [128, 2 * D], FP16))
    kts = ep(nc.sbuf_tensor("p3_k", [128, 2 * D], FP16))
    ktm = ep(nc.sbuf_tensor("p3_ktm", [128, 2 * D], FP16))
    vtm = ep(nc.sbuf_tensor("p3_v", [128, 2 * D], FP16))
    vbt = ep(nc.sbuf_tensor("p3_vb", [128, 2 * D], FP16))
    ytile = ep(nc.sbuf_tensor("p3_y", [128, 2 * D], FP16))
    m_tm = ep(nc.sbuf_tensor("p3_mtm", [128, 2 * NH], F32))
    mT = ep(nc.sbuf_tensor("p3_mT", [16, 2 * 128], F32))
    curveT = ep(nc.sbuf_tensor("p3_cT", [16, 2 * 128], F32))
    curve_tm = ep(nc.sbuf_tensor("p3_ctm", [128, 2 * NH], F32))
    dsb_a = ep(nc.sbuf_tensor("p3_dsb", [128, 2 * 128], F32))
    dec_a = ep(nc.sbuf_tensor("p3_dec", [128, 2 * 128], F32))
    adT_a = ep(nc.sbuf_tensor("p3_adT", [128, 2 * 128], FP16))
    ymem_a = ep(nc.sbuf_tensor("p3_ymem", [128, 2 * 64], F32))
    vbm_a = ep(nc.sbuf_tensor("p3_vbm", [128, 2 * 64], FP16))
    asum_c = ep(nc.sbuf_tensor("p3_asum", [128, 1], F32))
    S16 = ep(nc.sbuf_tensor("p3_S16", [128, c.B * NH * 32], FP16))

    ldq = [sctr(c, "d3l0"), sctr(c, "d3l1")]
    styc = [sctr(c, "d3y0"), sctr(c, "d3y1")]

    o_zero = ph.op("vector", lambda v: v.memset(S16[:], 0.0), ctr=c.vq)

    P = c.pb
    last = {}
    ev5_tok = {}
    st_tok = {}
    ld_tok = {}

    for b in range(c.B):
        for n in range(NCH):
            ci = b * NCH + n
            par = ci % 2
            t0 = ci * 128
            qg = qts[:, par * D:(par + 1) * D]
            kg = kts[:, par * D:(par + 1) * D]
            kmg = ktm[:, par * D:(par + 1) * D]
            vg = vtm[:, par * D:(par + 1) * D]
            vbg = vbt[:, par * D:(par + 1) * D]
            yg = ytile[:, par * D:(par + 1) * D]
            mtg = m_tm[:, par * NH:(par + 1) * NH]
            mTg = mT[:, par * 128:(par + 1) * 128]
            cTg = curveT[:, par * 128:(par + 1) * 128]
            ctg = curve_tm[:, par * NH:(par + 1) * NH]

            wl = [ld_tok.get(ci - 2)]
            o_l = None
            for name, dst, src in (("q", qg, c.QT), ("k", kg, c.KT)):
                o_l = ph.dma("sync", lambda s, dst=dst, src=src, t0=t0: s.dma_start(
                    out=dst[:].rearrange("p (a t) -> p a t", t=128),
                    in_=src[0:D, t0:t0 + 128].rearrange("(a p) t -> p a t", p=128)),
                    waits=wl, ctr=ldq[par])
            for dst, src in ((kmg, c.KTM), (vg, c.VTM), (vbg, c.VBS)):
                o_l = ph.dma("sync", lambda s, dst=dst, src=src, t0=t0: s.dma_start(
                    out=dst, in_=src[t0:t0 + 128, 0:D]), waits=wl, ctr=ldq[par])

            def oc1(t, ci=ci):
                return t.matmul(P[6][:, 0:16], c.uincf[:],
                                c.MLA[:, ci * NH:(ci + 1) * NH], start=True, stop=True)
            o_c1 = ph.op("tensor", oc1, waits=[o_zero, last.get("pb6")], ctr=c.tq)

            def oc2(v, mtg=mtg):
                return v.tensor_copy(out=mtg, in_=P[6][:, 0:16])
            o_c2 = ph.op("vector", oc2, waits=[o_c1, last.get(("mtm", par))], ctr=c.vq)
            last["pb6"] = o_c2

            def oc3(t, mtg=mtg):
                return t.transpose(P[7][0:16, 0:128], mtg, c.identf[:])
            o_c3 = ph.op("tensor", oc3, waits=[o_c2, last.get("pb7")], ctr=c.tq)

            def oc4(v, mTg=mTg):
                return v.tensor_copy(out=mTg, in_=P[7][0:16, 0:128])
            o_c4 = ph.op("vector", oc4, waits=[o_c3, last.get(("mT", par))], ctr=c.vq)
            last["pb7"] = o_c4

            def oc5(s, mTg=mTg, cTg=cTg):
                return s.activation(out=cTg, in_=mTg, func=AF.Exp)
            o_c5 = ph.op("scalar", oc5, waits=[o_c4, last.get(("cT", par))], ctr=c.sq)

            def oc6(s, mtg=mtg, ctg=ctg):
                return s.activation(out=ctg, in_=mtg, func=AF.Exp)
            o_c6 = ph.op("scalar", oc6, waits=[o_c2, last.get(("ctm", par))], ctr=c.sq)

            ev2_list = []
            for h in range(NH):
                p0 = (h % 2) * 64
                hp = h % 2
                dsb = dsb_a[:, hp * 128:(hp + 1) * 128]
                dec = dec_a[:, hp * 128:(hp + 1) * 128]
                adT = adT_a[:, hp * 128:(hp + 1) * 128]
                ymem = ymem_a[:, hp * 64:(hp + 1) * 64]
                vbm = vbm_a[:, hp * 64:(hp + 1) * 64]
                dc = h // 2
                qh = qg[p0:p0 + 64, dc * 128:(dc + 1) * 128]
                kh = kg[p0:p0 + 64, dc * 128:(dc + 1) * 128]
                cb = ((b * NH + h) // 2) * 64
                sS = c.S_all[p0:p0 + 64, cb:cb + 64]
                sS16 = S16[p0:p0 + 64, cb:cb + 64]

                o_m1 = ph.op("tensor", lambda t, kh=kh, qh=qh: t.matmul(
                    P[0][:, 0:128], kh, qh, start=True, stop=True),
                    waits=[o_l, last.get("pb0")], ctr=c.tq)
                o_m2 = ph.op("tensor", lambda t, mTg=mTg, h=h: t.matmul(
                    P[1][:, 0:128], c.rowsel[0:16, h * 128:(h + 1) * 128],
                    mTg[0:16, 0:128], start=True, stop=True),
                    waits=[o_c4, last.get("pb1")], ctr=c.tq)
                o_d1 = ph.op("vector", lambda v, mtg=mtg, h=h, dsb=dsb: v.scalar_tensor_tensor(
                    out=dsb[:], in0=P[1][:, 0:128], scalar=mtg[:, h:h + 1],
                    in1=c.madd[:], op0=ALU.subtract, op1=ALU.add),
                    waits=[o_m2, Hard(o_c2), last.get(("dsb", hp))], ctr=c.vq)
                last["pb1"] = o_d1
                o_d2 = ph.op("scalar", lambda s, dec=dec, dsb=dsb: s.activation(
                    out=dec[:], in_=dsb[:], func=AF.Exp),
                    waits=[o_d1, last.get(("dec", hp))], ctr=c.sq)
                last[("dsb", hp)] = o_d2
                o_d3 = ph.op("vector", lambda v, adT=adT, dec=dec: v.tensor_tensor(
                    out=adT[:], in0=P[0][:, 0:128], in1=dec[:], op=ALU.mult),
                    waits=[o_d2, o_m1, last.get(("adT", hp))], ctr=c.vq)
                last["pb0"] = o_d3
                last[("dec", hp)] = o_d3
                o_m3 = ph.op("tensor", lambda t, vbg=vbg, h=h, adT=adT: t.matmul(
                    P[2][:, 0:64], adT[:], vbg[:, h * 64:(h + 1) * 64],
                    start=True, stop=True), waits=[o_d3, last.get("pb2")], ctr=c.tq)
                last[("adT", hp)] = o_m3
                o_m4 = ph.op("tensor", lambda t, qh=qh, sS16=sS16: t.matmul(
                    P[5][:, 0:64], qh, sS16, start=True, stop=True),
                    waits=[o_l, ev5_tok.get((b, h), o_zero), last.get("pb5")], ctr=c.tq)
                o_d4 = ph.op("vector", lambda v, vg=vg, h=h, ci=ci, vbm=vbm: v.tensor_scalar(
                    out=vbm[:], in0=vg[:, h * 64:(h + 1) * 64],
                    scalar1=c.BMS[:, ci * NH + h:ci * NH + h + 1], scalar2=1.0 / HD,
                    op0=ALU.mult, op1=ALU.mult),
                    waits=[o_l, last.get(("vbm", hp))], ctr=c.vq)
                o_m5 = ph.op("tensor", lambda t, kmg=kmg, h=h, p0=p0, vbm=vbm: t.matmul(
                    P[3][p0:p0 + 64, 0:64], kmg[:, h * 64:(h + 1) * 64], vbm[:],
                    start=True, stop=True, tile_position=(0, p0)),
                    waits=[o_d4, last.get("pb3")], ctr=c.tq)
                last[("vbm", hp)] = o_m5
                o_m6 = ph.op("tensor", lambda t, cTg=cTg, h=h, p0=p0: t.matmul(
                    P[4][p0:p0 + 64, 0:1], c.rowsel[0:16, h * 128:h * 128 + 64],
                    cTg[0:16, 127:128], start=True, stop=True, tile_position=(0, p0)),
                    waits=[o_c5, last.get("pb4")], ctr=c.tq)
                o_e1 = ph.op("scalar", lambda s, ctg=ctg, h=h, ymem=ymem: s.activation(
                    out=ymem[:], in_=P[5][:, 0:64], func=AF.Identity,
                    scale=ctg[:, h:h + 1], bias=0.0),
                    waits=[o_m4, Hard(o_c6), last.get(("ymem", hp))], ctr=c.sq)
                last["pb5"] = o_e1
                o_e2 = ph.op("vector", lambda v, yg=yg, h=h, ymem=ymem: v.tensor_tensor(
                    out=yg[:, h * 64:(h + 1) * 64], in0=P[2][:, 0:64], in1=ymem[:],
                    op=ALU.add),
                    waits=[o_m3, o_e1, st_tok.get(ci - 2), last.get("yg")], ctr=c.vq)
                last["pb2"] = o_e2
                last[("ymem", hp)] = o_e2
                ev2_list.append(o_e2)
                o_e3 = ph.op("scalar", lambda s, p0=p0: s.copy(
                    out=asum_c[p0:p0 + 64, 0:1], in_=P[4][p0:p0 + 64, 0:1]),
                    waits=[o_m6, last.get("asumc")], ctr=c.sq)
                o_e4 = ph.op("vector", lambda v, sS=sS, p0=p0: v.scalar_tensor_tensor(
                    out=sS, in0=sS, scalar=asum_c[p0:p0 + 64, 0:1],
                    in1=P[3][p0:p0 + 64, 0:64],
                    op0=ALU.mult, op1=ALU.add), waits=[o_e3, o_m5], ctr=c.vq)
                last["pb3"] = o_e4
                last["pb4"] = o_e4
                last["asumc"] = o_e4
                o_e5 = ph.op("vector", lambda v, sS=sS, sS16=sS16: v.tensor_copy(
                    out=sS16, in_=sS), waits=[], ctr=c.vq)
                ev5_tok[(b, h)] = o_e5
            last["yg"] = ev2_list[-1]
            last[("mtm", par)] = ev2_list[-1]
            last[("mT", par)] = ev2_list[-1]
            last[("cT", par)] = ev2_list[-1]
            last[("ctm", par)] = ev2_list[-1]
            ld_tok[ci] = ev2_list[-1]
            st_tok[ci] = ph.dma("gpsimd", lambda g, t0=t0, yg=yg: g.dma_start(
                out=c.YATT[t0:t0 + 128, 0:D], in_=yg), waits=[ev2_list[-1]], ctr=styc[par])
    ph.emit()


def emit_quant_batch(c, ph, items, nb, gs6_col, rgs_col, qb, extra_waits=()):
    k = len(items)
    F = nb * 16
    K = k * nb
    sm = qb.sm
    amax = sm[:, 0 * K:1 * K]
    sc = sm[:, 1 * K:2 * K]
    cnt = sm[:, 2 * K:3 * K]
    q2 = sm[:, 3 * K:4 * K]
    rq = sm[:, 4 * K:5 * K]
    u2 = sm[:, 5 * K:6 * K]
    mm_ = sm[:, 6 * K:7 * K]
    dq = sm[:, 7 * K:8 * K]
    rdq = sm[:, 8 * K:9 * K]
    ind3_14 = qb.ind[:, 0:14 * K].rearrange("p (j n) -> p n j", n=K)
    ind3_16 = qb.ind[:, 0:16 * K].rearrange("p (j n) -> p n j", n=K)
    lastq = getattr(c, "_qlast", {})
    c._qlast = lastq

    o_am = []
    for i, (xin, _, w) in enumerate(items):
        def va(v, xin=xin, i=i):
            return v.tensor_reduce(out=amax[:, i * nb:(i + 1) * nb],
                                   in_=xin.rearrange("p (n j) -> p n j", j=16),
                                   axis=AX.X, op=ALU.max, apply_absolute_value=True)
        o_am.append(ph.op("vector", va,
                          waits=list(w) + list(extra_waits) +
                                ([lastq.get("sm")] if i == 0 else []), ctr=c.vq))

    def vsc(v):
        sv = lambda j: selfsync(c, "vector", v, j)
        sv(v.tensor_scalar(out=sc, in0=amax, scalar1=gs6_col, scalar2=448.0,
                           op0=ALU.mult, op1=ALU.min))
        for i, j in enumerate(range(-5, 9)):
            v.tensor_single_scalar(out=qb.ind[:, i * K:(i + 1) * K], in_=sc,
                                   scalar=float(2.0 ** j), op=ALU.is_ge)
        sv(v.memset(qb.ind[:, 14 * K:16 * K], 0.0))
        sv(v.tensor_reduce(out=cnt, in_=ind3_16, axis=AX.X, op=ALU.add))
        return v.tensor_copy(out=mm_, in_=cnt)
    o_sc = ph.op("vector", vsc, waits=o_am, ctr=c.vq)

    def sexp(s):
        s.activation(out=q2, in_=cnt, func=AF.Exp, scale=LN2, bias=c.constcols[:, 1:2])
        return s.activation(out=rq, in_=cnt, func=AF.Exp, scale=-LN2,
                            bias=c.constcols[:, 2:3])
    o_se = ph.op("scalar", sexp, waits=[o_sc, lastq.get("sm2")], ctr=c.sq)

    def vdq(v):
        sv = lambda j: selfsync(c, "vector", v, j)
        sv(v.tensor_tensor(out=u2, in0=sc, in1=rq, op=ALU.mult))
        for kk2 in range(16):
            v.tensor_single_scalar(out=qb.ind[:, kk2 * K:(kk2 + 1) * K], in_=u2,
                                   scalar=kk2 + 0.5, op=ALU.is_ge)
        sv(v.tensor_single_scalar(out=qb.ind[:, 15 * K:16 * K], in_=u2,
                                  scalar=15.5, op=ALU.is_ge))
        sv(v.tensor_reduce(out=mm_, in_=ind3_16, axis=AX.X, op=ALU.add))
        sv(v.tensor_tensor(out=dq, in0=mm_, in1=q2, op=ALU.mult))
        sv(v.tensor_scalar(out=dq, in0=dq, scalar1=rgs_col, scalar2=None,
                           op0=ALU.mult))
        sv(v.scalar_tensor_tensor(out=dq, in0=dq, scalar=0.0, in1=dq,
                                  op0=ALU.is_equal, op1=ALU.add))
        sv(v.reciprocal(out=rdq, in_=dq))
        return v.tensor_copy(out=qb.dq16[:, 0:K], in_=dq)
    o_vd = ph.op("vector", vdq, waits=[o_se], ctr=c.vq)
    lastq["sm"] = o_vd
    lastq["sm2"] = o_vd

    QW = ((0.25, 0.5), (0.75, 0.5), (1.25, 0.5), (1.75, 0.5),
          (2.5, 1.0), (3.5, 1.0), (5.0, 2.0))
    fb0 = qb.fb[:, 0 * F:1 * F]
    fb1 = qb.fb[:, 1 * F:2 * F]
    hsl = [qb.hb[:, i * F:(i + 1) * F] for i in range(8)]
    outs = []
    for i, (xin, out16, w) in enumerate(items):
        rdq_b = rdq[:, i * nb:(i + 1) * nb].unsqueeze(2).broadcast_to([128, nb, 16])
        dq16_b = qb.dq16[:, i * nb:(i + 1) * nb].unsqueeze(2).broadcast_to([128, nb, 16])
        o_abs = ph.op("scalar", lambda s, xin=xin: s.activation(
            out=fb0, in_=xin, func=AF.Abs),
            waits=list(w) + [o_vd, lastq.get("fb")], ctr=c.sq)

        def v1(v, rdq_b=rdq_b):
            sv = lambda j: selfsync(c, "vector", v, j)
            sv(v.tensor_tensor(out=fb1, in0=fb0, in1=rdq_b, op=ALU.mult))
            for j, (t, wt) in enumerate(QW):
                v.tensor_scalar(out=hsl[j], in0=fb1, scalar1=float(t),
                                scalar2=float(wt), op0=ALU.is_ge, op1=ALU.mult)
            sv(v.tensor_tensor(out=hsl[7], in0=hsl[5], in1=hsl[6], op=ALU.add))
            v.tensor_tensor(out=hsl[0], in0=hsl[0], in1=hsl[1], op=ALU.add)
            v.tensor_tensor(out=hsl[2], in0=hsl[2], in1=hsl[3], op=ALU.add)
            sv(v.tensor_tensor(out=hsl[4], in0=hsl[4], in1=hsl[7], op=ALU.add))
            sv(v.tensor_tensor(out=hsl[0], in0=hsl[0], in1=hsl[2], op=ALU.add))
            return v.tensor_tensor(out=hsl[0], in0=hsl[0], in1=hsl[4], op=ALU.add)
        o_v1 = ph.op("vector", v1, waits=[o_abs], ctr=c.vq)
        o_sg = ph.op("scalar", lambda s, xin=xin: s.activation(
            out=hsl[1], in_=xin, func=AF.Sign), waits=[o_v1], ctr=c.sq)

        def v3(v, dq16_b=dq16_b, out16=out16):
            selfsync(c, "vector", v, v.tensor_tensor(out=hsl[1], in0=hsl[1],
                                                     in1=dq16_b, op=ALU.mult))
            return v.tensor_tensor(out=out16, in0=hsl[0], in1=hsl[1], op=ALU.mult)
        o_v3 = ph.op("vector", v3, waits=[o_sg], ctr=c.vq)
        lastq["fb"] = o_v3
        outs.append(o_v3)
    return outs


def emit_quant(c, ph, xin, out16, nb, gs6_col, rgs_col, qb, waits):
    return emit_quant_batch(c, ph, [(xin, out16, list(waits))], nb,
                            gs6_col, rgs_col, qb)[-1]


def p4_moe(c):
    with contextlib.ExitStack() as px:
        _p4_inner(c, px.enter_context)


def _p4_inner(c, ep):
    nc = c.nc
    NT, CAP = c.NT, c.CAP
    TGE = min(512, CAP)
    NG = CAP // TGE
    NTT = TGE // 128
    P = c.pb

    ph = Phase(c, "p4a")
    base_sb = ep(nc.sbuf_tensor("p4_base", [64, 8], F32))
    baseT = ep(nc.sbuf_tensor("p4_baseT", [8, 64], F32))
    cntT = ep(nc.sbuf_tensor("p4_cntT", [64, 8], F32))
    b8 = ep(nc.sbuf_tensor("p4_b8", [8, 1], F32))
    diag8 = ep(nc.sbuf_tensor("p4_diag", [8, 8], F32))
    mp8 = ep(nc.sbuf_tensor("p4_mp", [128, 8], F32))
    dcol = ep(nc.sbuf_tensor("p4_dcol", [128, 4], F32))
    xnsb = ep(nc.sbuf_tensor("p4_xn", [128, 2 * D], FP16))
    zeroD = ep(nc.sbuf_tensor("p4_zero", [128, D], FP16))
    xldc = [sctr(c, "p4x0"), sctr(c, "p4x1")]
    sctc = sctr(c, "p4sct")

    o_t1 = ph.op("tensor", lambda t: t.transpose(P[7][0:c.NT, 0:8],
                                                 c.cnt_sb[0:8, 0:c.NT],
                                                 c.identf[0:8, 0:8]),
                 ctr=c.tq)
    o_v1 = ph.op("vector", lambda v: v.tensor_copy(out=cntT[0:c.NT, :],
                                                   in_=P[7][0:c.NT, 0:8]),
                 waits=[o_t1], ctr=c.vq)
    o_t2 = ph.op("tensor", lambda t: t.matmul(P[6][0:c.NT, 0:8],
                                              c.ustrf[0:c.NT, 0:c.NT], cntT[0:c.NT, :],
                                              start=True, stop=True),
                 waits=[o_v1], ctr=c.tq)
    o_v2 = ph.op("vector", lambda v: v.tensor_copy(out=base_sb[0:c.NT, :],
                                                   in_=P[6][0:c.NT, 0:8]),
                 waits=[o_t2], ctr=c.vq)
    o_t3 = ph.op("tensor", lambda t: t.transpose(P[7][0:8, 0:c.NT],
                                                 base_sb[0:c.NT, :],
                                                 c.identf[0:c.NT, 0:c.NT]),
                 waits=[o_v2, o_v1], ctr=c.tq)
    o_v3 = ph.op("vector", lambda v: v.tensor_copy(out=baseT[:, 0:c.NT],
                                                   in_=P[7][0:8, 0:c.NT]),
                 waits=[o_t3], ctr=c.vq)
    o_z = ph.op("vector", lambda v: v.memset(zeroD[:], 0.0), ctr=c.vq)
    zf = None
    for blk in range(E * CAP // 128):
        zf = ph.dma("gpsimd", lambda g, blk=blk: g.dma_start(
            out=c.XG[blk * 128:(blk + 1) * 128, :], in_=zeroD[:]),
            waits=[o_z], ctr=sctc)

    prev = {}
    for ti in range(NT):
        par = ti % 2

        def vdiag(v, ti=ti):
            sv = lambda i: selfsync(c, "vector", v, i)
            sv(v.tensor_tensor(out=b8[:], in0=baseT[:, ti:ti + 1], in1=c.ecap_col[:],
                               op=ALU.add))
            return v.tensor_scalar(out=diag8[:], in0=c.identf[0:8, 0:8], scalar1=b8[:],
                                   scalar2=None, op0=ALU.mult)
        o_vd = ph.op("vector", vdiag, waits=[o_v3, prev.get("posmm")], ctr=c.vq)

        def posmm(t, ti=ti):
            t.matmul(P[0][:, 0:8], c.uincf[:], c.MASKSF[:, ti * E:(ti + 1) * E],
                     start=True, stop=False)
            return t.matmul(P[0][:, 0:8], c.ones8f[:], diag8[:], start=False, stop=True)
        o_pm = ph.op("tensor", posmm, waits=[o_vd, prev.get("didx")], ctr=c.tq)
        prev["posmm"] = o_pm

        def vidx(v, ti=ti):
            sv = lambda i: selfsync(c, "vector", v, i)
            mf = c.MASKSF[:, ti * E:(ti + 1) * E]
            sv(v.tensor_tensor(out=mp8[:], in0=mf, in1=P[0][:, 0:8], op=ALU.mult))
            sv(v.tensor_reduce(out=dcol[:, 0:1], in_=mp8[:], axis=AX.X, op=ALU.add))
            sv(v.tensor_tensor(out=mp8[:], in0=mf, in1=c.ecapmax[:], op=ALU.mult))
            sv(v.tensor_reduce(out=dcol[:, 1:2], in_=mp8[:], axis=AX.X, op=ALU.add))
            sv(v.tensor_tensor(out=dcol[:, 2:3], in0=dcol[:, 0:1], in1=dcol[:, 1:2],
                               op=ALU.min))
            return v.tensor_copy(out=c.DIDX[:, ti:ti + 1], in_=dcol[:, 2:3])
        o_vi = ph.op("vector", vidx, waits=[o_pm], ctr=c.vq)
        prev["didx"] = o_vi
        if ti == 0 and hasattr(c, "dbg_capture"):
            def vcap(v):
                v.tensor_copy(out=c.dbg_capture[0:8, 0:8], in_=diag8[:])
                v.tensor_copy(out=c.dbg_capture[0:128, 8:16], in_=P[0][:, 0:8])
                v.tensor_copy(out=c.dbg_capture[0:128, 16:20], in_=dcol[:])
                return v.tensor_copy(out=c.dbg_capture[0:8, 20:21], in_=b8[:])
            o_cap = ph.op("vector", vcap, waits=[o_pm], ctr=c.vq)
            prev["didx"] = o_cap

        o_xl = ph.dma("sync", lambda s, ti=ti, par=par: s.dma_start(
            out=xnsb[:, par * D:(par + 1) * D], in_=c.XN[ti * 128:(ti + 1) * 128, :]),
            waits=[prev.get(("sct", par))], ctr=xldc[par])
        o_sc = ph.dma("gpsimd", lambda g, ti=ti, par=par: g.indirect_dma_start(
            out=c.XG, out_offset=bass.IndirectOffsetOnAxis(c.DIDX[:, ti:ti + 1], 0),
            in_=xnsb[:, par * D:(par + 1) * D], in_offset=None),
            waits=[o_xl, o_vi] + ([zf] if ti == 0 else []), ctr=sctc)
        prev[("sct", par)] = o_sc
    ph.emit()

    barrier(c, "b4b")
    ph = Phase(c, "p4b")
    W1sb = ep(nc.sbuf_tensor("p4_w1", [128, 8 * MH], FP16))
    b1row = ep(nc.sbuf_tensor("p4_b1", [1, MH], FP16))
    b2row = ep(nc.sbuf_tensor("p4_b2", [1, D], FP16))
    xgT = ep(nc.sbuf_tensor("p4_xgT", [128, 2 * 8 * TGE], FP16))
    hT = ep(nc.sbuf_tensor("p4_hT", [128, 32 * TGE], FP16))
    w2sl = ep(nc.sbuf_tensor("p4_w2", [128, 2 * D], FP16))
    ysb = ep(nc.sbuf_tensor("p4_y", [128, 2 * D], FP16))
    w1c = sctr(c, "p4w1")
    bxc = sctr(c, "p4bx")
    xgl = [sctr(c, "p4g0"), sctr(c, "p4g1")]
    w2c = [sctr(c, "p4w20"), sctr(c, "p4w21")]
    ygst = [sctr(c, "p4yg0"), sctr(c, "p4yg1")]
    NYB = 2 * NTT
    last = {}
    for e in range(E):
        o_w1 = None
        for dc in range(8):
            o_w1 = ph.dma("gpsimd", lambda g, e=e, dc=dc: g.dma_start(
                out=W1sb[:, dc * MH:(dc + 1) * MH],
                in_=c.ins["W1"][e, dc * 128:(dc + 1) * 128, :]),
                waits=[last.get("w1free")], ctr=w1c)
        o_b1 = ph.dma("gpsimd", lambda g, e=e: g.dma_start(
            out=b1row[:], in_=c.ins["b1"][e].unsqueeze(0)),
            waits=[last.get("b1free")], ctr=bxc)
        o_b2 = ph.dma("gpsimd", lambda g, e=e: g.dma_start(
            out=b2row[:], in_=c.ins["b2"][e].unsqueeze(0)),
            waits=[last.get("b2free")], ctr=bxc)
        for g_ in range(NG):
            gp = g_ % 2
            xgv = xgT[:, gp * 8 * TGE:(gp + 1) * 8 * TGE]
            hTv = hT[:]
            row0 = e * CAP + g_ * TGE
            o_xg = None
            for dc in range(8):
                o_xg = ph.dma("sync", lambda s, dc=dc, row0=row0, xgv=xgv:
                              s.dma_start_transpose(
                                  xgv[:, dc * TGE:(dc + 1) * TGE],
                                  c.XG[row0:row0 + TGE, dc * 128:(dc + 1) * 128]),
                              waits=[last.get(("xg", gp))], ctr=xgl[gp])
            hlast = None
            hm_last = None
            for hc in range(32):
                bk = hc % 4

                def mmh(t, hc=hc, xgv=xgv, bk=bk):
                    for dc in range(8):
                        t.matmul(P[bk][:, 0:TGE],
                                 W1sb[:, dc * MH + hc * 128:dc * MH + (hc + 1) * 128],
                                 xgv[:, dc * TGE:(dc + 1) * TGE],
                                 start=(dc == 0), stop=False)
                    return t.matmul(P[bk][:, 0:TGE],
                                    b1row[0:1, hc * 128:(hc + 1) * 128],
                                    c.ones_row[0:1, 0:TGE], start=False, stop=True)
                o_mh = ph.op("tensor", mmh,
                             waits=[o_xg, o_w1, o_b1, last.get(("pb", bk))], ctr=c.tq)
                hm_last = o_mh

                def evh(s, hc=hc, hTv=hTv, bk=bk):
                    return s.activation(out=hTv[:, hc * TGE:(hc + 1) * TGE],
                                        in_=P[bk][:, 0:TGE], func=AF.Gelu)
                o_eh = ph.op("scalar", evh, waits=[o_mh, last.get("hTb")], ctr=c.sq)
                last[("pb", bk)] = o_eh
                hlast = o_eh
            last[("xg", gp)] = hm_last
            o_w2 = {}
            for pre in range(2):
                o_w2[pre] = ph.dma("sync", lambda s, e=e, hc=pre, wp=pre % 2: s.dma_start(
                    out=w2sl[:, (hc % 2) * D:((hc % 2) + 1) * D],
                    in_=c.W2S[e, hc * 128:(hc + 1) * 128, :]),
                    waits=[last.get(("w2", pre % 2))], ctr=w2c[pre % 2])
            ylast = None
            for hc in range(32):
                wp = hc % 2

                def mmy(t, hc=hc, wp=wp, hTv=hTv):
                    r = None
                    for tt in range(NTT):
                        for oh in range(2):
                            bk = tt * 2 + oh
                            r = t.matmul(
                                P[bk][:, 0:512],
                                hTv[:, hc * TGE + tt * 128:hc * TGE + (tt + 1) * 128],
                                w2sl[:, wp * D + oh * 512:wp * D + (oh + 1) * 512],
                                start=(hc == 0), stop=False)
                    return r
                wts = [o_w2[hc], hlast]
                if hc == 0:
                    wts += [last.get(("pb", b2_)) for b2_ in range(NYB)]
                o_my = ph.op("tensor", mmy, waits=wts, ctr=c.tq)
                last[("w2", wp)] = o_my
                ylast = o_my
                if hc + 2 < 32:
                    o_w2[hc + 2] = ph.dma("sync", lambda s, e=e, hc=hc, wp=wp:
                                          s.dma_start(
                                              out=w2sl[:, wp * D:(wp + 1) * D],
                                              in_=c.W2S[e, (hc + 2) * 128:(hc + 3) * 128, :]),
                                          waits=[o_my], ctr=w2c[wp])
            last["hTb"] = ylast
            o_mb = None
            for tt in range(NTT):
                def mmb(t, tt=tt):
                    r = None
                    for oh in range(2):
                        bk = tt * 2 + oh
                        r = t.matmul(P[bk][:, 0:512], c.ones_row[0:1, 0:128],
                                     b2row[0:1, oh * 512:(oh + 1) * 512],
                                     start=False, stop=True)
                    return r
                o_mb = ph.op("tensor", mmb, waits=[ylast, o_b2], ctr=c.tq)
                yp = tt % 2

                def evy(s, tt=tt, yp=yp):
                    r = None
                    for oh in range(2):
                        bk = tt * 2 + oh
                        r = s.activation(
                            out=ysb[:, yp * D + oh * 512:yp * D + (oh + 1) * 512],
                            in_=P[bk][:, 0:512], func=AF.Copy)
                    return r
                o_ey = ph.op("scalar", evy, waits=[o_mb, last.get(("ysb", yp))], ctr=c.sq)
                for oh in range(2):
                    last[("pb", tt * 2 + oh)] = o_ey
                o_sy = ph.dma("gpsimd", lambda g, row0=row0, tt=tt, yp=yp: g.dma_start(
                    out=c.YG[row0 + tt * 128:row0 + (tt + 1) * 128, :],
                    in_=ysb[:, yp * D:(yp + 1) * D]), waits=[o_ey], ctr=ygst[yp])
                last[("ysb", yp)] = o_sy
        last["w1free"] = ylast
        last["b1free"] = hm_last
        last["b2free"] = o_mb
    ph.emit()


def p4c_gate(c):
    with contextlib.ExitStack() as px:
        _p4c_inner(c, px.enter_context)


def _p4c_inner(c, ep):
    nc = c.nc
    ph = Phase(c, "p4c")
    ymo = ep(nc.sbuf_tensor("p4c_ym", [128, 2 * D], FP16))
    yat = ep(nc.sbuf_tensor("p4c_ya", [128, 2 * D], FP16))
    xin = ep(nc.sbuf_tensor("p4c_x", [128, 2 * D], F32))
    gsil = ep(nc.sbuf_tensor("p4c_g", [128, D], F32))
    x1sb = ep(nc.sbuf_tensor("p4c_x1", [128, 2 * D], F32))
    xcs = ep(nc.sbuf_tensor("p4c_xc", [128, 2 * D], F32))
    sqs = ep(nc.sbuf_tensor("p4c_sq", [128, D], F32))
    stt = ep(nc.sbuf_tensor("p4c_st", [128, 16], F32))
    xn1f = ep(nc.sbuf_tensor("p4c_xn1", [128, D], F32))
    o16 = ep(nc.sbuf_tensor("p4c_o16", [128, 2 * D], FP16))
    xq16 = ep(nc.sbuf_tensor("p4c_xq", [128, 2 * D], FP16))
    qb = SimpleNamespace(
        sm=ep(nc.sbuf_tensor("p4c_qsm", [128, 10 * 64], F32)),
        ind=ep(nc.sbuf_tensor("p4c_qind", [128, 16 * 64], F32)),
        dq16=ep(nc.sbuf_tensor("p4c_qdq", [128, 64], FP16)),
        fb=ep(nc.sbuf_tensor("p4c_qfb", [128, 2 * D], F32)),
        hb=ep(nc.sbuf_tensor("p4c_qhb", [128, 8 * D], FP16)))
    gt = sctr(c, "p4gt")
    yac = [sctr(c, "p4ya0"), sctr(c, "p4ya1")]
    xxc = [sctr(c, "p4xx0"), sctr(c, "p4xx1")]
    x1c = [sctr(c, "p4x10"), sctr(c, "p4x11")]
    xnc = [sctr(c, "p4xn0"), sctr(c, "p4xn1")]
    xqc = [sctr(c, "p4xq0"), sctr(c, "p4xq1")]
    prev = {}
    for ti in range(c.NT):
        p = ti % 2
        ymv = ymo[:, p * D:(p + 1) * D]
        yav = yat[:, p * D:(p + 1) * D]
        xv = xin[:, p * D:(p + 1) * D]
        x1v = x1sb[:, p * D:(p + 1) * D]
        xcv = xcs[:, p * D:(p + 1) * D]
        ss = stt[:, p * 8:(p + 1) * 8]
        ov = o16[:, p * D:(p + 1) * D]
        xqv = xq16[:, p * D:(p + 1) * D]
        o_g = ph.dma("gpsimd", lambda g, ti=ti, ymv=ymv: g.indirect_dma_start(
            out=ymv, out_offset=None, in_=c.YG,
            in_offset=bass.IndirectOffsetOnAxis(c.DIDX[:, ti:ti + 1], 0)),
            waits=[prev.get(("ym", p))], ctr=gt)
        o_ya = ph.dma("sync", lambda s, ti=ti, yav=yav: s.dma_start(
            out=yav, in_=c.YATT[ti * 128:(ti + 1) * 128, :]),
            waits=[prev.get(("ya", p))], ctr=yac[p])
        o_xx = ph.dma("sync", lambda s, ti=ti, xv=xv: s.dma_start(
            out=xv, in_=c.ins["x"][ti * 128:(ti + 1) * 128, :]),
            waits=[prev.get(("xx", p))], ctr=xxc[p])
        o_si = ph.op("scalar", lambda s, ymv=ymv: s.activation(
            out=gsil[:], in_=ymv, func=AF.Silu),
            waits=[o_g, prev.get("gsil")], ctr=c.sq)

        def vmix(v, yav=yav, xv=xv, x1v=x1v):
            selfsync(c, "vector", v,
                     v.tensor_tensor(out=gsil[:], in0=gsil[:], in1=yav, op=ALU.mult))
            return v.tensor_tensor(out=x1v, in0=gsil[:], in1=xv, op=ALU.add)
        o_vm = ph.op("vector", vmix,
                     waits=[o_si, o_ya, o_xx, prev.get(("x1", p))], ctr=c.vq)
        prev[("ym", p)] = o_vm
        prev[("ya", p)] = o_vm
        prev["gsil"] = o_vm
        o_sx1 = ph.dma("sync", lambda s, ti=ti, x1v=x1v: s.dma_start(
            out=c.X1[ti * 128:(ti + 1) * 128, :], in_=x1v), waits=[o_vm], ctr=x1c[p])
        prev[("x1", p)] = o_sx1
        def v1(v, x1v=x1v, ss=ss):
            selfsync(c, "vector", v,
                     v.tensor_reduce(out=ss[:, 0:1], in_=x1v, axis=AX.X, op=ALU.add))
            return v.tensor_scalar_mul(out=ss[:, 1:2], in0=ss[:, 0:1], scalar1=-1.0 / D)
        o_v1 = ph.op("vector", v1, waits=[o_vm], ctr=c.vq)

        def s1(s, x1v=x1v, xcv=xcv, ss=ss):
            s.activation(out=xcv, in_=x1v, func=AF.Identity, bias=ss[:, 1:2])
            return s.activation(out=sqs[:], in_=xcv, func=AF.Square, accum_out=ss[:, 3:4])
        o_s1 = ph.op("scalar", s1, waits=[o_v1], ctr=c.sq)

        def v2(v, ss=ss):
            selfsync(c, "vector", v,
                     v.tensor_scalar(out=ss[:, 4:5], in0=ss[:, 3:4], scalar1=1.0 / D,
                                     scalar2=LNEPS, op0=ALU.mult, op1=ALU.add))
            return v.reciprocal(out=ss[:, 5:6], in_=ss[:, 4:5])
        o_v2 = ph.op("vector", v2, waits=[o_s1], ctr=c.vq)
        o_s2 = ph.op("scalar", lambda s, ss=ss: s.activation(
            out=ss[:, 6:7], in_=ss[:, 5:6], func=AF.Sqrt), waits=[o_v2], ctr=c.sq)

        def v3(v, xcv=xcv, ss=ss, ov=ov):
            v.tensor_scalar(out=xn1f[:], in0=xcv, scalar1=ss[:, 6:7], scalar2=None,
                            op0=ALU.mult)
            return v.tensor_copy(out=ov, in_=xn1f[:])
        o_v3 = ph.op("vector", v3,
                     waits=[o_s2, prev.get(("o16", p)), prev.get("xn1f")], ctr=c.vq)
        o_sxn = ph.dma("gpsimd", lambda g, ti=ti, ov=ov: g.dma_start(
            out=c.XN1[ti * 128:(ti + 1) * 128, :], in_=ov), waits=[o_v3], ctr=xnc[p])
        prev[("o16", p)] = o_sxn
        o_q = emit_quant(c, ph, xn1f[:], xqv, 64, c.gs_cols[:, 0:1], c.gs_cols[:, 1:2],
                         qb, waits=[o_v3, prev.get(("xq", p))])
        prev["xn1f"] = o_q
        o_sxq = ph.dma("gpsimd", lambda g, ti=ti, xqv=xqv: g.dma_start(
            out=c.XQ[ti * 128:(ti + 1) * 128, :], in_=xqv), waits=[o_q], ctr=xqc[p])
        prev[("xq", p)] = o_sxq
    ph.emit()


def p5_prep(c, ep):
    nc = c.nc
    ph = Phase(c, "p5")
    c.Wk1sb = ep(nc.sbuf_tensor("p5_wk1", [128, 6 * D], FP16))
    c.Wv1sb = ep(nc.sbuf_tensor("p5_wv1", [128, 6 * D], FP16))
    wlc = sctr(c, "p5wl")
    o_ct = None
    o_wl = None
    for w, dst in (("Wk1", c.Wk1sb), ("Wv1", c.Wv1sb)):
        for cdc in range(6):
            o_wl = ph.dma("gpsimd", lambda g, w=w, dst=dst, cdc=cdc: g.dma_start(
                out=dst[:, cdc * D:(cdc + 1) * D],
                in_=c.ins[w][cdc * 128:(cdc + 1) * 128, :]), ctr=wlc)
    c.p5toks = [o_wl]

    with contextlib.ExitStack() as p5x:
        e5 = p5x.enter_context
        Wq1sb = e5(nc.sbuf_tensor("p5_wq1", [128, 8 * D], FP16))
        xnt = e5(nc.sbuf_tensor("p5_xnt", [128, 2 * 8 * 512], FP16))
        evb = e5(nc.sbuf_tensor("p5_ev", [128, 4 * 512], FP16))
        o_wq = None
        for dc in range(8):
            o_wq = ph.dma("gpsimd", lambda g, dc=dc: g.dma_start(
                out=Wq1sb[:, dc * D:(dc + 1) * D],
                in_=c.ins["Wq1"][dc * 128:(dc + 1) * 128, :]), ctr=wlc)
        xl = [sctr(c, "p5x0"), sctr(c, "p5x1")]
        q1st = [sctr(c, f"p5q{i}") for i in range(4)]
        last = {}
        for tg in range(c.NTG):
            gb = tg % 2
            xg = xnt[:, gb * 4096:(gb + 1) * 4096]
            o_l = None
            for dc in range(8):
                o_l = ph.dma("sync", lambda s, tg=tg, dc=dc, xg=xg: s.dma_start_transpose(
                    xg[:, dc * 512:(dc + 1) * 512],
                    c.XN1[tg * 512:(tg + 1) * 512, dc * 128:(dc + 1) * 128]),
                    waits=[last.get(("xg", gb))], ctr=xl[gb])
            for oc in range(8):
                bk = oc % 4

                def mm(t, oc=oc, xg=xg, bk=bk):
                    r = None
                    for dc in range(8):
                        r = t.matmul(c.pb[bk],
                                     Wq1sb[:, dc * D + oc * 128:dc * D + (oc + 1) * 128],
                                     xg[:, dc * 512:(dc + 1) * 512],
                                     start=(dc == 0), stop=(dc == 7))
                    return r
                o_mm = ph.op("tensor", mm,
                             waits=[o_l, o_wq, last.get(("pb", bk))] +
                                   ([] if tg else [o_ct]), ctr=c.tq)
                last[("xg", gb)] = o_mm
                sl = oc % 4
                eb = evb[:, sl * 512:(sl + 1) * 512]
                o_ev = ph.op("scalar", lambda s, bk=bk, eb=eb: s.activation(
                    out=eb, in_=c.pb[bk], func=AF.Copy),
                    waits=[o_mm, last.get(("ev", sl))], ctr=c.sq)
                last[("pb", bk)] = o_ev
                o_st = ph.dma("gpsimd", lambda g, oc=oc, tg=tg, eb=eb: g.dma_start(
                    out=c.QT1[oc * 128:(oc + 1) * 128, tg * 512:(tg + 1) * 512], in_=eb),
                    waits=[o_ev], ctr=q1st[sl])
                last[("ev", sl)] = o_st
        ph.emit()


def p6_attn(c):
    with contextlib.ExitStack() as px:
        _p6_inner(c, px.enter_context)


def _p6_inner(c, ep):
    nc = c.nc
    NKT = CTXT // 128
    NKG = CTXT // 512
    ph = Phase(c, "p6")
    khT = ep(nc.sbuf_tensor("p6_khT", [128, 2 * CTXT], FP16))
    vh = ep(nc.sbuf_tensor("p6_vh", [128, 2 * CTXT], FP16))
    qh = ep(nc.sbuf_tensor("p6_qh", [128, 2 * 128], FP16))
    Psb = ep(nc.sbuf_tensor("p6_P", [128, 2 * CTXT], FP16))
    rmx = ep(nc.sbuf_tensor("p6_rm", [128, 2 * 16], F32))
    PT = ep(nc.sbuf_tensor("p6_PT", [128, NKT * 1024], FP16))
    otb = ep(nc.sbuf_tensor("p6_ot", [128, 2 * 512], FP16))
    ctb = ep(nc.sbuf_tensor("p6_ctb", [128, 2 * 6 * 512], FP16))
    qlc = [sctr(c, "p6q0"), sctr(c, "p6q1")]
    pstc = [sctr(c, "p6p0"), sctr(c, "p6p1")]
    ptl = [sctr(c, "p6t0"), sctr(c, "p6t1")]
    otc = [sctr(c, "p6o0"), sctr(c, "p6o1")]
    ctl = [sctr(c, "p6c0"), sctr(c, "p6c1")]
    last = {}
    PS = c.PS
    NSQ = c.T // 1024

    def emit_khvh(h):
        hp = h % 2
        for kg in range(NKG):
            cp = kg % 2
            ctv = ctb[:, cp * 3072:(cp + 1) * 3072]
            o_ld = None
            for cdc in range(6):
                o_ld = ph.dma("sync", lambda s, kg=kg, cdc=cdc, ctv=ctv:
                              s.dma_start_transpose(
                                  ctv[:, cdc * 512:(cdc + 1) * 512],
                                  c.CTX16[kg * 512:(kg + 1) * 512,
                                          cdc * 128:(cdc + 1) * 128]),
                              waits=[last.get(("ctb", cp))], ctr=ctl[cp])

            def mm1(t, h=h, ctv=ctv):
                r = None
                for cdc in range(6):
                    r = t.matmul(c.pb[4],
                                 c.Wk1sb[:, cdc * D + h * 128:cdc * D + (h + 1) * 128],
                                 ctv[:, cdc * 512:(cdc + 1) * 512],
                                 start=(cdc == 0), stop=(cdc == 5))
                return r
            o_m = ph.op("tensor", mm1, waits=c.p5toks + [o_ld, last.get("pb4")],
                        ctr=c.tq)
            o_e = ph.op("scalar", lambda s, kg=kg, hp=hp: s.activation(
                out=khT[:, hp * CTXT + kg * 512:hp * CTXT + (kg + 1) * 512],
                in_=c.pb[4], func=AF.Copy), waits=[o_m], ctr=c.sq)
            last["pb4"] = o_e
            o_vh = None
            for kt4 in range(4):
                kt = kg * 4 + kt4

                def mm2(t, h=h, kt4=kt4, ctv=ctv):
                    r = None
                    for cdc in range(6):
                        r = t.matmul(c.pb[5][:, 0:128],
                                     ctv[:, cdc * 512 + kt4 * 128:cdc * 512 + (kt4 + 1) * 128],
                                     c.Wv1sb[:, cdc * D + h * 128:cdc * D + (h + 1) * 128],
                                     start=(cdc == 0), stop=(cdc == 5))
                    return r
                o_m2 = ph.op("tensor", mm2,
                             waits=c.p5toks + [o_ld, last.get("pb5")], ctr=c.tq)
                o_e2 = ph.op("vector", lambda v, kt=kt, hp=hp: v.tensor_copy(
                    out=vh[:, hp * CTXT + kt * 128:hp * CTXT + (kt + 1) * 128],
                    in_=c.pb[5][:, 0:128]), waits=[o_m2], ctr=c.vq)
                last["pb5"] = o_e2
                o_vh = o_e2
            last[("ctb", cp)] = o_vh
            last[("khdone", hp)] = o_e
            last[("vhdone", hp)] = o_vh

    def emit_softmax_tile(h, ti):
        hp = h % 2
        par = ti % 2
        qv = qh[:, par * 128:(par + 1) * 128]
        pv = Psb[:, par * CTXT:(par + 1) * CTXT]
        rm = rmx[:, par * 16:(par + 1) * 16]
        o_ql = ph.dma("sync", lambda s, h=h, ti=ti, qv=qv: s.dma_start(
            out=qv, in_=c.QT1[h * 128:(h + 1) * 128, ti * 128:(ti + 1) * 128]),
            waits=[last.get(("qh", par))], ctr=qlc[par])
        o_exps = []
        for half in range(2):
            k0 = half * 2048

            def mmq(t, qv=qv, hp=hp, k0=k0):
                r = None
                for kg in range(4):
                    r = t.matmul(PS[:, kg * 512:(kg + 1) * 512], qv,
                                 khT[:, hp * CTXT + k0 + kg * 512:
                                     hp * CTXT + k0 + (kg + 1) * 512],
                                 start=True, stop=True)
                return r
            o_mq = ph.op("tensor", mmq,
                         waits=[o_ql, last.get(("khdone", hp)), last.get("ps03")],
                         ctr=c.tq)
            last[("qh", par)] = o_mq

            def rmax(v, rm=rm, half=half):
                sv = lambda i: selfsync(c, "vector", v, i)
                sv(v.tensor_reduce(out=rm[:, half:half + 1], in_=PS[:, 0:2048],
                                   axis=AX.X, op=ALU.max))
                return v.tensor_scalar_mul(out=rm[:, 2 + half:3 + half],
                                           in0=rm[:, half:half + 1],
                                           scalar1=-ASCALE)
            o_rm = ph.op("vector", rmax,
                         waits=[o_mq, last.get(("rmx", par))], ctr=c.vq)

            def expev(s, pv=pv, rm=rm, half=half, k0=k0):
                return s.activation(out=pv[:, k0:k0 + 2048], in_=PS[:, 0:2048],
                                    func=AF.Exp, scale=ASCALE,
                                    bias=rm[:, 2 + half:3 + half],
                                    accum_out=rm[:, 4 + half:5 + half])
            o_ex = ph.op("scalar", expev,
                         waits=[o_rm, last.get(("Psb", par))], ctr=c.sq)
            last["ps03"] = o_ex
            o_exps.append(o_ex)

        def vfix(v, rm=rm):
            sv = lambda i: selfsync(c, "vector", v, i)
            sv(v.tensor_tensor(out=rm[:, 6:7], in0=rm[:, 0:1], in1=rm[:, 1:2],
                               op=ALU.max))
            sv(v.tensor_tensor(out=rm[:, 8:9], in0=rm[:, 0:1], in1=rm[:, 6:7],
                               op=ALU.subtract))
            sv(v.tensor_tensor(out=rm[:, 9:10], in0=rm[:, 1:2], in1=rm[:, 6:7],
                               op=ALU.subtract))
            return v.tensor_scalar_mul(out=rm[:, 8:10], in0=rm[:, 8:10],
                                       scalar1=ASCALE)
        o_vf = ph.op("vector", vfix, waits=[], ctr=c.vq)
        o_sf = ph.op("scalar", lambda s, rm=rm: s.activation(
            out=rm[:, 10:12], in_=rm[:, 8:10], func=AF.Exp),
            waits=[o_vf, o_exps[1]], ctr=c.sq)

        def vnorm(v, rm=rm, pv=pv):
            sv = lambda i: selfsync(c, "vector", v, i)
            sv(v.tensor_tensor(out=rm[:, 12:14], in0=rm[:, 4:6], in1=rm[:, 10:12],
                               op=ALU.mult))
            sv(v.tensor_tensor(out=rm[:, 12:13], in0=rm[:, 12:13],
                               in1=rm[:, 13:14], op=ALU.add))
            sv(v.reciprocal(out=rm[:, 14:15], in_=rm[:, 12:13]))
            sv(v.tensor_scalar(out=rm[:, 10:12], in0=rm[:, 10:12],
                               scalar1=rm[:, 14:15], scalar2=None, op0=ALU.mult))
            v.tensor_scalar(out=pv[:, 0:2048], in0=pv[:, 0:2048],
                            scalar1=rm[:, 10:11], scalar2=None, op0=ALU.mult)
            return v.tensor_scalar(out=pv[:, 2048:4096], in0=pv[:, 2048:4096],
                                   scalar1=rm[:, 11:12], scalar2=None, op0=ALU.mult)
        o_vn = ph.op("vector", vnorm, waits=[o_sf], ctr=c.vq)
        last[("rmx", par)] = o_vn
        o_ps = ph.dma("gpsimd", lambda g, ti=ti, hp=hp, pv=pv: g.dma_start(
            out=c.PSCR[hp * c.T + ti * 128:hp * c.T + (ti + 1) * 128, :], in_=pv),
            waits=[o_vn, last.get(("pscr", hp))], ctr=pstc[par])
        last[("Psb", par)] = o_ps
        return o_ps

    def emit_av(h, sq):
        hp = h % 2
        spar = sq % 2
        o_pt = None
        for kc in range(NKT):
            o_pt = ph.dma("sync", lambda s, sq=sq, kc=kc, hp=hp:
                          s.dma_start_transpose(
                              PT[:, kc * 1024:(kc + 1) * 1024],
                              c.PSCR[hp * c.T + sq * 1024:hp * c.T + (sq + 1) * 1024,
                                     kc * 128:(kc + 1) * 128]),
                          waits=[last.get(("plast", hp)), last.get("PTbusy")],
                          ctr=ptl[spar])
        o_av_last = None
        for qg in range(2):
            bk = 6 + qg

            def mmav(t, qg=qg, bk=bk, hp=hp):
                r = None
                for kc in range(NKT):
                    r = t.matmul(c.pb[bk],
                                 vh[:, hp * CTXT + kc * 128:hp * CTXT + (kc + 1) * 128],
                                 PT[:, kc * 1024 + qg * 512:kc * 1024 + (qg + 1) * 512],
                                 start=(kc == 0), stop=(kc == NKT - 1))
                return r
            o_av = ph.op("tensor", mmav,
                         waits=[o_pt, last.get(("vhdone", hp)), last.get(("pb", bk))],
                         ctr=c.tq)
            ob = otb[:, qg * 512:(qg + 1) * 512]
            o_oe = ph.op("scalar", lambda s, bk=bk, ob=ob: s.activation(
                out=ob, in_=c.pb[bk], func=AF.Copy),
                waits=[o_av, last.get(("otb", qg))], ctr=c.sq)
            last[("pb", bk)] = o_oe
            o_ost = ph.dma("gpsimd", lambda g, h=h, sq=sq, qg=qg, ob=ob: g.dma_start(
                out=c.OUTT[h * 128:(h + 1) * 128,
                           sq * 1024 + qg * 512:sq * 1024 + (qg + 1) * 512],
                in_=ob), waits=[o_oe], ctr=otc[qg])
            last[("otb", qg)] = o_ost
            o_av_last = o_av
        last["PTbusy"] = o_av_last
        return o_av_last

    emit_khvh(0)
    SPT = c.NT // NSQ
    for h in range(L1H + 1):
        av_done = None
        for ti in range(c.NT):
            if h < L1H:
                o_ps = emit_softmax_tile(h, ti)
                if ti == c.NT - 1:
                    last[("plast", h % 2)] = o_ps
            if h > 0 and ti % SPT == 0:
                av_done = emit_av(h - 1, ti // SPT)
            if ti == c.NT - SPT and h + 1 < L1H:
                emit_khvh(h + 1)
        if h > 0:
            last[("pscr", (h - 1) % 2)] = av_done
    ph.emit()


def p65_ffn(c):
    with contextlib.ExitStack() as px:
        _p65a_inner(c, px.enter_context)
    barrier(c, "b65b")
    with contextlib.ExitStack() as px:
        _p65b_inner(c, px.enter_context)
    barrier(c, "b65c")
    with contextlib.ExitStack() as px:
        _p65c_inner(c, px.enter_context)


def _p65a_inner(c, ep):
    nc = c.nc
    T = c.T
    ph = Phase(c, "p65a")
    wsl = ep(nc.sbuf_tensor("p65_wsl", [128, 2 * MH], F32))
    gmx = ep(nc.sbuf_tensor("p65_gmx", [128, 4], F32))
    grow = ep(nc.sbuf_tensor("p65_grow", [1, 8], F32))
    wq16 = ep(nc.sbuf_tensor("p65_wq16", [128, 2 * 1024], FP16))
    qb = SimpleNamespace(
        sm=ep(nc.sbuf_tensor("p65_qsm", [128, 10 * 64], F32)),
        ind=ep(nc.sbuf_tensor("p65_qind", [128, 16 * 64], F32)),
        dq16=ep(nc.sbuf_tensor("p65_qdq", [128, 64], FP16)),
        fb=ep(nc.sbuf_tensor("p65_qfb", [128, 2 * 1024], F32)),
        hb=ep(nc.sbuf_tensor("p65_qhb", [128, 8 * 1024], FP16)))
    wlc = [sctr(c, "p65w0"), sctr(c, "p65w1")]
    wst = [sctr(c, "p65s0"), sctr(c, "p65s1")]
    WCFG = (("GW", c.GQ, FH, D, 4), ("UW", c.UQ, FH, D, 6), ("DW", c.DQ, D, FH, 8))
    prevw16 = {}
    qci = 0
    o_wdone = None
    for wname, wdst, R, Ccol, gslot in WCFG:
        nsl = R // 128
        o_red_last = None
        prevld = {0: o_wdone, 1: o_wdone}
        for i in range(nsl):
            b = i % 2
            o_l = ph.dma("sync", lambda s, wname=wname, i=i, b=b, Ccol=Ccol: s.dma_start(
                out=wsl[:, b * MH:b * MH + Ccol],
                in_=c.ins[wname][i * 128:(i + 1) * 128, :]),
                waits=[prevld.get(b)], ctr=wlc[b])

            def vred(v, i=i, b=b, Ccol=Ccol):
                sv = lambda j: selfsync(c, "vector", v, j)
                sv(v.tensor_reduce(out=gmx[:, 1:2], in_=wsl[:, b * MH:b * MH + Ccol],
                                   axis=AX.X, op=ALU.max, apply_absolute_value=True))
                if i == 0:
                    return v.tensor_copy(out=gmx[:, 0:1], in_=gmx[:, 1:2])
                return v.tensor_tensor(out=gmx[:, 0:1], in0=gmx[:, 0:1],
                                       in1=gmx[:, 1:2], op=ALU.max)
            o_r = ph.op("vector", vred, waits=[o_l], ctr=c.vq)
            prevld[b] = o_r
            o_red_last = o_r
        o_tp = ph.op("tensor", lambda t: t.transpose(
            c.PS[0:1, 0:128], gmx[:, 0:1], c.identf[:]),
            waits=[o_red_last, last_ps(c, ph)], ctr=c.tq)

        def vg(v):
            sv = lambda j: selfsync(c, "vector", v, j)
            sv(v.tensor_reduce(out=grow[0:1, 0:1], in_=c.PS[0:1, 0:128], axis=AX.X,
                               op=ALU.max))
            sv(v.reciprocal(out=grow[0:1, 1:2], in_=grow[0:1, 0:1]))
            sv(v.tensor_scalar_mul(out=grow[0:1, 2:3], in0=grow[0:1, 1:2],
                                   scalar1=448.0))
            return v.tensor_scalar_mul(out=grow[0:1, 3:4], in0=grow[0:1, 0:1],
                                       scalar1=1.0 / 2688.0)
        o_vg = ph.op("vector", vg, waits=[o_tp], ctr=c.vq)
        o_bc = ph.op("tensor", lambda t: t.matmul(
            c.PS[:, 0:2], c.ones_rowf[0:1, 0:128], grow[0:1, 2:4],
            start=True, stop=True), waits=[o_vg], ctr=c.tq)
        o_gc = ph.op("vector", lambda v, gslot=gslot: v.tensor_copy(
            out=c.gs_cols[:, gslot:gslot + 2], in_=c.PS[:, 0:2]),
            waits=[o_bc], ctr=c.vq)
        set_last_ps(c, o_gc)
        prevld = {0: o_gc, 1: o_gc}
        for i in range(nsl):
            b = i % 2
            o_l = ph.dma("sync", lambda s, wname=wname, i=i, b=b, Ccol=Ccol: s.dma_start(
                out=wsl[:, b * MH:b * MH + Ccol],
                in_=c.ins[wname][i * 128:(i + 1) * 128, :]),
                waits=[prevld.get(b)], ctr=wlc[b])
            o_qlast = None
            for qc in range(Ccol // 1024):
                sl16 = qci % 2
                qci += 1
                o_q = emit_quant(c, ph,
                                 wsl[:, b * MH + qc * 1024:b * MH + (qc + 1) * 1024],
                                 wq16[:, sl16 * 1024:(sl16 + 1) * 1024],
                                 64, c.gs_cols[:, gslot:gslot + 1],
                                 c.gs_cols[:, gslot + 1:gslot + 2], qb,
                                 waits=[o_l, o_gc, prevw16.get(sl16)])
                o_s = ph.dma("gpsimd", lambda g, wdst=wdst, i=i, qc=qc, sl16=sl16:
                             g.dma_start(out=wdst[i * 128:(i + 1) * 128,
                                                  qc * 1024:(qc + 1) * 1024],
                                         in_=wq16[:, sl16 * 1024:(sl16 + 1) * 1024]),
                             waits=[o_q], ctr=wst[sl16])
                prevw16[sl16] = o_s
                o_qlast = o_q
            prevld[b] = o_qlast
        o_wdone = o_qlast
    ph.emit()


def _p65b_inner(c, ep):
    nc = c.nc
    T = c.T
    ph = Phase(c, "p65b")
    TH = T // 4 if T >= 4096 else T // 2
    xqT = ep(nc.sbuf_tensor("p65_xqT", [128, 8 * (T // 2)], FP16))
    gqt = ep(nc.sbuf_tensor("p65_gqt", [128, 2 * 8 * 512], FP16))
    uqt = ep(nc.sbuf_tensor("p65_uqt", [128, 2 * 8 * 512], FP16))
    gsl = ep(nc.sbuf_tensor("p65_gsl", [128, 512], F32))
    hfl = ep(nc.sbuf_tensor("p65_hf", [128, 8 * 512], F32))
    hq16 = ep(nc.sbuf_tensor("p65_hq", [128, 8 * 512], FP16))
    qb2 = SimpleNamespace(
        sm=ep(nc.sbuf_tensor("p65_qsm2", [128, 10 * 256], F32)),
        ind=ep(nc.sbuf_tensor("p65_qind2", [128, 16 * 256], FP16)),
        dq16=ep(nc.sbuf_tensor("p65_qdq2", [128, 256], FP16)),
        fb=ep(nc.sbuf_tensor("p65_qfb2", [128, 2 * 512], F32)),
        hb=ep(nc.sbuf_tensor("p65_qhb2", [128, 8 * 512], FP16)))
    xql = sctr(c, "p65x")
    gul = [sctr(c, "p65g0"), sctr(c, "p65g1")]
    hst = [sctr(c, "p65h0"), sctr(c, "p65h1")]
    last = {}
    for half in range(T // TH):
        t0 = half * TH
        o_x = None
        for dc in range(8):
            o_x = ph.dma("sync", lambda s, dc=dc, t0=t0: s.dma_start_transpose(
                xqT[:, dc * TH:(dc + 1) * TH],
                c.XQ[t0:t0 + TH, dc * 128:(dc + 1) * 128]),
                waits=[last.get("xqT")], ctr=xql)
        for hg in range(8):
            hp = hg % 2
            o_g = None
            for dc in range(8):
                o_g = ph.dma("sync", lambda s, dc=dc, hg=hg, hp=hp: s.dma_start_transpose(
                    gqt[:, hp * 4096 + dc * 512:hp * 4096 + (dc + 1) * 512],
                    c.GQ[hg * 512:(hg + 1) * 512, dc * 128:(dc + 1) * 128]),
                    waits=[last.get(("gq", hp))], ctr=gul[hp])
                o_g = ph.dma("sync", lambda s, dc=dc, hg=hg, hp=hp: s.dma_start_transpose(
                    uqt[:, hp * 4096 + dc * 512:hp * 4096 + (dc + 1) * 512],
                    c.UQ[hg * 512:(hg + 1) * 512, dc * 128:(dc + 1) * 128]),
                    waits=[last.get(("gq", hp))], ctr=gul[hp])
            o_mmlast = None
            ntt = TH // 128
            _qb = int(_os.environ.get('QBATCH', '8'))
            for sb0 in range(0, ntt, _qb):
                kk = min(_qb, ntt - sb0)
                items = []
                for j in range(kk):
                    tt = sb0 + j
                    gbk = tt % 2
                    ubk = 2 + tt % 2

                    def mmg(t, tt=tt, hp=hp, gbk=gbk, ubk=ubk):
                        for dc in range(8):
                            t.matmul(c.pb[gbk],
                                     xqT[:, dc * TH + tt * 128:dc * TH + (tt + 1) * 128],
                                     gqt[:, hp * 4096 + dc * 512:hp * 4096 + (dc + 1) * 512],
                                     start=(dc == 0), stop=(dc == 7))
                        r = None
                        for dc in range(8):
                            r = t.matmul(c.pb[ubk],
                                         xqT[:, dc * TH + tt * 128:dc * TH + (tt + 1) * 128],
                                         uqt[:, hp * 4096 + dc * 512:hp * 4096 + (dc + 1) * 512],
                                         start=(dc == 0), stop=(dc == 7))
                        return r
                    o_mm = ph.op("tensor", mmg,
                                 waits=[o_x, o_g, last.get(("pb", gbk)),
                                        last.get(("pb", ubk))], ctr=c.tq)
                    o_mmlast = o_mm
                    o_sil = ph.op("scalar", lambda s, gbk=gbk: s.activation(
                        out=gsl[:], in_=c.pb[gbk], func=AF.Silu),
                        waits=[o_mm, last.get("gsl")], ctr=c.sq)

                    def vh_(v, ubk=ubk, j=j):
                        return v.tensor_tensor(out=hfl[:, j * 512:(j + 1) * 512],
                                               in0=gsl[:], in1=c.pb[ubk], op=ALU.mult)
                    o_vh = ph.op("vector", vh_,
                                 waits=[o_sil, last.get(("hfl", j))], ctr=c.vq)
                    last[("pb", gbk)] = o_vh
                    last[("pb", ubk)] = o_vh
                    last["gsl"] = o_vh
                    items.append((hfl[:, j * 512:(j + 1) * 512],
                                  hq16[:, j * 512:(j + 1) * 512],
                                  [o_vh, last.get(("hq", j))]))
                qouts = emit_quant_batch(c, ph, items, 32,
                                         c.gs_cols[:, 2:3], c.gs_cols[:, 3:4], qb2)
                for j in range(kk):
                    tt = sb0 + j
                    ti_glob = (t0 // 128) + tt
                    hb_ = tt % 2
                    o_hs = ph.dma("gpsimd", lambda g, ti_glob=ti_glob, hg=hg, j=j:
                                  g.dma_start(out=c.HQ[ti_glob * 128:(ti_glob + 1) * 128,
                                                       hg * 512:(hg + 1) * 512],
                                              in_=hq16[:, j * 512:(j + 1) * 512]),
                                  waits=[qouts[j]], ctr=hst[j % 2])
                    last[("hfl", j)] = qouts[j]
                    last[("hq", j)] = o_hs
            last[("gq", hp)] = o_mmlast
        last["xqT"] = o_mmlast
    ph.emit()


def _p65c_inner(c, ep):
    nc = c.nc
    ph = Phase(c, "p65c")
    dqT = ep(nc.sbuf_tensor("p65_dqT", [128, 32 * 1024], FP16))
    hqT = ep(nc.sbuf_tensor("p65_hqT", [128, 2 * 32 * 512], FP16))
    f1b = ep(nc.sbuf_tensor("p65_f1", [128, 2 * D], FP16))
    dql = sctr(c, "p65d")
    hql = [sctr(c, "p65hq0"), sctr(c, "p65hq1")]
    f1st = [sctr(c, "p65f0"), sctr(c, "p65f1")]
    o_d = None
    for hc in range(32):
        o_d = ph.dma("sync", lambda s, hc=hc: s.dma_start_transpose(
            dqT[:, hc * 1024:(hc + 1) * 1024],
            c.DQ[0:1024, hc * 128:(hc + 1) * 128]), ctr=dql)
    last = {}
    for tg in range(c.NTG):
        gp = tg % 2
        hv = hqT[:, gp * 16384:(gp + 1) * 16384]
        o_h = None
        for hc in range(32):
            o_h = ph.dma("sync", lambda s, tg=tg, hc=hc, hv=hv: s.dma_start_transpose(
                hv[:, hc * 512:(hc + 1) * 512],
                c.HQ[tg * 512:(tg + 1) * 512, hc * 128:(hc + 1) * 128]),
                waits=[last.get(("hqT", gp))], ctr=hql[gp])
        o_mml = None
        for tt in range(4):
            for og in range(2):
                bk = tt % 2 * 2 + og

                def mmd(t, tt=tt, og=og, bk=bk, hv=hv):
                    r = None
                    for hc in range(32):
                        r = t.matmul(c.pb[bk],
                                     hv[:, hc * 512 + tt * 128:hc * 512 + (tt + 1) * 128],
                                     dqT[:, hc * 1024 + og * 512:hc * 1024 + (og + 1) * 512],
                                     start=(hc == 0), stop=(hc == 31))
                    return r
                o_mm = ph.op("tensor", mmd,
                             waits=[o_h, o_d, last.get(("pb", bk))], ctr=c.tq)
                o_mml = o_mm
                fp = tt % 2
                fb = f1b[:, fp * D + og * 512:fp * D + (og + 1) * 512]
                o_fe = ph.op("scalar", lambda s, bk=bk, fb=fb: s.activation(
                    out=fb, in_=c.pb[bk], func=AF.Copy),
                    waits=[o_mm, last.get(("f1", fp, og))], ctr=c.sq)
                last[("pb", bk)] = o_fe
                ti = tg * 4 + tt
                o_fs = ph.dma("gpsimd", lambda g, ti=ti, og=og, fb=fb: g.dma_start(
                    out=c.F1[ti * 128:(ti + 1) * 128, og * 512:(og + 1) * 512], in_=fb),
                    waits=[o_fe], ctr=f1st[fp])
                last[("f1", fp, og)] = o_fs
        last[("hqT", gp)] = o_mml
    ph.emit()


def last_ps(c, ph):
    return getattr(c, "_ps_last", None)


def set_last_ps(c, op):
    c._ps_last = op


def p7_final(c):
    with contextlib.ExitStack() as px:
        _p7_inner(c, px.enter_context)


def _p7_inner(c, ep):
    nc = c.nc
    ph = Phase(c, "p7")
    Wo1sb = ep(nc.sbuf_tensor("p7_wo", [128, 8 * D], FP16))
    otl = ep(nc.sbuf_tensor("p7_ot", [128, 2 * 8 * 128], FP16))
    f1l = ep(nc.sbuf_tensor("p7_f1", [128, 2 * D], FP16))
    x1l = ep(nc.sbuf_tensor("p7_x1", [128, 2 * D], F32))
    g1 = ep(nc.sbuf_tensor("p7_g1", [128, D], F32))
    tsb = ep(nc.sbuf_tensor("p7_t", [128, D], F32))
    osb = ep(nc.sbuf_tensor("p7_o", [128, 2 * D], F32))
    wol = sctr(c, "p7w")
    otc = [sctr(c, "p7o0"), sctr(c, "p7o1")]
    f1c = [sctr(c, "p7f0"), sctr(c, "p7f1")]
    x1c = [sctr(c, "p7x0"), sctr(c, "p7x1")]
    ostc = [sctr(c, "p7s0"), sctr(c, "p7s1")]
    o_w = None
    for dc in range(8):
        o_w = ph.dma("gpsimd", lambda g, dc=dc: g.dma_start(
            out=Wo1sb[:, dc * D:(dc + 1) * D],
            in_=c.ins["Wo1"][dc * 128:(dc + 1) * 128, :]), ctr=wol)
    last = {}
    for ti in range(c.NT):
        p = ti % 2
        ov = otl[:, p * 1024:(p + 1) * 1024]
        fv = f1l[:, p * D:(p + 1) * D]
        xv = x1l[:, p * D:(p + 1) * D]
        outv = osb[:, p * D:(p + 1) * D]
        o_ot = ph.dma("sync", lambda s, ti=ti, ov=ov: s.dma_start(
            out=ov.rearrange("q (a t) -> q a t", t=128),
            in_=c.OUTT[0:D, ti * 128:(ti + 1) * 128].rearrange(
                "(a q) t -> q a t", q=128)),
            waits=[last.get(("ot", p))], ctr=otc[p])
        o_f1 = ph.dma("sync", lambda s, ti=ti, fv=fv: s.dma_start(
            out=fv, in_=c.F1[ti * 128:(ti + 1) * 128, :]),
            waits=[last.get(("f1", p))], ctr=f1c[p])
        o_x1 = ph.dma("sync", lambda s, ti=ti, xv=xv: s.dma_start(
            out=xv, in_=c.X1[ti * 128:(ti + 1) * 128, :]),
            waits=[last.get(("x1", p))], ctr=x1c[p])
        o_sil = ph.op("scalar", lambda s, fv=fv: s.activation(
            out=g1[:], in_=fv, func=AF.Silu), waits=[o_f1, last.get("g1")], ctr=c.sq)
        for og in range(2):
            bk = p * 2 + og

            def mms(t, ov=ov, og=og, bk=bk):
                for ic in range(8):
                    t.matmul(c.pb[bk], ov[:, ic * 128:(ic + 1) * 128],
                             Wo1sb[:, ic * D + og * 512:ic * D + (og + 1) * 512],
                             start=(ic == 0), stop=False)
                return t.matmul(c.pb[bk], c.ones_row[0:1, 0:128],
                                c.bo1_row[0:1, og * 512:(og + 1) * 512],
                                start=False, stop=True)
            o_mm = ph.op("tensor", mms,
                         waits=[o_ot, o_w, last.get(("pb", bk))], ctr=c.tq)

            def vout(v, og=og, bk=bk, outv=outv, xv=xv):
                selfsync(c, "vector", v, v.tensor_tensor(
                    out=tsb[:, og * 512:(og + 1) * 512],
                    in0=g1[:, og * 512:(og + 1) * 512], in1=c.pb[bk], op=ALU.mult))
                return v.tensor_tensor(out=outv[:, og * 512:(og + 1) * 512],
                                       in0=tsb[:, og * 512:(og + 1) * 512],
                                       in1=xv[:, og * 512:(og + 1) * 512], op=ALU.add)
            o_vo = ph.op("vector", vout,
                         waits=[o_mm, o_sil, o_x1, last.get(("ost", p))], ctr=c.vq)
            last[("pb", bk)] = o_vo
            last["g1"] = o_vo
            last[(ti, og)] = o_vo
        o_st = ph.dma("sync", lambda s, ti=ti, outv=outv: s.dma_start(
            out=c.out[ti * 128:(ti + 1) * 128, :], in_=outv),
            waits=[last[(ti, 0)], last[(ti, 1)]], ctr=ostc[p])
        last[("ost", p)] = o_st
        last[("ot", p)] = o_mm
        last[("f1", p)] = o_vo
        last[("x1", p)] = o_vo
    ph.emit()



def sctr(c, name):
    if name not in c.ctrs:
        c.ctrs[name] = Ctr(c.eg(c.nc.semaphore(name)))
    return c.ctrs[name]


def make_ctx(B, S, dbg=()):
    T = B * S
    c = SimpleNamespace()
    c.B, c.S, c.T = B, S, T
    c.NT = T // 128
    c.NCH = S // 128
    c.CAP = 2 * (T // E)
    c.NTG = T // 512
    nc = bass.Bass("TRN2", target_bir_lowering=False, detect_race_conditions=False)
    c.nc = nc

    def din(name, shape):
        return nc.dram_tensor(name, shape, F32, kind="ExternalInput").ap()

    def dscr(name, shape, dt=FP16):
        kind = "ExternalOutput" if name in dbg else "Internal"
        return nc.dram_tensor(name, shape, dt, kind=kind).ap()
    c.ins = {}
    for nm, shape in [
        ("x", [T, D]), ("Wq0", [D, D]), ("bq0", [D]), ("Wk0", [D, D]), ("bk0", [D]),
        ("Wv0", [D, D]), ("bv0", [D]), ("Wb0", [D, D]), ("bb0", [D]),
        ("Wa0", [D, D]), ("ba0", [D]),
        ("Wr", [D, E]), ("br", [E]), ("W1", [E, D, MH]), ("b1", [E, MH]),
        ("W2", [E, MH, D]), ("b2", [E, D]),
        ("ctx", [CTXT, CD]), ("Wq1", [D, D]), ("Wk1", [CD, D]), ("Wv1", [CD, D]),
        ("Wo1", [D, D]), ("bo1", [D]),
        ("GW", [FH, D]), ("UW", [FH, D]), ("DW", [D, FH]),
        ("s_in", [1]), ("s_in_d", [1]),
    ]:
        c.ins[nm] = din(nm, shape)
    c.out = nc.dram_tensor("OUT", [T, D], F32, kind="ExternalOutput").ap()
    c.W16 = {w + "": dscr(w + "16", [D, D]) for w in ("Wq0", "Wk0", "Wv0", "Wb0", "Wa0")}
    c.XN = dscr("XN", [T, D])
    c.QT = dscr("QT", [D, T])
    c.KT = dscr("KT", [D, T])
    c.KTM = dscr("KTM", [T, D])
    c.VTM = dscr("VTM", [T, D])
    c.VBS = dscr("VBS", [T, D])
    c.YATT = dscr("YATT", [T, D])
    c.W2S = dscr("W2S", [E, MH, D])
    c.XG = dscr("XG", [E * c.CAP, D])
    c.YG = dscr("YG", [E * c.CAP, D])
    c.X1 = dscr("X1", [T, D], F32)
    c.XN1 = dscr("XN1", [T, D])
    c.XQ = dscr("XQ", [T, D])
    c.QT1 = dscr("QT1", [D, T])
    c.CTX16 = dscr("CTX16", [CTXT, CD])
    c.PSCR = dscr("PSCR", [2 * T, CTXT], FP16)
    c.OUTT = dscr("OUTT", [D, T])
    c.GQ = dscr("GQ", [FH, D])
    c.UQ = dscr("UQ", [FH, D])
    c.DQ = dscr("DQ", [D, FH])
    c.HQ = dscr("HQ", [T, FH])
    c.F1 = dscr("F1", [T, D])
    c.DBGS = dscr("DBGS", [128, 3 * c.NT * NH], F32)
    return c


def build(B=4, S=2048, upto=7, dbg=()):
    c = make_ctx(B, S, dbg)
    nc = c.nc
    with contextlib.ExitStack() as gx:
        c.eg = gx.enter_context
        eg = c.eg
        c.ident = eg(nc.sbuf_tensor("ident", [128, 128], FP16))
        c.identf = eg(nc.sbuf_tensor("identf", [128, 128], F32))
        c.iotav = eg(nc.sbuf_tensor("iotav", [128, 128], F32))
        c.madd = eg(nc.sbuf_tensor("madd", [128, 128], F32))
        c.uinc = eg(nc.sbuf_tensor("uinc", [128, 128], FP16))
        c.ustrf = eg(nc.sbuf_tensor("ustrf", [128, 128], F32))
        c.uincf = eg(nc.sbuf_tensor("uincf", [128, 128], F32))
        c.rowsel = eg(nc.sbuf_tensor("rowsel", [16, NH * 128], F32))
        c.rsel_t = eg(nc.sbuf_tensor("rsel_t", [16, NH * 128], F32))
        c.ones_row = eg(nc.sbuf_tensor("ones_row", [1, 1024], FP16))
        c.ones_rowf = eg(nc.sbuf_tensor("ones_rowf", [1, 128], F32))
        c.ones_colb = eg(nc.sbuf_tensor("ones_colb", [128, 1], FP16))
        c.ones8f = eg(nc.sbuf_tensor("ones8f", [8, 128], F32))
        c.MLA = eg(nc.sbuf_tensor("MLA", [128, c.NT * NH], F32))
        c.BMS = eg(nc.sbuf_tensor("BMS", [128, c.NT * NH], F32))
        c.MASKS = eg(nc.sbuf_tensor("MASKS", [128, c.NT * E], FP16))
        c.MASKSF = eg(nc.sbuf_tensor("MASKSF", [128, c.NT * E], F32))
        c.DIDX = eg(nc.sbuf_tensor("DIDX", [128, c.NT], I32))
        c.S_all = eg(nc.sbuf_tensor("S_all", [128, B * NH * 32], F32))
        c.ecap_col = eg(nc.sbuf_tensor("ecap_col", [8, 1], F32))
        c.ecapmax = eg(nc.sbuf_tensor("ecapmax", [128, 8], F32))
        c.bias_rows = eg(nc.sbuf_tensor("bias_rows", [1, 4 * 1024], FP16))
        c.br_row = eg(nc.sbuf_tensor("br_row", [1, 8], FP16))
        c.bo1_row = eg(nc.sbuf_tensor("bo1_row", [1, 1024], FP16))
        c.bq_cols = eg(nc.sbuf_tensor("bq_cols", [128, 16], F32))
        c.gs_cols = eg(nc.sbuf_tensor("gs_cols", [128, 12], F32))
        c.sc_sm = eg(nc.sbuf_tensor("sc_sm", [1, 8], F32))
        c.cnt_sb = eg(nc.sbuf_tensor("cnt_sb", [8, 64], F32))
        c.constcols = eg(nc.sbuf_tensor("constcols", [128, 4], F32))
        c.PS = eg(nc.psum_tensor("PS", [128, 4096], F32))
        c.pb = [c.PS[:, i * 512:(i + 1) * 512] for i in range(8)]
        c.ctrs = {n: Ctr(eg(nc.semaphore(n))) for n in
                  ["ldA", "ldB", "ldC", "stA", "tq", "vq", "sq", "gq"]}
        c.self_ctrs = {n: Ctr(eg(nc.semaphore("self_" + n)))
                       for n in ("vector", "scalar", "gpsimd", "tensor")}
        for n, ct in c.self_ctrs.items():
            c.ctrs["self_" + n] = ct
        for k, v in c.ctrs.items():
            setattr(c, k if k != "sq" else "sq", v)
        c.ldA, c.ldB, c.ldC, c.stA = (c.ctrs[k] for k in ("ldA", "ldB", "ldC", "stA"))
        c.tq, c.vq, c.sq, c.gq = (c.ctrs[k] for k in ("tq", "vq", "sq", "gq"))
        c.block = eg(nc.Block())

        p0_consts(c)
        if upto >= 1:
            barrier(c, "b1")
            emit_layernorm(c, "p1", c.ins["x"], c.XN)
        if upto >= 2:
            barrier(c, "b2")
            p2_proj(c)
        if upto >= 3:
            barrier(c, "b3")
            p3_deltanet(c)
        if upto >= 4:
            barrier(c, "b4")
            p4_moe(c)
            barrier(c, "b4c")
            p4c_gate(c)
        if upto >= 5:
            barrier(c, "b5")
            with contextlib.ExitStack() as p5s:
                p5_prep(c, p5s.enter_context)
                if upto >= 6:
                    barrier(c, "b6")
                    p6_attn(c)
        if upto >= 6.5:
            barrier(c, "b65")
            p65_ffn(c)
        if upto >= 7:
            barrier(c, "b7")
            p7_final(c)
        if "DBGS" in dbg:
            barrier(c, "bd")
            pdbg(c)
        barrier(c, "bf")
    return c


def pdbg(c):
    ph = Phase(c, "pdbg")
    n = c.NT * NH
    o1 = ph.dma("gpsimd", lambda g: g.dma_start(out=c.DBGS[0:128, 0:n], in_=c.MLA[:]),
                ctr=c.stA)
    o2 = ph.dma("gpsimd", lambda g: g.dma_start(out=c.DBGS[0:128, n:2 * n], in_=c.BMS[:]),
                ctr=c.stA)
    o3 = ph.dma("gpsimd", lambda g: g.dma_start(
        out=c.DBGS[0:128, 2 * n:2 * n + c.NT * E], in_=c.MASKSF[:]), ctr=c.stA)
    ph.emit()


_NAME_MAP = {
    "x": "L0_attn_x", "Wq0": "L0_attn_Wq", "bq0": "L0_attn_bq",
    "Wk0": "L0_attn_Wk", "bk0": "L0_attn_bk", "Wv0": "L0_attn_Wv",
    "bv0": "L0_attn_bv", "Wb0": "L0_attn_Wbeta", "bb0": "L0_attn_bbeta",
    "Wa0": "L0_attn_Walpha", "ba0": "L0_attn_balpha",
    "Wr": "L0_ffn_Wr", "br": "L0_ffn_br", "W1": "L0_ffn_W1", "b1": "L0_ffn_b1",
    "W2": "L0_ffn_W2", "b2": "L0_ffn_b2",
    "ctx": "L1_attn_context", "Wq1": "L1_attn_Wq", "Wk1": "L1_attn_Wk",
    "Wv1": "L1_attn_Wv", "Wo1": "L1_attn_Wout", "bo1": "L1_attn_bout",
    "GW": "L1_ffn_gate_w", "UW": "L1_ffn_up_w", "DW": "L1_ffn_down_w",
    "s_in": "L1_ffn_s_in", "s_in_d": "L1_ffn_s_in_d" + "own",
}


def kernel(**inputs):
    NCORES = 8
    B, S = 4, 2048
    T = B * S
    x = np.ascontiguousarray(np.asarray(inputs["L0_attn_x"], np.float32))
    shards = [np.ascontiguousarray(x[i * B:(i + 1) * B].reshape(T, D))
              for i in range(NCORES)]
    common = {}
    for knl, kref in _NAME_MAP.items():
        if knl == "x":
            continue
        v = np.ascontiguousarray(np.asarray(inputs[kref], np.float32))
        if knl == "ctx":
            v = v.reshape(-1, CD)
        common[knl] = v
    c = build(B=B, S=S, upto=7)
    in_maps = [dict(common, x=shards[i]) for i in range(NCORES)]
    res = run_bass_kernel_spmd(c.nc, in_maps, core_ids=list(range(NCORES)))
    out = np.concatenate([np.asarray(r["OUT"], np.float32).reshape(B, S, D)
                          for r in res.results], axis=0)
    return out


# version 09
# speedup vs baseline: 1.0797x; 1.0797x over previous
import contextlib
import math
import os as _os
from types import SimpleNamespace
import numpy as np
import concourse.bass as bass
import concourse.mybir as mybir
from concourse.bass_utils import run_bass_kernel_spmd

F32 = mybir.dt.float32
BF16 = mybir.dt.bfloat16
FP16 = mybir.dt.float16
I32 = mybir.dt.int32
AF = mybir.ActivationFunctionType
ALU = mybir.AluOpType
AX = mybir.AxisListType

D = 1024
HD = 64
NH = 16
E = 8
MH = 4096
L1H = 8
L1D = 128
CD = 768
CTXT = 4096
FH = 4096
QSCALE = HD ** -0.5
ASCALE = 1.5 * (L1D ** -0.5)
LNEPS = 1e-5
LN2 = math.log(2.0)

ENGS = ("tensor", "vector", "scalar", "sync", "gpsimd")


class Ctr:
    def __init__(self, sem):
        self.sem = sem
        self.n = 0


class Op:
    __slots__ = ("eng", "fn", "waits", "ctr", "k", "tok", "idx", "ph")

    def __init__(self, eng, fn, waits, ctr, k, idx, ph=None):
        self.eng, self.fn, self.waits, self.ctr, self.k, self.idx = \
            eng, fn, list(waits), ctr, k, idx
        self.tok = None
        self.ph = ph


class Hard:

    def __init__(self, op):
        self.op = op


class Phase:
    def __init__(self, c, name):
        self.c = c
        self.name = name
        self.ops = []

    def op(self, eng, fn, waits=(), ctr=None, k=1):
        o = Op(eng, fn, waits, ctr, k, len(self.ops), self)
        self.ops.append(o)
        return o

    def dma(self, eng, fn, waits=(), ctr=None):
        return self.op(eng, fn, waits, ctr, 16)

    def emit(self):
        c = self.c
        for o in self.ops:
            if o.ctr is not None:
                o.ctr.n += o.k
                o.tok = o.ctr.n
        for eng_name in ENGS:
            ops = [o for o in self.ops if o.eng == eng_name]
            if not ops:
                continue

            def body(e, ops=ops, eng_name=eng_name):
                hw = {}
                for o in ops:
                    for w in o.waits:
                        if w is None:
                            continue
                        if isinstance(w, Hard):
                            w = w.op
                            ctr, tok = w.ctr, w.tok
                        elif isinstance(w, Op):
                            if w.eng == eng_name:
                                assert w.ph is not self or w.idx < o.idx, \
                                    f"{self.name}: same-engine forward wait"
                                continue
                            ctr, tok = w.ctr, w.tok
                        else:
                            ctr, tok = w
                        if ctr is None or tok is None:
                            continue
                        if hw.get(id(ctr), -1) < tok:
                            e.wait_ge(ctr.sem, tok)
                            hw[id(ctr)] = tok
                    last = o.fn(e)
                    if o.ctr is not None:
                        assert last is not None, f"{self.name}: op fn returned None"
                        last.then_inc(o.ctr.sem, o.k)
            getattr(c.block, eng_name)(body)


def selfsync(c, engname, eng, instr):
    ctr = c.self_ctrs[engname]
    instr.then_inc(ctr.sem, 1)
    ctr.n += 1
    eng.wait_ge(ctr.sem, ctr.n)


def barrier(c, name):
    snap = [(ctr, ctr.n) for ctr in c.ctrs.values() if ctr.n > 0]
    for eng_name in ENGS:
        def body(e, snap=snap):
            for ctr, n in snap:
                e.wait_ge(ctr.sem, n)
        getattr(c.block, eng_name)(body)



def p0_consts(c):
    ph = Phase(c, "p0")
    o_iota = ph.op("gpsimd", lambda g: g.iota(
        c.iotav[:], [[1, 128]], base=0, channel_multiplier=-1,
        allow_small_or_imprecise_dtypes=True), ctr=c.gq)
    o_i2 = ph.op("gpsimd", lambda g: g.iota(
        c.ecap_col[:, 0:1], [[1, 1]], base=-1, channel_multiplier=c.CAP,
        allow_small_or_imprecise_dtypes=True), ctr=c.gq)
    o_i3 = ph.op("gpsimd", lambda g: g.iota(
        c.ecapmax[:, 0:E], [[c.CAP, E]], base=c.CAP - 1, channel_multiplier=0,
        allow_small_or_imprecise_dtypes=True), ctr=c.gq)
    o_i4 = ph.op("gpsimd", lambda g: g.iota(
        c.rsel_t[:], [[1, NH * 128]], base=0, channel_multiplier=-128,
        allow_small_or_imprecise_dtypes=True), ctr=c.gq)
    for w in ("Wq0", "Wk0", "Wv0", "Wb0", "Wa0"):
        ph.dma("gpsimd", lambda g, w=w: g.dma_start(out=c.W16[w], in_=c.ins[w]), ctr=c.ldC)
    ph.dma("gpsimd", lambda g: g.dma_start(out=c.CTX16, in_=c.ins["ctx"]), ctr=c.ldC)
    for ei in range(E):
        ph.dma("gpsimd", lambda g, ei=ei: g.dma_start(out=c.W2S[ei], in_=c.ins["W2"][ei]),
               ctr=c.ldC)
    for i, bn in enumerate(("bk0", "bv0", "bb0", "ba0")):
        ph.dma("gpsimd", lambda g, i=i, bn=bn: g.dma_start(
            out=c.bias_rows[0:1, i * 1024:(i + 1) * 1024], in_=c.ins[bn].unsqueeze(0)),
            ctr=c.ldC)
    ph.dma("gpsimd", lambda g: g.dma_start(out=c.br_row[:], in_=c.ins["br"].unsqueeze(0)),
           ctr=c.ldC)
    ph.dma("gpsimd", lambda g: g.dma_start(out=c.bo1_row[:], in_=c.ins["bo1"].unsqueeze(0)),
           ctr=c.ldC)
    o_bq = ph.dma("sync", lambda s: s.dma_start(
        out=c.bq_cols[:, 0:8], in_=c.ins["bq0"].rearrange("(a p) -> p a", p=128),
        allow_slow_non_contiguous=True), ctr=c.ldA)
    o_bk = ph.dma("sync", lambda s: s.dma_start(
        out=c.bq_cols[:, 8:16], in_=c.ins["bk0"].rearrange("(a p) -> p a", p=128),
        allow_slow_non_contiguous=True), ctr=c.ldA)
    o_sin = ph.dma("sync", lambda s: s.dma_start(
        out=c.sc_sm[0:1, 0:1], in_=c.ins["s_in"].unsqueeze(0)), ctr=c.ldA)
    o_sind = ph.dma("sync", lambda s: s.dma_start(
        out=c.sc_sm[0:1, 1:2], in_=c.ins["s_in_d"].unsqueeze(0)), ctr=c.ldA)
    ph.op("scalar", lambda s: s.mul(out=c.bq_cols[:, 0:16], in_=c.bq_cols[:, 0:16],
                                    mul=QSCALE), waits=[o_sind], ctr=c.sq)

    def mkconsts(v):
        v.tensor_single_scalar(out=c.ident[:], in_=c.iotav[:], scalar=0.0, op=ALU.is_equal)
        v.tensor_single_scalar(out=c.identf[:], in_=c.iotav[:], scalar=0.0, op=ALU.is_equal)
        v.tensor_single_scalar(out=c.uinc[:], in_=c.iotav[:], scalar=0.0, op=ALU.is_ge)
        v.tensor_single_scalar(out=c.uincf[:], in_=c.iotav[:], scalar=0.0, op=ALU.is_ge)
        v.tensor_single_scalar(out=c.ustrf[:], in_=c.iotav[:], scalar=0.0, op=ALU.is_gt)
        v.tensor_scalar(out=c.madd[:], in0=c.iotav[:], scalar1=0.0, scalar2=-1e9,
                        op0=ALU.is_lt, op1=ALU.mult)
        v.memset(c.ones_row[:], 1.0)
        v.memset(c.ones_rowf[:], 1.0)
        v.memset(c.ones_colb[:], 1.0)
        v.memset(c.ones8f[:], 1.0)
        v.memset(c.S_all[:], 0.0)
        v.memset(c.constcols[:, 0:1], LNEPS)
        v.memset(c.constcols[:, 1:2], -9.0 * LN2)
        v.memset(c.constcols[:, 2:3], 9.0 * LN2)
        v.tensor_scalar(out=c.rowsel[:], in0=c.rsel_t[:], scalar1=0.0, scalar2=None,
                        op0=ALU.is_ge)
        v.tensor_single_scalar(out=c.rsel_t[:], in_=c.rsel_t[:], scalar=128.0,
                               op=ALU.is_lt)
        v.tensor_tensor(out=c.rowsel[:], in0=c.rowsel[:], in1=c.rsel_t[:], op=ALU.mult)
        return v.memset(c.MASKS[:], 0.0)

    o_consts = ph.op("vector", mkconsts, waits=[o_iota, o_i4], ctr=c.vq)
    o_gsmm = ph.op("tensor", lambda t: t.matmul(
        c.pb[0][:, 0:2], c.ones_rowf[0:1, 0:128], c.sc_sm[0:1, 0:2],
        start=True, stop=True), waits=[o_sind, o_consts], ctr=c.tq)

    def gs_ev(v):
        v.tensor_scalar_mul(out=c.gs_cols[:, 0:1], in0=c.pb[0][:, 0:1], scalar1=1.0 / 6.0)
        v.reciprocal(out=c.gs_cols[:, 1:2], in_=c.pb[0][:, 0:1])
        v.tensor_scalar_mul(out=c.gs_cols[:, 2:3], in0=c.pb[0][:, 1:2], scalar1=1.0 / 6.0)
        return v.reciprocal(out=c.gs_cols[:, 3:4], in_=c.pb[0][:, 1:2])

    ph.op("vector", gs_ev, waits=[o_gsmm], ctr=c.vq)
    ph.emit()


def emit_layernorm(c, name, x_src, xn_dst, x1_store=None, tile_epilogue=None):
    with contextlib.ExitStack() as px:
        _ln_inner(c, name, x_src, xn_dst, tile_epilogue, px.enter_context)


def _ln_inner(c, name, x_src, xn_dst, tile_epilogue, ep):
    ph = Phase(c, name)
    ldp = [sctr(c, "lnx0"), sctr(c, "lnx1")]
    stp = [sctr(c, "lnst0"), sctr(c, "lnst1")]
    xa = ep(c.nc.sbuf_tensor(name + "_x", [128, 2 * D], F32))
    oa = ep(c.nc.sbuf_tensor(name + "_o", [128, 2 * D], FP16))
    xc = ep(c.nc.sbuf_tensor(name + "_xc", [128, 2 * D], F32))
    sqs = ep(c.nc.sbuf_tensor(name + "_sq", [128, D], F32))
    stt = ep(c.nc.sbuf_tensor(name + "_st", [128, 16], F32))
    v3tok, g1tok = {}, {}
    for ti in range(c.NT):
        b = ti % 2
        xs = xa[:, b * D:(b + 1) * D]
        os_ = oa[:, b * D:(b + 1) * D]
        xcs = xc[:, b * D:(b + 1) * D]
        ss = stt[:, b * 8:(b + 1) * 8]
        w = [v3tok[ti - 2]] if ti >= 2 else []
        o_ld = ph.dma("sync", lambda s, ti=ti, xs=xs: s.dma_start(
            out=xs, in_=x_src[ti * 128:(ti + 1) * 128, :]), waits=w, ctr=ldp[b])

        def v1(v, xs=xs, ss=ss):
            selfsync(c, "vector", v,
                     v.tensor_reduce(out=ss[:, 0:1], in_=xs, axis=AX.X, op=ALU.add))
            return v.tensor_scalar_mul(out=ss[:, 1:2], in0=ss[:, 0:1], scalar1=-1.0 / D)
        o_v1 = ph.op("vector", v1, waits=[o_ld], ctr=c.vq)

        def s1(s, xs=xs, xcs=xcs, ss=ss):
            s.activation(out=xcs, in_=xs, func=AF.Identity, bias=ss[:, 1:2])
            return s.activation(out=sqs[:], in_=xcs, func=AF.Square, accum_out=ss[:, 3:4])
        o_s1 = ph.op("scalar", s1, waits=[o_v1], ctr=c.sq)

        def v2(v, ss=ss):
            selfsync(c, "vector", v,
                     v.tensor_scalar(out=ss[:, 4:5], in0=ss[:, 3:4], scalar1=1.0 / D,
                                     scalar2=LNEPS, op0=ALU.mult, op1=ALU.add))
            return v.reciprocal(out=ss[:, 5:6], in_=ss[:, 4:5])
        o_v2 = ph.op("vector", v2, waits=[o_s1], ctr=c.vq)

        def s2(s, ss=ss):
            return s.activation(out=ss[:, 6:7], in_=ss[:, 5:6], func=AF.Sqrt)
        o_s2 = ph.op("scalar", s2, waits=[o_v2], ctr=c.sq)

        def v3(v, xcs=xcs, ss=ss, os_=os_):
            return v.tensor_scalar(out=os_, in0=xcs, scalar1=ss[:, 6:7],
                                   scalar2=None, op0=ALU.mult)
        w3 = [o_s2] + ([g1tok[ti - 2]] if ti >= 2 else [])
        v3tok[ti] = ph.op("vector", v3, waits=w3, ctr=c.vq)
        g1tok[ti] = ph.dma("gpsimd", lambda g, ti=ti, os_=os_: g.dma_start(
            out=xn_dst[ti * 128:(ti + 1) * 128, :], in_=os_), waits=[v3tok[ti]], ctr=stp[b])
        if tile_epilogue is not None:
            tile_epilogue(ph, ti, v2tok[ti], SimpleNamespace(xn=os_, x=xs, xcs=xcs))
    ph.emit()


def p2_proj(c):
    with contextlib.ExitStack() as px:
        _p2_inner(c, px.enter_context)


def _p2_inner(c, ep):
    ph = Phase(c, "p2")
    nc = c.nc
    WT = {w: ep(nc.sbuf_tensor("WT_" + w, [128, 8 * D], FP16))
          for w in ("Wq0", "Wk0", "Wv0", "Wb0", "Wa0")}
    WrS = ep(nc.sbuf_tensor("WrS", [128, 8 * E], FP16))
    wWT = sctr(c, "wWT")
    xntc = [sctr(c, "xnt0"), sctr(c, "xnt1")]
    evs = [sctr(c, f"evs{i}") for i in range(4)]
    ktmc = [sctr(c, "ktm0"), sctr(c, "ktm1")]
    vtmc = [sctr(c, "vtm0"), sctr(c, "vtm1")]
    vbsc = [sctr(c, "vbs0"), sctr(c, "vbs1")]

    o_w = None
    for w in ("Wq0", "Wk0", "Wv0", "Wb0", "Wa0"):
        for dc in range(8):
            o_w = ph.dma("sync", lambda s, w=w, dc=dc: s.dma_start_transpose(
                WT[w][:, dc * D:(dc + 1) * D], c.W16[w][0:D, dc * 128:(dc + 1) * 128]),
                ctr=wWT)
    wWr = sctr(c, "wWr")
    o_wr = None
    for dc in range(8):
        o_wr = ph.dma("gpsimd", lambda g, dc=dc: g.dma_start(
            out=WrS[:, dc * E:(dc + 1) * E], in_=c.ins["Wr"][dc * 128:(dc + 1) * 128, :]),
            ctr=wWr)
    wload = [o_w, o_wr]

    xnt = ep(nc.sbuf_tensor("p2_xnt", [128, 2 * 8 * 512], FP16))
    evfm = ep(nc.sbuf_tensor("p2_evfm", [128, 4 * 512], FP16))
    evktm = ep(nc.sbuf_tensor("p2_evktm", [128, 2 * 512], FP16))
    vsb = ep(nc.sbuf_tensor("p2_v", [128, 2 * D], FP16))
    vbsb = ep(nc.sbuf_tensor("p2_vb", [128, 2 * D], FP16))
    betas = ep(nc.sbuf_tensor("p2_beta", [128, 2 * D], F32))
    nlss = ep(nc.sbuf_tensor("p2_nls", [128, 2 * D], F32))
    e1s = ep(nc.sbuf_tensor("p2_e1", [128, 512], F32))
    a1s = ep(nc.sbuf_tensor("p2_a1", [128, 2 * 512], F32))
    red16 = ep(nc.sbuf_tensor("p2_red", [128, 16], F32))
    logt = ep(nc.sbuf_tensor("p2_logt", [128, 48], F32))

    last_pb = {}
    last_slot = {}
    vt_hist = {}
    aev_hist = {}
    last_xnt_use = {}

    for tg in range(c.NTG):
        gb = tg % 2
        xg = xnt[:, gb * 4096:(gb + 1) * 4096]
        w = [last_xnt_use.get(gb)]
        o_l = None
        for dc in range(8):
            o_l = ph.dma("sync", lambda s, tg=tg, dc=dc, xg=xg: s.dma_start_transpose(
                xg[:, dc * 512:(dc + 1) * 512],
                c.XN[tg * 512:(tg + 1) * 512, dc * 128:(dc + 1) * 128]),
                waits=w, ctr=xntc[gb])
        lw = [o_l] + (wload if tg == 0 else [])

        for wi, (wname, dst, colofs) in enumerate((("Wq0", c.QT, 0), ("Wk0", c.KT, 8))):
            for oc in range(8):
                bk = oc % 4
                bank = c.pb[bk]

                def mm(t, wname=wname, oc=oc, xg=xg, bank=bank):
                    r = None
                    for dc in range(8):
                        r = t.matmul(bank[:],
                                     WT[wname][:, dc * D + oc * 128:dc * D + (oc + 1) * 128],
                                     xg[:, dc * 512:(dc + 1) * 512],
                                     start=(dc == 0), stop=(dc == 7))
                    return r
                o_mm = ph.op("tensor", mm, waits=lw + [last_pb.get(bk)], ctr=c.tq)
                last_xnt_use[gb] = o_mm
                sl = (wi * 8 + oc) % 4
                eb = evfm[:, sl * 512:(sl + 1) * 512]

                def evac(s, bank=bank, eb=eb, colofs=colofs, oc=oc):
                    return s.activation(out=eb, in_=bank[:], func=AF.Identity,
                                        scale=QSCALE,
                                        bias=c.bq_cols[:, colofs + oc:colofs + oc + 1])
                o_ev = ph.op("scalar", evac, waits=[o_mm, last_slot.get(sl)], ctr=c.sq)
                last_pb[bk] = o_ev
                o_st = ph.dma("gpsimd", lambda g, dst=dst, oc=oc, tg=tg, eb=eb: g.dma_start(
                    out=dst[oc * 128:(oc + 1) * 128, tg * 512:(tg + 1) * 512], in_=eb),
                    waits=[o_ev], ctr=evs[sl])
                last_slot[sl] = o_st

        for tt in range(4):
            ti = tg * 4 + tt
            p = ti % 2
            o_vt_m2 = vt_hist.get(ti - 2)
            ev_ops = []
            for wi, wname in enumerate(("Wk0", "Wv0", "Wb0", "Wa0")):
                for oh in range(2):
                    bk = (wi * 2 + oh) % 4
                    bank = c.pb[bk]

                    def mmtm(t, wname=wname, oh=oh, tt=tt, xg=xg, bank=bank, wi=wi):
                        for dc in range(8):
                            t.matmul(bank[:],
                                     xg[:, dc * 512 + tt * 128:dc * 512 + (tt + 1) * 128],
                                     WT[wname][:, dc * D + oh * 512:dc * D + (oh + 1) * 512],
                                     start=(dc == 0), stop=False)
                        return t.matmul(
                            bank[:], c.ones_row[0:1, 0:128],
                            c.bias_rows[0:1, wi * 1024 + oh * 512:wi * 1024 + (oh + 1) * 512],
                            start=False, stop=True)
                    o_mm = ph.op("tensor", mmtm, waits=lw + [last_pb.get(bk)], ctr=c.tq)
                    last_xnt_use[gb] = o_mm
                    if wname == "Wk0":
                        eb = evktm[:, oh * 512:(oh + 1) * 512]

                        def evk(s, bank=bank, eb=eb):
                            return s.activation(out=eb, in_=bank[:], func=AF.Copy,
                                                scale=QSCALE)
                        o_ev = ph.op("scalar", evk,
                                     waits=[o_mm, last_slot.get(("ktm", oh))], ctr=c.sq)
                        o_st = ph.dma("gpsimd", lambda g, ti=ti, oh=oh, eb=eb: g.dma_start(
                            out=c.KTM[ti * 128:(ti + 1) * 128, oh * 512:(oh + 1) * 512],
                            in_=eb), waits=[o_ev], ctr=ktmc[oh])
                        last_slot[("ktm", oh)] = o_st
                        last_pb[bk] = o_ev
                    elif wname == "Wv0":
                        vb_half = vsb[:, p * D + oh * 512:p * D + (oh + 1) * 512]

                        def evv(s, bank=bank, vb_half=vb_half):
                            return s.activation(out=vb_half, in_=bank[:], func=AF.Copy)
                        o_ev = ph.op("scalar", evv,
                                     waits=[o_mm, o_vt_m2, last_slot.get(("vtm", p))],
                                     ctr=c.sq)
                        o_st = ph.dma("gpsimd",
                                      lambda g, ti=ti, oh=oh, vb_half=vb_half: g.dma_start(
                                          out=c.VTM[ti * 128:(ti + 1) * 128,
                                                    oh * 512:(oh + 1) * 512], in_=vb_half),
                                      waits=[o_ev], ctr=vtmc[p])
                        last_slot[("vtm", p)] = o_st
                        last_pb[bk] = o_ev
                        ev_ops.append(o_ev)
                    elif wname == "Wb0":
                        bslice = betas[:, p * D + oh * 512:p * D + (oh + 1) * 512]

                        def evb(s, bank=bank, bslice=bslice):
                            s.activation(out=e1s[:], in_=bank[:], func=AF.Exp)
                            return s.activation(out=bslice, in_=e1s[:], func=AF.Ln, bias=1.0)
                        o_ev = ph.op("scalar", evb, waits=[o_mm, o_vt_m2], ctr=c.sq)
                        last_pb[bk] = o_ev
                        ev_ops.append(o_ev)
                    else:
                        aslice = nlss[:, p * D + oh * 512:p * D + (oh + 1) * 512]
                        asl = a1s[:, oh * 512:(oh + 1) * 512]

                        def eva_v(v, bank=bank, asl=asl):
                            return v.tensor_scalar(out=asl, in0=bank[:], scalar1=-10.0,
                                                   scalar2=10.0, op0=ALU.max, op1=ALU.min)
                        o_cl = ph.op("vector", eva_v,
                                     waits=[o_mm, aev_hist.get(oh)], ctr=c.vq)
                        last_pb[bk] = o_cl

                        def eva_s(s, aslice=aslice, asl=asl):
                            s.activation(out=e1s[:], in_=asl, func=AF.Exp, scale=-1.0)
                            return s.activation(out=aslice, in_=e1s[:], func=AF.Ln, bias=1.0)
                        o_ev = ph.op("scalar", eva_s, waits=[o_cl, o_vt_m2], ctr=c.sq)
                        aev_hist[oh] = o_ev
                        ev_ops.append(o_ev)

            def mmr(t, tt=tt, xg=xg):
                for dc in range(8):
                    t.matmul(c.pb[4][:, 0:8],
                             xg[:, dc * 512 + tt * 128:dc * 512 + (tt + 1) * 128],
                             WrS[:, dc * E:(dc + 1) * E], start=(dc == 0), stop=False)
                return t.matmul(c.pb[4][:, 0:8], c.ones_row[0:1, 0:128],
                                c.br_row[0:1, 0:8], start=False, stop=True)
            o_mr = ph.op("tensor", mmr, waits=lw + [last_pb.get(4)], ctr=c.tq)
            last_xnt_use[gb] = o_mr

            def vtile(v, ti=ti, p=p):
                bet = betas[:, p * D:(p + 1) * D]
                nls = nlss[:, p * D:(p + 1) * D]
                vv = vsb[:, p * D:(p + 1) * D]
                vb = vbsb[:, p * D:(p + 1) * D]
                selfsync(c, "vector", v, v.tensor_reduce(
                    out=red16[:], in_=bet.rearrange("q (h d) -> q h d", d=HD),
                    axis=AX.X, op=ALU.add))
                v.tensor_copy(out=c.BMS[:, ti * NH:(ti + 1) * NH], in_=red16[:])
                selfsync(c, "vector", v, v.tensor_reduce(
                    out=red16[:], in_=nls.rearrange("q (h d) -> q h d", d=HD),
                    axis=AX.X, op=ALU.add))
                v.tensor_scalar_mul(out=c.MLA[:, ti * NH:(ti + 1) * NH], in0=red16[:],
                                    scalar1=-1.0 / HD)
                v.tensor_tensor(out=vb, in0=vv, in1=bet, op=ALU.mult)
                selfsync(c, "vector", v,
                         v.tensor_reduce(out=logt[:, 8:9], in_=c.pb[4][:, 0:8],
                                         axis=AX.X, op=ALU.max))
                eq = logt[:, 16:24]
                selfsync(c, "vector", v,
                         v.tensor_scalar(out=eq, in0=c.pb[4][:, 0:8],
                                         scalar1=logt[:, 8:9], scalar2=None,
                                         op0=ALU.is_ge))
                cs = logt[:, 24:32]
                t1 = logt[:, 32:40]
                sv = lambda i: selfsync(c, "vector", v, i)
                sv(v.tensor_copy(out=cs, in_=eq))
                sv(v.tensor_copy(out=t1[:, 1:8], in_=cs[:, 0:7]))
                sv(v.tensor_tensor(out=cs[:, 1:8], in0=cs[:, 1:8], in1=t1[:, 1:8], op=ALU.add))
                sv(v.tensor_copy(out=t1[:, 2:8], in_=cs[:, 0:6]))
                sv(v.tensor_tensor(out=cs[:, 2:8], in0=cs[:, 2:8], in1=t1[:, 2:8], op=ALU.add))
                sv(v.tensor_copy(out=t1[:, 4:8], in_=cs[:, 0:4]))
                sv(v.tensor_tensor(out=cs[:, 4:8], in0=cs[:, 4:8], in1=t1[:, 4:8], op=ALU.add))
                first = logt[:, 40:48]
                sv(v.tensor_single_scalar(out=first, in_=cs, scalar=1.0, op=ALU.is_equal))
                mf = c.MASKSF[:, ti * E:(ti + 1) * E]
                sv(v.tensor_tensor(out=mf, in0=eq, in1=first, op=ALU.mult))
                return v.tensor_copy(out=c.MASKS[:, ti * E:(ti + 1) * E], in_=mf)
            o_vt = ph.op("vector", vtile,
                         waits=ev_ops + [o_mr, last_slot.get(("vbs", p))], ctr=c.vq)
            vt_hist[ti] = o_vt
            last_pb[4] = o_vt
            o_vbst = ph.dma("gpsimd", lambda g, ti=ti, p=p: g.dma_start(
                out=c.VBS[ti * 128:(ti + 1) * 128, :], in_=vbsb[:, p * D:(p + 1) * D]),
                waits=[o_vt], ctr=vbsc[p])
            last_slot[("vbs", p)] = o_vbst

            def mcnt(t, ti=ti):
                return t.matmul(c.pb[5][0:8, ti:ti + 1], c.MASKS[:, ti * E:(ti + 1) * E],
                                c.ones_colb[:, 0:1], start=True, stop=True)
            o_cnt_last = ph.op("tensor", mcnt, waits=[o_vt], ctr=c.tq)
    ph.op("vector", lambda v: v.tensor_copy(out=c.cnt_sb[0:8, 0:c.NT],
                                            in_=c.pb[5][0:8, 0:c.NT]),
          waits=[o_cnt_last], ctr=c.vq)
    ph.emit()


def p3_deltanet(c):
    with contextlib.ExitStack() as px:
        _p3_inner(c, px.enter_context)


def _p3_inner(c, ep):
    ph = Phase(c, "p3")
    nc = c.nc
    NCH = c.NCH
    qts = ep(nc.sbuf_tensor("p3_q", [128, 2 * D], FP16))
    kts = ep(nc.sbuf_tensor("p3_k", [128, 2 * D], FP16))
    ktm = ep(nc.sbuf_tensor("p3_ktm", [128, 2 * D], FP16))
    vtm = ep(nc.sbuf_tensor("p3_v", [128, 2 * D], FP16))
    vbt = ep(nc.sbuf_tensor("p3_vb", [128, 2 * D], FP16))
    ytile = ep(nc.sbuf_tensor("p3_y", [128, 2 * D], FP16))
    m_tm = ep(nc.sbuf_tensor("p3_mtm", [128, 2 * NH], F32))
    mT = ep(nc.sbuf_tensor("p3_mT", [16, 2 * 128], F32))
    curveT = ep(nc.sbuf_tensor("p3_cT", [16, 2 * 128], F32))
    curve_tm = ep(nc.sbuf_tensor("p3_ctm", [128, 2 * NH], F32))
    dsb_a = ep(nc.sbuf_tensor("p3_dsb", [128, 2 * 128], F32))
    dec_a = ep(nc.sbuf_tensor("p3_dec", [128, 2 * 128], F32))
    adT_a = ep(nc.sbuf_tensor("p3_adT", [128, 2 * 128], FP16))
    ymem_a = ep(nc.sbuf_tensor("p3_ymem", [128, 2 * 64], F32))
    vbm_a = ep(nc.sbuf_tensor("p3_vbm", [128, 2 * 64], FP16))
    asum_c = ep(nc.sbuf_tensor("p3_asum", [128, 1], F32))
    S16 = ep(nc.sbuf_tensor("p3_S16", [128, c.B * NH * 32], FP16))

    ldq = [sctr(c, "d3l0"), sctr(c, "d3l1")]
    styc = [sctr(c, "d3y0"), sctr(c, "d3y1")]

    o_zero = ph.op("vector", lambda v: v.memset(S16[:], 0.0), ctr=c.vq)

    P = c.pb
    last = {}
    ev5_tok = {}
    st_tok = {}
    ld_tok = {}

    for b in range(c.B):
        for n in range(NCH):
            ci = b * NCH + n
            par = ci % 2
            t0 = ci * 128
            qg = qts[:, par * D:(par + 1) * D]
            kg = kts[:, par * D:(par + 1) * D]
            kmg = ktm[:, par * D:(par + 1) * D]
            vg = vtm[:, par * D:(par + 1) * D]
            vbg = vbt[:, par * D:(par + 1) * D]
            yg = ytile[:, par * D:(par + 1) * D]
            mtg = m_tm[:, par * NH:(par + 1) * NH]
            mTg = mT[:, par * 128:(par + 1) * 128]
            cTg = curveT[:, par * 128:(par + 1) * 128]
            ctg = curve_tm[:, par * NH:(par + 1) * NH]

            wl = [ld_tok.get(ci - 2)]
            o_l = None
            for name, dst, src in (("q", qg, c.QT), ("k", kg, c.KT)):
                o_l = ph.dma("sync", lambda s, dst=dst, src=src, t0=t0: s.dma_start(
                    out=dst[:].rearrange("p (a t) -> p a t", t=128),
                    in_=src[0:D, t0:t0 + 128].rearrange("(a p) t -> p a t", p=128)),
                    waits=wl, ctr=ldq[par])
            for dst, src in ((kmg, c.KTM), (vg, c.VTM), (vbg, c.VBS)):
                o_l = ph.dma("sync", lambda s, dst=dst, src=src, t0=t0: s.dma_start(
                    out=dst, in_=src[t0:t0 + 128, 0:D]), waits=wl, ctr=ldq[par])

            def oc1(t, ci=ci):
                return t.matmul(P[6][:, 0:16], c.uincf[:],
                                c.MLA[:, ci * NH:(ci + 1) * NH], start=True, stop=True)
            o_c1 = ph.op("tensor", oc1, waits=[o_zero, last.get("pb6")], ctr=c.tq)

            def oc2(v, mtg=mtg):
                return v.tensor_copy(out=mtg, in_=P[6][:, 0:16])
            o_c2 = ph.op("vector", oc2, waits=[o_c1, last.get(("mtm", par))], ctr=c.vq)
            last["pb6"] = o_c2

            def oc3(t, mtg=mtg):
                return t.transpose(P[7][0:16, 0:128], mtg, c.identf[:])
            o_c3 = ph.op("tensor", oc3, waits=[o_c2, last.get("pb7")], ctr=c.tq)

            def oc4(v, mTg=mTg):
                return v.tensor_copy(out=mTg, in_=P[7][0:16, 0:128])
            o_c4 = ph.op("vector", oc4, waits=[o_c3, last.get(("mT", par))], ctr=c.vq)
            last["pb7"] = o_c4

            def oc5(s, mTg=mTg, cTg=cTg):
                return s.activation(out=cTg, in_=mTg, func=AF.Exp)
            o_c5 = ph.op("scalar", oc5, waits=[o_c4, last.get(("cT", par))], ctr=c.sq)

            def oc6(s, mtg=mtg, ctg=ctg):
                return s.activation(out=ctg, in_=mtg, func=AF.Exp)
            o_c6 = ph.op("scalar", oc6, waits=[o_c2, last.get(("ctm", par))], ctr=c.sq)

            ev2_list = []
            for h in range(NH):
                p0 = (h % 2) * 64
                hp = h % 2
                dsb = dsb_a[:, hp * 128:(hp + 1) * 128]
                dec = dec_a[:, hp * 128:(hp + 1) * 128]
                adT = adT_a[:, hp * 128:(hp + 1) * 128]
                ymem = ymem_a[:, hp * 64:(hp + 1) * 64]
                vbm = vbm_a[:, hp * 64:(hp + 1) * 64]
                dc = h // 2
                qh = qg[p0:p0 + 64, dc * 128:(dc + 1) * 128]
                kh = kg[p0:p0 + 64, dc * 128:(dc + 1) * 128]
                cb = ((b * NH + h) // 2) * 64
                sS = c.S_all[p0:p0 + 64, cb:cb + 64]
                sS16 = S16[p0:p0 + 64, cb:cb + 64]

                o_m1 = ph.op("tensor", lambda t, kh=kh, qh=qh: t.matmul(
                    P[0][:, 0:128], kh, qh, start=True, stop=True),
                    waits=[o_l, last.get("pb0")], ctr=c.tq)
                o_m2 = ph.op("tensor", lambda t, mTg=mTg, h=h: t.matmul(
                    P[1][:, 0:128], c.rowsel[0:16, h * 128:(h + 1) * 128],
                    mTg[0:16, 0:128], start=True, stop=True),
                    waits=[o_c4, last.get("pb1")], ctr=c.tq)
                o_d1 = ph.op("vector", lambda v, mtg=mtg, h=h, dsb=dsb: v.scalar_tensor_tensor(
                    out=dsb[:], in0=P[1][:, 0:128], scalar=mtg[:, h:h + 1],
                    in1=c.madd[:], op0=ALU.subtract, op1=ALU.add),
                    waits=[o_m2, Hard(o_c2), last.get(("dsb", hp))], ctr=c.vq)
                last["pb1"] = o_d1
                o_d2 = ph.op("scalar", lambda s, dec=dec, dsb=dsb: s.activation(
                    out=dec[:], in_=dsb[:], func=AF.Exp),
                    waits=[o_d1, last.get(("dec", hp))], ctr=c.sq)
                last[("dsb", hp)] = o_d2
                o_d3 = ph.op("vector", lambda v, adT=adT, dec=dec: v.tensor_tensor(
                    out=adT[:], in0=P[0][:, 0:128], in1=dec[:], op=ALU.mult),
                    waits=[o_d2, o_m1, last.get(("adT", hp))], ctr=c.vq)
                last["pb0"] = o_d3
                last[("dec", hp)] = o_d3
                o_m3 = ph.op("tensor", lambda t, vbg=vbg, h=h, adT=adT: t.matmul(
                    P[2][:, 0:64], adT[:], vbg[:, h * 64:(h + 1) * 64],
                    start=True, stop=True), waits=[o_d3, last.get("pb2")], ctr=c.tq)
                last[("adT", hp)] = o_m3
                o_m4 = ph.op("tensor", lambda t, qh=qh, sS16=sS16: t.matmul(
                    P[5][:, 0:64], qh, sS16, start=True, stop=True),
                    waits=[o_l, ev5_tok.get((b, h), o_zero), last.get("pb5")], ctr=c.tq)
                o_d4 = ph.op("vector", lambda v, vg=vg, h=h, ci=ci, vbm=vbm: v.tensor_scalar(
                    out=vbm[:], in0=vg[:, h * 64:(h + 1) * 64],
                    scalar1=c.BMS[:, ci * NH + h:ci * NH + h + 1], scalar2=1.0 / HD,
                    op0=ALU.mult, op1=ALU.mult),
                    waits=[o_l, last.get(("vbm", hp))], ctr=c.vq)
                o_m5 = ph.op("tensor", lambda t, kmg=kmg, h=h, p0=p0, vbm=vbm: t.matmul(
                    P[3][p0:p0 + 64, 0:64], kmg[:, h * 64:(h + 1) * 64], vbm[:],
                    start=True, stop=True, tile_position=(0, p0)),
                    waits=[o_d4, last.get("pb3")], ctr=c.tq)
                last[("vbm", hp)] = o_m5
                o_m6 = ph.op("tensor", lambda t, cTg=cTg, h=h, p0=p0: t.matmul(
                    P[4][p0:p0 + 64, 0:1], c.rowsel[0:16, h * 128:h * 128 + 64],
                    cTg[0:16, 127:128], start=True, stop=True, tile_position=(0, p0)),
                    waits=[o_c5, last.get("pb4")], ctr=c.tq)
                o_e1 = ph.op("scalar", lambda s, ctg=ctg, h=h, ymem=ymem: s.activation(
                    out=ymem[:], in_=P[5][:, 0:64], func=AF.Identity,
                    scale=ctg[:, h:h + 1], bias=0.0),
                    waits=[o_m4, Hard(o_c6), last.get(("ymem", hp))], ctr=c.sq)
                last["pb5"] = o_e1
                o_e2 = ph.op("vector", lambda v, yg=yg, h=h, ymem=ymem: v.tensor_tensor(
                    out=yg[:, h * 64:(h + 1) * 64], in0=P[2][:, 0:64], in1=ymem[:],
                    op=ALU.add),
                    waits=[o_m3, o_e1, st_tok.get(ci - 2), last.get("yg")], ctr=c.vq)
                last["pb2"] = o_e2
                last[("ymem", hp)] = o_e2
                ev2_list.append(o_e2)
                o_e3 = ph.op("scalar", lambda s, p0=p0: s.copy(
                    out=asum_c[p0:p0 + 64, 0:1], in_=P[4][p0:p0 + 64, 0:1]),
                    waits=[o_m6, last.get("asumc")], ctr=c.sq)
                o_e4 = ph.op("vector", lambda v, sS=sS, p0=p0: v.scalar_tensor_tensor(
                    out=sS, in0=sS, scalar=asum_c[p0:p0 + 64, 0:1],
                    in1=P[3][p0:p0 + 64, 0:64],
                    op0=ALU.mult, op1=ALU.add), waits=[o_e3, o_m5], ctr=c.vq)
                last["pb3"] = o_e4
                last["pb4"] = o_e4
                last["asumc"] = o_e4
                o_e5 = ph.op("vector", lambda v, sS=sS, sS16=sS16: v.tensor_copy(
                    out=sS16, in_=sS), waits=[], ctr=c.vq)
                ev5_tok[(b, h)] = o_e5
            last["yg"] = ev2_list[-1]
            last[("mtm", par)] = ev2_list[-1]
            last[("mT", par)] = ev2_list[-1]
            last[("cT", par)] = ev2_list[-1]
            last[("ctm", par)] = ev2_list[-1]
            ld_tok[ci] = ev2_list[-1]
            st_tok[ci] = ph.dma("gpsimd", lambda g, t0=t0, yg=yg: g.dma_start(
                out=c.YATT[t0:t0 + 128, 0:D], in_=yg), waits=[ev2_list[-1]], ctr=styc[par])
    ph.emit()


def emit_quant_batch(c, ph, items, nb, gs6_col, rgs_col, qb, extra_waits=()):
    k = len(items)
    F = nb * 16
    K = k * nb
    sm = qb.sm
    amax = sm[:, 0 * K:1 * K]
    sc = sm[:, 1 * K:2 * K]
    cnt = sm[:, 2 * K:3 * K]
    q2 = sm[:, 3 * K:4 * K]
    rq = sm[:, 4 * K:5 * K]
    u2 = sm[:, 5 * K:6 * K]
    mm_ = sm[:, 6 * K:7 * K]
    dq = sm[:, 7 * K:8 * K]
    rdq = sm[:, 8 * K:9 * K]
    ind3_14 = qb.ind[:, 0:14 * K].rearrange("p (j n) -> p n j", n=K)
    ind3_16 = qb.ind[:, 0:16 * K].rearrange("p (j n) -> p n j", n=K)
    lastq = getattr(c, "_qlast", {})
    c._qlast = lastq

    o_am = []
    for i, (xin, _, w) in enumerate(items):
        def va(v, xin=xin, i=i):
            return v.tensor_reduce(out=amax[:, i * nb:(i + 1) * nb],
                                   in_=xin.rearrange("p (n j) -> p n j", j=16),
                                   axis=AX.X, op=ALU.max, apply_absolute_value=True)
        o_am.append(ph.op("vector", va,
                          waits=list(w) + list(extra_waits) +
                                ([lastq.get("sm")] if i == 0 else []), ctr=c.vq))

    def vsc(v):
        sv = lambda j: selfsync(c, "vector", v, j)
        sv(v.tensor_scalar(out=sc, in0=amax, scalar1=gs6_col, scalar2=448.0,
                           op0=ALU.mult, op1=ALU.min))
        for i, j in enumerate(range(-5, 9)):
            v.tensor_single_scalar(out=qb.ind[:, i * K:(i + 1) * K], in_=sc,
                                   scalar=float(2.0 ** j), op=ALU.is_ge)
        sv(v.memset(qb.ind[:, 14 * K:16 * K], 0.0))
        sv(v.tensor_reduce(out=cnt, in_=ind3_16, axis=AX.X, op=ALU.add))
        return v.tensor_copy(out=mm_, in_=cnt)
    o_sc = ph.op("vector", vsc, waits=o_am, ctr=c.vq)

    def sexp(s):
        s.activation(out=q2, in_=cnt, func=AF.Exp, scale=LN2, bias=c.constcols[:, 1:2])
        return s.activation(out=rq, in_=cnt, func=AF.Exp, scale=-LN2,
                            bias=c.constcols[:, 2:3])
    o_se = ph.op("scalar", sexp, waits=[o_sc, lastq.get("sm2")], ctr=c.sq)

    def vdq(v):
        sv = lambda j: selfsync(c, "vector", v, j)
        sv(v.tensor_tensor(out=u2, in0=sc, in1=rq, op=ALU.mult))
        for kk2 in range(16):
            v.tensor_single_scalar(out=qb.ind[:, kk2 * K:(kk2 + 1) * K], in_=u2,
                                   scalar=kk2 + 0.5, op=ALU.is_ge)
        sv(v.tensor_single_scalar(out=qb.ind[:, 15 * K:16 * K], in_=u2,
                                  scalar=15.5, op=ALU.is_ge))
        sv(v.tensor_reduce(out=mm_, in_=ind3_16, axis=AX.X, op=ALU.add))
        sv(v.tensor_tensor(out=dq, in0=mm_, in1=q2, op=ALU.mult))
        sv(v.tensor_scalar(out=dq, in0=dq, scalar1=rgs_col, scalar2=None,
                           op0=ALU.mult))
        sv(v.scalar_tensor_tensor(out=dq, in0=dq, scalar=0.0, in1=dq,
                                  op0=ALU.is_equal, op1=ALU.add))
        sv(v.reciprocal(out=rdq, in_=dq))
        return v.tensor_copy(out=qb.dq16[:, 0:K], in_=dq)
    o_vd = ph.op("vector", vdq, waits=[o_se], ctr=c.vq)
    lastq["sm"] = o_vd
    lastq["sm2"] = o_vd

    QW = ((0.25, 0.5), (0.75, 0.5), (1.25, 0.5), (1.75, 0.5),
          (2.5, 1.0), (3.5, 1.0), (5.0, 2.0))
    fb0 = qb.fb[:, 0 * F:1 * F]
    fb1 = qb.fb[:, 1 * F:2 * F]
    hsl = [qb.hb[:, i * F:(i + 1) * F] for i in range(8)]
    outs = []
    for i, (xin, out16, w) in enumerate(items):
        rdq_b = rdq[:, i * nb:(i + 1) * nb].unsqueeze(2).broadcast_to([128, nb, 16])
        dq16_b = qb.dq16[:, i * nb:(i + 1) * nb].unsqueeze(2).broadcast_to([128, nb, 16])
        o_abs = ph.op("scalar", lambda s, xin=xin: s.activation(
            out=fb0, in_=xin, func=AF.Abs),
            waits=list(w) + [o_vd, lastq.get("fb")], ctr=c.sq)

        def v1(v, rdq_b=rdq_b):
            sv = lambda j: selfsync(c, "vector", v, j)
            sv(v.tensor_tensor(out=fb1, in0=fb0, in1=rdq_b, op=ALU.mult))
            for j, (t, wt) in enumerate(QW):
                v.tensor_scalar(out=hsl[j], in0=fb1, scalar1=float(t),
                                scalar2=float(wt), op0=ALU.is_ge, op1=ALU.mult)
            sv(v.tensor_tensor(out=hsl[7], in0=hsl[5], in1=hsl[6], op=ALU.add))
            v.tensor_tensor(out=hsl[0], in0=hsl[0], in1=hsl[1], op=ALU.add)
            v.tensor_tensor(out=hsl[2], in0=hsl[2], in1=hsl[3], op=ALU.add)
            sv(v.tensor_tensor(out=hsl[4], in0=hsl[4], in1=hsl[7], op=ALU.add))
            sv(v.tensor_tensor(out=hsl[0], in0=hsl[0], in1=hsl[2], op=ALU.add))
            return v.tensor_tensor(out=hsl[0], in0=hsl[0], in1=hsl[4], op=ALU.add)
        o_v1 = ph.op("vector", v1, waits=[o_abs], ctr=c.vq)
        o_sg = ph.op("scalar", lambda s, xin=xin: s.activation(
            out=hsl[1], in_=xin, func=AF.Sign), waits=[o_v1], ctr=c.sq)

        def v3(v, dq16_b=dq16_b, out16=out16):
            selfsync(c, "vector", v, v.tensor_tensor(out=hsl[1], in0=hsl[1],
                                                     in1=dq16_b, op=ALU.mult))
            return v.tensor_tensor(out=out16, in0=hsl[0], in1=hsl[1], op=ALU.mult)
        o_v3 = ph.op("vector", v3, waits=[o_sg], ctr=c.vq)
        lastq["fb"] = o_v3
        outs.append(o_v3)
    return outs


def emit_quant(c, ph, xin, out16, nb, gs6_col, rgs_col, qb, waits):
    return emit_quant_batch(c, ph, [(xin, out16, list(waits))], nb,
                            gs6_col, rgs_col, qb)[-1]


def p4_moe(c):
    with contextlib.ExitStack() as px:
        _p4_inner(c, px.enter_context)


def _p4_inner(c, ep):
    nc = c.nc
    NT, CAP = c.NT, c.CAP
    TGE = min(512, CAP)
    NG = CAP // TGE
    NTT = TGE // 128
    P = c.pb

    ph = Phase(c, "p4a")
    base_sb = ep(nc.sbuf_tensor("p4_base", [64, 8], F32))
    baseT = ep(nc.sbuf_tensor("p4_baseT", [8, 64], F32))
    cntT = ep(nc.sbuf_tensor("p4_cntT", [64, 8], F32))
    b8 = ep(nc.sbuf_tensor("p4_b8", [8, 1], F32))
    diag8 = ep(nc.sbuf_tensor("p4_diag", [8, 8], F32))
    mp8 = ep(nc.sbuf_tensor("p4_mp", [128, 8], F32))
    dcol = ep(nc.sbuf_tensor("p4_dcol", [128, 4], F32))
    xnsb = ep(nc.sbuf_tensor("p4_xn", [128, 2 * D], FP16))
    zeroD = ep(nc.sbuf_tensor("p4_zero", [128, D], FP16))
    xldc = [sctr(c, "p4x0"), sctr(c, "p4x1")]
    sctc = sctr(c, "p4sct")

    o_t1 = ph.op("tensor", lambda t: t.transpose(P[7][0:c.NT, 0:8],
                                                 c.cnt_sb[0:8, 0:c.NT],
                                                 c.identf[0:8, 0:8]),
                 ctr=c.tq)
    o_v1 = ph.op("vector", lambda v: v.tensor_copy(out=cntT[0:c.NT, :],
                                                   in_=P[7][0:c.NT, 0:8]),
                 waits=[o_t1], ctr=c.vq)
    o_t2 = ph.op("tensor", lambda t: t.matmul(P[6][0:c.NT, 0:8],
                                              c.ustrf[0:c.NT, 0:c.NT], cntT[0:c.NT, :],
                                              start=True, stop=True),
                 waits=[o_v1], ctr=c.tq)
    o_v2 = ph.op("vector", lambda v: v.tensor_copy(out=base_sb[0:c.NT, :],
                                                   in_=P[6][0:c.NT, 0:8]),
                 waits=[o_t2], ctr=c.vq)
    o_t3 = ph.op("tensor", lambda t: t.transpose(P[7][0:8, 0:c.NT],
                                                 base_sb[0:c.NT, :],
                                                 c.identf[0:c.NT, 0:c.NT]),
                 waits=[o_v2, o_v1], ctr=c.tq)
    o_v3 = ph.op("vector", lambda v: v.tensor_copy(out=baseT[:, 0:c.NT],
                                                   in_=P[7][0:8, 0:c.NT]),
                 waits=[o_t3], ctr=c.vq)
    o_z = ph.op("vector", lambda v: v.memset(zeroD[:], 0.0), ctr=c.vq)
    zf = None
    for blk in range(E * CAP // 128):
        zf = ph.dma("gpsimd", lambda g, blk=blk: g.dma_start(
            out=c.XG[blk * 128:(blk + 1) * 128, :], in_=zeroD[:]),
            waits=[o_z], ctr=sctc)

    prev = {}
    for ti in range(NT):
        par = ti % 2

        def vdiag(v, ti=ti):
            sv = lambda i: selfsync(c, "vector", v, i)
            sv(v.tensor_tensor(out=b8[:], in0=baseT[:, ti:ti + 1], in1=c.ecap_col[:],
                               op=ALU.add))
            return v.tensor_scalar(out=diag8[:], in0=c.identf[0:8, 0:8], scalar1=b8[:],
                                   scalar2=None, op0=ALU.mult)
        o_vd = ph.op("vector", vdiag, waits=[o_v3, prev.get("posmm")], ctr=c.vq)

        def posmm(t, ti=ti):
            t.matmul(P[0][:, 0:8], c.uincf[:], c.MASKSF[:, ti * E:(ti + 1) * E],
                     start=True, stop=False)
            return t.matmul(P[0][:, 0:8], c.ones8f[:], diag8[:], start=False, stop=True)
        o_pm = ph.op("tensor", posmm, waits=[o_vd, prev.get("didx")], ctr=c.tq)
        prev["posmm"] = o_pm

        def vidx(v, ti=ti):
            sv = lambda i: selfsync(c, "vector", v, i)
            mf = c.MASKSF[:, ti * E:(ti + 1) * E]
            sv(v.tensor_tensor(out=mp8[:], in0=mf, in1=P[0][:, 0:8], op=ALU.mult))
            sv(v.tensor_reduce(out=dcol[:, 0:1], in_=mp8[:], axis=AX.X, op=ALU.add))
            sv(v.tensor_tensor(out=mp8[:], in0=mf, in1=c.ecapmax[:], op=ALU.mult))
            sv(v.tensor_reduce(out=dcol[:, 1:2], in_=mp8[:], axis=AX.X, op=ALU.add))
            sv(v.tensor_tensor(out=dcol[:, 2:3], in0=dcol[:, 0:1], in1=dcol[:, 1:2],
                               op=ALU.min))
            return v.tensor_copy(out=c.DIDX[:, ti:ti + 1], in_=dcol[:, 2:3])
        o_vi = ph.op("vector", vidx, waits=[o_pm], ctr=c.vq)
        prev["didx"] = o_vi
        if ti == 0 and hasattr(c, "dbg_capture"):
            def vcap(v):
                v.tensor_copy(out=c.dbg_capture[0:8, 0:8], in_=diag8[:])
                v.tensor_copy(out=c.dbg_capture[0:128, 8:16], in_=P[0][:, 0:8])
                v.tensor_copy(out=c.dbg_capture[0:128, 16:20], in_=dcol[:])
                return v.tensor_copy(out=c.dbg_capture[0:8, 20:21], in_=b8[:])
            o_cap = ph.op("vector", vcap, waits=[o_pm], ctr=c.vq)
            prev["didx"] = o_cap

        o_xl = ph.dma("sync", lambda s, ti=ti, par=par: s.dma_start(
            out=xnsb[:, par * D:(par + 1) * D], in_=c.XN[ti * 128:(ti + 1) * 128, :]),
            waits=[prev.get(("sct", par))], ctr=xldc[par])
        o_sc = ph.dma("gpsimd", lambda g, ti=ti, par=par: g.indirect_dma_start(
            out=c.XG, out_offset=bass.IndirectOffsetOnAxis(c.DIDX[:, ti:ti + 1], 0),
            in_=xnsb[:, par * D:(par + 1) * D], in_offset=None),
            waits=[o_xl, o_vi] + ([zf] if ti == 0 else []), ctr=sctc)
        prev[("sct", par)] = o_sc
    ph.emit()

    barrier(c, "b4b")
    ph = Phase(c, "p4b")
    W1sb = ep(nc.sbuf_tensor("p4_w1", [128, 8 * MH], FP16))
    b1row = ep(nc.sbuf_tensor("p4_b1", [1, MH], FP16))
    b2row = ep(nc.sbuf_tensor("p4_b2", [1, D], FP16))
    xgT = ep(nc.sbuf_tensor("p4_xgT", [128, 2 * 8 * TGE], FP16))
    hT = ep(nc.sbuf_tensor("p4_hT", [128, 32 * TGE], FP16))
    w2sl = ep(nc.sbuf_tensor("p4_w2", [128, 2 * D], FP16))
    ysb = ep(nc.sbuf_tensor("p4_y", [128, 2 * D], FP16))
    w1c = sctr(c, "p4w1")
    bxc = sctr(c, "p4bx")
    xgl = [sctr(c, "p4g0"), sctr(c, "p4g1")]
    w2c = [sctr(c, "p4w20"), sctr(c, "p4w21")]
    ygst = [sctr(c, "p4yg0"), sctr(c, "p4yg1")]
    NYB = 2 * NTT
    last = {}
    for e in range(E):
        o_w1 = None
        for dc in range(8):
            o_w1 = ph.dma("gpsimd", lambda g, e=e, dc=dc: g.dma_start(
                out=W1sb[:, dc * MH:(dc + 1) * MH],
                in_=c.ins["W1"][e, dc * 128:(dc + 1) * 128, :]),
                waits=[last.get("w1free")], ctr=w1c)
        o_b1 = ph.dma("gpsimd", lambda g, e=e: g.dma_start(
            out=b1row[:], in_=c.ins["b1"][e].unsqueeze(0)),
            waits=[last.get("b1free")], ctr=bxc)
        o_b2 = ph.dma("gpsimd", lambda g, e=e: g.dma_start(
            out=b2row[:], in_=c.ins["b2"][e].unsqueeze(0)),
            waits=[last.get("b2free")], ctr=bxc)
        for g_ in range(NG):
            gp = g_ % 2
            xgv = xgT[:, gp * 8 * TGE:(gp + 1) * 8 * TGE]
            hTv = hT[:]
            row0 = e * CAP + g_ * TGE
            o_xg = None
            for dc in range(8):
                o_xg = ph.dma("sync", lambda s, dc=dc, row0=row0, xgv=xgv:
                              s.dma_start_transpose(
                                  xgv[:, dc * TGE:(dc + 1) * TGE],
                                  c.XG[row0:row0 + TGE, dc * 128:(dc + 1) * 128]),
                              waits=[last.get(("xg", gp))], ctr=xgl[gp])
            hlast = None
            hm_last = None
            for hc in range(32):
                bk = hc % 4

                def mmh(t, hc=hc, xgv=xgv, bk=bk):
                    for dc in range(8):
                        t.matmul(P[bk][:, 0:TGE],
                                 W1sb[:, dc * MH + hc * 128:dc * MH + (hc + 1) * 128],
                                 xgv[:, dc * TGE:(dc + 1) * TGE],
                                 start=(dc == 0), stop=False)
                    return t.matmul(P[bk][:, 0:TGE],
                                    b1row[0:1, hc * 128:(hc + 1) * 128],
                                    c.ones_row[0:1, 0:TGE], start=False, stop=True)
                o_mh = ph.op("tensor", mmh,
                             waits=[o_xg, o_w1, o_b1, last.get(("pb", bk))], ctr=c.tq)
                hm_last = o_mh

                def evh(s, hc=hc, hTv=hTv, bk=bk):
                    return s.activation(out=hTv[:, hc * TGE:(hc + 1) * TGE],
                                        in_=P[bk][:, 0:TGE], func=AF.Gelu)
                o_eh = ph.op("scalar", evh, waits=[o_mh, last.get("hTb")], ctr=c.sq)
                last[("pb", bk)] = o_eh
                hlast = o_eh
            last[("xg", gp)] = hm_last
            o_w2 = {}
            for pre in range(2):
                o_w2[pre] = ph.dma("sync", lambda s, e=e, hc=pre, wp=pre % 2: s.dma_start(
                    out=w2sl[:, (hc % 2) * D:((hc % 2) + 1) * D],
                    in_=c.W2S[e, hc * 128:(hc + 1) * 128, :]),
                    waits=[last.get(("w2", pre % 2))], ctr=w2c[pre % 2])
            ylast = None
            for hc in range(32):
                wp = hc % 2

                def mmy(t, hc=hc, wp=wp, hTv=hTv):
                    r = None
                    for tt in range(NTT):
                        for oh in range(2):
                            bk = tt * 2 + oh
                            r = t.matmul(
                                P[bk][:, 0:512],
                                hTv[:, hc * TGE + tt * 128:hc * TGE + (tt + 1) * 128],
                                w2sl[:, wp * D + oh * 512:wp * D + (oh + 1) * 512],
                                start=(hc == 0), stop=False)
                    return r
                wts = [o_w2[hc], hlast]
                if hc == 0:
                    wts += [last.get(("pb", b2_)) for b2_ in range(NYB)]
                o_my = ph.op("tensor", mmy, waits=wts, ctr=c.tq)
                last[("w2", wp)] = o_my
                ylast = o_my
                if hc + 2 < 32:
                    o_w2[hc + 2] = ph.dma("sync", lambda s, e=e, hc=hc, wp=wp:
                                          s.dma_start(
                                              out=w2sl[:, wp * D:(wp + 1) * D],
                                              in_=c.W2S[e, (hc + 2) * 128:(hc + 3) * 128, :]),
                                          waits=[o_my], ctr=w2c[wp])
            last["hTb"] = ylast
            o_mb = None
            for tt in range(NTT):
                def mmb(t, tt=tt):
                    r = None
                    for oh in range(2):
                        bk = tt * 2 + oh
                        r = t.matmul(P[bk][:, 0:512], c.ones_row[0:1, 0:128],
                                     b2row[0:1, oh * 512:(oh + 1) * 512],
                                     start=False, stop=True)
                    return r
                o_mb = ph.op("tensor", mmb, waits=[ylast, o_b2], ctr=c.tq)
                yp = tt % 2

                def evy(s, tt=tt, yp=yp):
                    r = None
                    for oh in range(2):
                        bk = tt * 2 + oh
                        r = s.activation(
                            out=ysb[:, yp * D + oh * 512:yp * D + (oh + 1) * 512],
                            in_=P[bk][:, 0:512], func=AF.Copy)
                    return r
                o_ey = ph.op("scalar", evy, waits=[o_mb, last.get(("ysb", yp))], ctr=c.sq)
                for oh in range(2):
                    last[("pb", tt * 2 + oh)] = o_ey
                o_sy = ph.dma("gpsimd", lambda g, row0=row0, tt=tt, yp=yp: g.dma_start(
                    out=c.YG[row0 + tt * 128:row0 + (tt + 1) * 128, :],
                    in_=ysb[:, yp * D:(yp + 1) * D]), waits=[o_ey], ctr=ygst[yp])
                last[("ysb", yp)] = o_sy
        last["w1free"] = ylast
        last["b1free"] = hm_last
        last["b2free"] = o_mb
    ph.emit()


def p4c_gate(c):
    with contextlib.ExitStack() as px:
        _p4c_inner(c, px.enter_context)


def _p4c_inner(c, ep):
    nc = c.nc
    ph = Phase(c, "p4c")
    ymo = ep(nc.sbuf_tensor("p4c_ym", [128, 2 * D], FP16))
    yat = ep(nc.sbuf_tensor("p4c_ya", [128, 2 * D], FP16))
    xin = ep(nc.sbuf_tensor("p4c_x", [128, 2 * D], F32))
    gsil = ep(nc.sbuf_tensor("p4c_g", [128, D], F32))
    x1sb = ep(nc.sbuf_tensor("p4c_x1", [128, 2 * D], F32))
    xcs = ep(nc.sbuf_tensor("p4c_xc", [128, 2 * D], F32))
    sqs = ep(nc.sbuf_tensor("p4c_sq", [128, D], F32))
    stt = ep(nc.sbuf_tensor("p4c_st", [128, 16], F32))
    xn1f = ep(nc.sbuf_tensor("p4c_xn1", [128, D], F32))
    o16 = ep(nc.sbuf_tensor("p4c_o16", [128, 2 * D], FP16))
    xq16 = ep(nc.sbuf_tensor("p4c_xq", [128, 2 * D], FP16))
    qb = SimpleNamespace(
        sm=ep(nc.sbuf_tensor("p4c_qsm", [128, 10 * 64], F32)),
        ind=ep(nc.sbuf_tensor("p4c_qind", [128, 16 * 64], F32)),
        dq16=ep(nc.sbuf_tensor("p4c_qdq", [128, 64], FP16)),
        fb=ep(nc.sbuf_tensor("p4c_qfb", [128, 2 * D], F32)),
        hb=ep(nc.sbuf_tensor("p4c_qhb", [128, 8 * D], FP16)))
    gt = sctr(c, "p4gt")
    yac = [sctr(c, "p4ya0"), sctr(c, "p4ya1")]
    xxc = [sctr(c, "p4xx0"), sctr(c, "p4xx1")]
    x1c = [sctr(c, "p4x10"), sctr(c, "p4x11")]
    xnc = [sctr(c, "p4xn0"), sctr(c, "p4xn1")]
    xqc = [sctr(c, "p4xq0"), sctr(c, "p4xq1")]
    prev = {}
    for ti in range(c.NT):
        p = ti % 2
        ymv = ymo[:, p * D:(p + 1) * D]
        yav = yat[:, p * D:(p + 1) * D]
        xv = xin[:, p * D:(p + 1) * D]
        x1v = x1sb[:, p * D:(p + 1) * D]
        xcv = xcs[:, p * D:(p + 1) * D]
        ss = stt[:, p * 8:(p + 1) * 8]
        ov = o16[:, p * D:(p + 1) * D]
        xqv = xq16[:, p * D:(p + 1) * D]
        o_g = ph.dma("gpsimd", lambda g, ti=ti, ymv=ymv: g.indirect_dma_start(
            out=ymv, out_offset=None, in_=c.YG,
            in_offset=bass.IndirectOffsetOnAxis(c.DIDX[:, ti:ti + 1], 0)),
            waits=[prev.get(("ym", p))], ctr=gt)
        o_ya = ph.dma("sync", lambda s, ti=ti, yav=yav: s.dma_start(
            out=yav, in_=c.YATT[ti * 128:(ti + 1) * 128, :]),
            waits=[prev.get(("ya", p))], ctr=yac[p])
        o_xx = ph.dma("sync", lambda s, ti=ti, xv=xv: s.dma_start(
            out=xv, in_=c.ins["x"][ti * 128:(ti + 1) * 128, :]),
            waits=[prev.get(("xx", p))], ctr=xxc[p])
        o_si = ph.op("scalar", lambda s, ymv=ymv: s.activation(
            out=gsil[:], in_=ymv, func=AF.Silu),
            waits=[o_g, prev.get("gsil")], ctr=c.sq)

        def vmix(v, yav=yav, xv=xv, x1v=x1v):
            selfsync(c, "vector", v,
                     v.tensor_tensor(out=gsil[:], in0=gsil[:], in1=yav, op=ALU.mult))
            return v.tensor_tensor(out=x1v, in0=gsil[:], in1=xv, op=ALU.add)
        o_vm = ph.op("vector", vmix,
                     waits=[o_si, o_ya, o_xx, prev.get(("x1", p))], ctr=c.vq)
        prev[("ym", p)] = o_vm
        prev[("ya", p)] = o_vm
        prev["gsil"] = o_vm
        o_sx1 = ph.dma("sync", lambda s, ti=ti, x1v=x1v: s.dma_start(
            out=c.X1[ti * 128:(ti + 1) * 128, :], in_=x1v), waits=[o_vm], ctr=x1c[p])
        prev[("x1", p)] = o_sx1
        def v1(v, x1v=x1v, ss=ss):
            selfsync(c, "vector", v,
                     v.tensor_reduce(out=ss[:, 0:1], in_=x1v, axis=AX.X, op=ALU.add))
            return v.tensor_scalar_mul(out=ss[:, 1:2], in0=ss[:, 0:1], scalar1=-1.0 / D)
        o_v1 = ph.op("vector", v1, waits=[o_vm], ctr=c.vq)

        def s1(s, x1v=x1v, xcv=xcv, ss=ss):
            s.activation(out=xcv, in_=x1v, func=AF.Identity, bias=ss[:, 1:2])
            return s.activation(out=sqs[:], in_=xcv, func=AF.Square, accum_out=ss[:, 3:4])
        o_s1 = ph.op("scalar", s1, waits=[o_v1], ctr=c.sq)

        def v2(v, ss=ss):
            selfsync(c, "vector", v,
                     v.tensor_scalar(out=ss[:, 4:5], in0=ss[:, 3:4], scalar1=1.0 / D,
                                     scalar2=LNEPS, op0=ALU.mult, op1=ALU.add))
            return v.reciprocal(out=ss[:, 5:6], in_=ss[:, 4:5])
        o_v2 = ph.op("vector", v2, waits=[o_s1], ctr=c.vq)
        o_s2 = ph.op("scalar", lambda s, ss=ss: s.activation(
            out=ss[:, 6:7], in_=ss[:, 5:6], func=AF.Sqrt), waits=[o_v2], ctr=c.sq)

        def v3(v, xcv=xcv, ss=ss, ov=ov):
            v.tensor_scalar(out=xn1f[:], in0=xcv, scalar1=ss[:, 6:7], scalar2=None,
                            op0=ALU.mult)
            return v.tensor_copy(out=ov, in_=xn1f[:])
        o_v3 = ph.op("vector", v3,
                     waits=[o_s2, prev.get(("o16", p)), prev.get("xn1f")], ctr=c.vq)
        o_sxn = ph.dma("gpsimd", lambda g, ti=ti, ov=ov: g.dma_start(
            out=c.XN1[ti * 128:(ti + 1) * 128, :], in_=ov), waits=[o_v3], ctr=xnc[p])
        prev[("o16", p)] = o_sxn
        o_q = emit_quant(c, ph, xn1f[:], xqv, 64, c.gs_cols[:, 0:1], c.gs_cols[:, 1:2],
                         qb, waits=[o_v3, prev.get(("xq", p))])
        prev["xn1f"] = o_q
        o_sxq = ph.dma("gpsimd", lambda g, ti=ti, xqv=xqv: g.dma_start(
            out=c.XQ[ti * 128:(ti + 1) * 128, :], in_=xqv), waits=[o_q], ctr=xqc[p])
        prev[("xq", p)] = o_sxq
    ph.emit()


def p5_prep(c, ep):
    nc = c.nc
    ph = Phase(c, "p5")
    c.Wk1sb = ep(nc.sbuf_tensor("p5_wk1", [128, 6 * D], FP16))
    c.Wv1sb = ep(nc.sbuf_tensor("p5_wv1", [128, 6 * D], FP16))
    wlc = sctr(c, "p5wl")
    o_ct = None
    o_wl = None
    for w, dst in (("Wk1", c.Wk1sb), ("Wv1", c.Wv1sb)):
        for cdc in range(6):
            o_wl = ph.dma("gpsimd", lambda g, w=w, dst=dst, cdc=cdc: g.dma_start(
                out=dst[:, cdc * D:(cdc + 1) * D],
                in_=c.ins[w][cdc * 128:(cdc + 1) * 128, :]), ctr=wlc)
    c.p5toks = [o_wl]

    with contextlib.ExitStack() as p5x:
        e5 = p5x.enter_context
        Wq1sb = e5(nc.sbuf_tensor("p5_wq1", [128, 8 * D], FP16))
        xnt = e5(nc.sbuf_tensor("p5_xnt", [128, 2 * 8 * 512], FP16))
        evb = e5(nc.sbuf_tensor("p5_ev", [128, 4 * 512], FP16))
        o_wq = None
        for dc in range(8):
            o_wq = ph.dma("gpsimd", lambda g, dc=dc: g.dma_start(
                out=Wq1sb[:, dc * D:(dc + 1) * D],
                in_=c.ins["Wq1"][dc * 128:(dc + 1) * 128, :]), ctr=wlc)
        xl = [sctr(c, "p5x0"), sctr(c, "p5x1")]
        q1st = [sctr(c, f"p5q{i}") for i in range(4)]
        last = {}
        for tg in range(c.NTG):
            gb = tg % 2
            xg = xnt[:, gb * 4096:(gb + 1) * 4096]
            o_l = None
            for dc in range(8):
                o_l = ph.dma("sync", lambda s, tg=tg, dc=dc, xg=xg: s.dma_start_transpose(
                    xg[:, dc * 512:(dc + 1) * 512],
                    c.XN1[tg * 512:(tg + 1) * 512, dc * 128:(dc + 1) * 128]),
                    waits=[last.get(("xg", gb))], ctr=xl[gb])
            for oc in range(8):
                bk = oc % 4

                def mm(t, oc=oc, xg=xg, bk=bk):
                    r = None
                    for dc in range(8):
                        r = t.matmul(c.pb[bk],
                                     Wq1sb[:, dc * D + oc * 128:dc * D + (oc + 1) * 128],
                                     xg[:, dc * 512:(dc + 1) * 512],
                                     start=(dc == 0), stop=(dc == 7))
                    return r
                o_mm = ph.op("tensor", mm,
                             waits=[o_l, o_wq, last.get(("pb", bk))] +
                                   ([] if tg else [o_ct]), ctr=c.tq)
                last[("xg", gb)] = o_mm
                sl = oc % 4
                eb = evb[:, sl * 512:(sl + 1) * 512]
                o_ev = ph.op("scalar", lambda s, bk=bk, eb=eb: s.activation(
                    out=eb, in_=c.pb[bk], func=AF.Copy),
                    waits=[o_mm, last.get(("ev", sl))], ctr=c.sq)
                last[("pb", bk)] = o_ev
                o_st = ph.dma("gpsimd", lambda g, oc=oc, tg=tg, eb=eb: g.dma_start(
                    out=c.QT1[oc * 128:(oc + 1) * 128, tg * 512:(tg + 1) * 512], in_=eb),
                    waits=[o_ev], ctr=q1st[sl])
                last[("ev", sl)] = o_st
        ph.emit()


def p6_attn(c):
    with contextlib.ExitStack() as px:
        _p6_inner(c, px.enter_context)


def _p6_inner(c, ep):
    nc = c.nc
    NKT = CTXT // 128
    NKG = CTXT // 512
    ph = Phase(c, "p6")
    khT = ep(nc.sbuf_tensor("p6_khT", [128, 2 * CTXT], FP16))
    vh = ep(nc.sbuf_tensor("p6_vh", [128, 3 * CTXT], FP16))
    qh = ep(nc.sbuf_tensor("p6_qh", [128, 2 * 128], FP16))
    Psb = ep(nc.sbuf_tensor("p6_P", [128, 2 * CTXT], FP16))
    rmx = ep(nc.sbuf_tensor("p6_rm", [128, 2 * 16], F32))
    PT = ep(nc.sbuf_tensor("p6_PT", [128, NKT * 1024], FP16))
    otb = ep(nc.sbuf_tensor("p6_ot", [128, 2 * 512], FP16))
    ctb = ep(nc.sbuf_tensor("p6_ctb", [128, 6 * 512], FP16))
    qlc = [sctr(c, "p6q0"), sctr(c, "p6q1")]
    pstc = [sctr(c, "p6p0"), sctr(c, "p6p1")]
    ptlc = {"sync": sctr(c, "p6ts"), "scalar": sctr(c, "p6tc")}
    otc = [sctr(c, "p6o0"), sctr(c, "p6o1")]
    ctl = [sctr(c, "p6c0"), sctr(c, "p6c1")]
    last = {}
    PS = c.PS
    NSQ = c.T // 1024
    slot = [0]

    def next_quad():
        q = slot[0] % 2
        slot[0] += 1
        return q

    def soft_half(h, ti, half, o_ql):
        hp = h % 2
        par = ti % 2
        q = next_quad()
        qv = qh[:, par * 128:(par + 1) * 128]
        pv = Psb[:, par * CTXT:(par + 1) * CTXT]
        rm = rmx[:, par * 16:(par + 1) * 16]
        k0 = half * 2048
        b0 = q * 2048

        def mmq(t, qv=qv, hp=hp, k0=k0, b0=b0):
            r = None
            for kg in range(4):
                r = t.matmul(PS[:, b0 + kg * 512:b0 + (kg + 1) * 512], qv,
                             khT[:, hp * CTXT + k0 + kg * 512:
                                 hp * CTXT + k0 + (kg + 1) * 512],
                             start=True, stop=True)
            return r
        o_mq = ph.op("tensor", mmq,
                     waits=[o_ql, last.get(("khdone", hp))] + last.get(("quad", q), []),
                     ctr=c.tq)

        def rmax(v, rm=rm, half=half, b0=b0):
            sv = lambda i: selfsync(c, "vector", v, i)
            sv(v.tensor_reduce(out=rm[:, half:half + 1], in_=PS[:, b0:b0 + 2048],
                               axis=AX.X, op=ALU.max))
            return v.tensor_scalar_mul(out=rm[:, 2 + half:3 + half],
                                       in0=rm[:, half:half + 1], scalar1=-ASCALE)
        o_rm = ph.op("vector", rmax,
                     waits=[o_mq, last.get(("rmx", par))], ctr=c.vq)

        def expev(s, pv=pv, rm=rm, half=half, k0=k0, b0=b0):
            return s.activation(out=pv[:, k0:k0 + 2048], in_=PS[:, b0:b0 + 2048],
                                func=AF.Exp, scale=ASCALE,
                                bias=rm[:, 2 + half:3 + half],
                                accum_out=rm[:, 4 + half:5 + half])
        o_ex = ph.op("scalar", expev,
                     waits=[o_rm, last.get(("Psb", par))], ctr=c.sq)
        last[("quad", q)] = [o_ex]
        return o_mq, o_ex

    def soft_tile(h, ti):
        hp = h % 2
        par = ti % 2
        o_ql = ph.dma("sync", lambda s, h=h, ti=ti, par=par: s.dma_start(
            out=qh[:, par * 128:(par + 1) * 128],
            in_=c.QT1[h * 128:(h + 1) * 128, ti * 128:(ti + 1) * 128]),
            waits=[last.get(("qh", par))], ctr=qlc[par])
        oA = soft_half(h, ti, 0, o_ql)
        oB = soft_half(h, ti, 1, o_ql)
        last[("qh", par)] = oB[0]
        pv = Psb[:, par * CTXT:(par + 1) * CTXT]
        rm = rmx[:, par * 16:(par + 1) * 16]

        def vfix(v, rm=rm):
            sv = lambda i: selfsync(c, "vector", v, i)
            sv(v.tensor_tensor(out=rm[:, 6:7], in0=rm[:, 0:1], in1=rm[:, 1:2],
                               op=ALU.max))
            sv(v.tensor_tensor(out=rm[:, 8:9], in0=rm[:, 0:1], in1=rm[:, 6:7],
                               op=ALU.subtract))
            sv(v.tensor_tensor(out=rm[:, 9:10], in0=rm[:, 1:2], in1=rm[:, 6:7],
                               op=ALU.subtract))
            return v.tensor_scalar_mul(out=rm[:, 8:10], in0=rm[:, 8:10],
                                       scalar1=ASCALE)
        o_vf = ph.op("vector", vfix, waits=[], ctr=c.vq)
        o_sf = ph.op("scalar", lambda s, rm=rm: s.activation(
            out=rm[:, 10:12], in_=rm[:, 8:10], func=AF.Exp),
            waits=[o_vf, oB[1]], ctr=c.sq)

        def vnorm(v, rm=rm, pv=pv):
            sv = lambda i: selfsync(c, "vector", v, i)
            sv(v.tensor_tensor(out=rm[:, 12:14], in0=rm[:, 4:6], in1=rm[:, 10:12],
                               op=ALU.mult))
            sv(v.tensor_tensor(out=rm[:, 12:13], in0=rm[:, 12:13],
                               in1=rm[:, 13:14], op=ALU.add))
            sv(v.reciprocal(out=rm[:, 14:15], in_=rm[:, 12:13]))
            sv(v.tensor_scalar(out=rm[:, 10:12], in0=rm[:, 10:12],
                               scalar1=rm[:, 14:15], scalar2=None, op0=ALU.mult))
            v.tensor_scalar(out=pv[:, 0:2048], in0=pv[:, 0:2048],
                            scalar1=rm[:, 10:11], scalar2=None, op0=ALU.mult)
            return v.tensor_scalar(out=pv[:, 2048:4096], in0=pv[:, 2048:4096],
                                   scalar1=rm[:, 11:12], scalar2=None, op0=ALU.mult)
        o_vn = ph.op("vector", vnorm, waits=[o_sf], ctr=c.vq)
        last[("rmx", par)] = o_vn
        o_ps = ph.dma("gpsimd", lambda g, ti=ti, hp=hp, pv=pv: g.dma_start(
            out=c.PSCR[hp * c.T + ti * 128:hp * c.T + (ti + 1) * 128, :], in_=pv),
            waits=[o_vn, last.get(("pscr", hp))], ctr=pstc[par])
        last[("Psb", par)] = o_ps
        return o_ps

    def pt_loads(h1, sq, extra):
        hp = h1 % 2
        for hf in range(2):
            toks = {}
            for j in range(16):
                kc = hf * 16 + j
                eng = "sync" if kc % 2 == 0 else "scalar"
                toks[eng] = ph.dma(eng, lambda s, sq=sq, kc=kc, hp=hp:
                                   s.dma_start_transpose(
                                       PT[:, kc * 1024:(kc + 1) * 1024],
                                       c.PSCR[hp * c.T + sq * 1024:
                                              hp * c.T + (sq + 1) * 1024,
                                              kc * 128:(kc + 1) * 128]),
                                   waits=extra + last.get(("pth", hf), []),
                                   ctr=ptlc[eng])
            last[("ptld", hf)] = list(toks.values())

    def av_slot(h1, sq):
        hp = h1 % 2
        vs = h1 % 3
        q = next_quad()
        if sq == 0:
            pt_loads(h1, 0, [last.get(("plast", hp))])
        h0rd = []
        h1rd = []
        o_av = {}
        for qg in range(2):
            bk = q * 4 + qg

            def mm_h0(t, qg=qg, bk=bk, vs=vs):
                r = None
                for kc in range(16):
                    r = t.matmul(c.pb[bk],
                                 vh[:, vs * CTXT + kc * 128:vs * CTXT + (kc + 1) * 128],
                                 PT[:, kc * 1024 + qg * 512:kc * 1024 + (qg + 1) * 512],
                                 start=(kc == 0), stop=False)
                return r
            o1 = ph.op("tensor", mm_h0,
                       waits=last[("ptld", 0)] + [last.get(("vhdone", hp))] +
                             last.get(("quad", q), []), ctr=c.tq)

            def mm_h1(t, qg=qg, bk=bk, vs=vs):
                r = None
                for kc in range(16, 32):
                    r = t.matmul(c.pb[bk],
                                 vh[:, vs * CTXT + kc * 128:vs * CTXT + (kc + 1) * 128],
                                 PT[:, kc * 1024 + qg * 512:kc * 1024 + (qg + 1) * 512],
                                 start=False, stop=(kc == 31))
                return r
            o2 = ph.op("tensor", mm_h1, waits=last[("ptld", 1)], ctr=c.tq)
            h0rd.append(o1)
            h1rd.append(o2)
            o_av[qg] = o2
        evs = []
        for qg in range(2):
            bk = q * 4 + qg
            ob = otb[:, qg * 512:(qg + 1) * 512]
            o_oe = ph.op("scalar", lambda s, bk=bk, ob=ob: s.activation(
                out=ob, in_=c.pb[bk], func=AF.Copy),
                waits=[o_av[qg], last.get(("otb", qg))], ctr=c.sq)
            evs.append(o_oe)
            o_ost = ph.dma("gpsimd", lambda g, h1=h1, sq=sq, qg=qg, ob=ob: g.dma_start(
                out=c.OUTT[h1 * 128:(h1 + 1) * 128,
                           sq * 1024 + qg * 512:sq * 1024 + (qg + 1) * 512],
                in_=ob), waits=[o_oe], ctr=otc[qg])
            last[("otb", qg)] = o_ost
        last[("quad", q)] = evs
        last[("pth", 0)] = [h0rd[1]]
        last[("pth", 1)] = [h1rd[1]]
        if sq + 1 < NSQ:
            pt_loads(h1, sq + 1, [])
        return h1rd[1]

    def khvh_slot(h2, kg):
        hp = h2 % 2
        vs = h2 % 3
        q = next_quad()
        cp = 0
        ctv = ctb[:, 0:3072]
        o_ld = None
        for cdc in range(6):
            o_ld = ph.dma("sync", lambda s, kg=kg, cdc=cdc, ctv=ctv:
                          s.dma_start_transpose(
                              ctv[:, cdc * 512:(cdc + 1) * 512],
                              c.CTX16[kg * 512:(kg + 1) * 512,
                                      cdc * 128:(cdc + 1) * 128]),
                          waits=[last.get(("ctb", cp))], ctr=ctl[cp])
        bkh = q * 4 + 2
        bvh = q * 4 + 3

        def mm1(t, h2=h2, ctv=ctv, bkh=bkh):
            r = None
            for cdc in range(6):
                r = t.matmul(c.pb[bkh],
                             c.Wk1sb[:, cdc * D + h2 * 128:cdc * D + (h2 + 1) * 128],
                             ctv[:, cdc * 512:(cdc + 1) * 512],
                             start=(cdc == 0), stop=(cdc == 5))
            return r
        o_m = ph.op("tensor", mm1,
                    waits=[o_ld] + last.get(("quad", q), []), ctr=c.tq)
        o_e = ph.op("scalar", lambda s, kg=kg, hp=hp, bkh=bkh: s.activation(
            out=khT[:, hp * CTXT + kg * 512:hp * CTXT + (kg + 1) * 512],
            in_=c.pb[bkh], func=AF.Copy), waits=[o_m], ctr=c.sq)
        o_vh = None
        o_prev = None
        for kt4 in range(4):
            kt = kg * 4 + kt4

            def mm2(t, h2=h2, kt4=kt4, ctv=ctv, bvh=bvh):
                r = None
                for cdc in range(6):
                    r = t.matmul(c.pb[bvh][:, 0:128],
                                 ctv[:, cdc * 512 + kt4 * 128:cdc * 512 + (kt4 + 1) * 128],
                                 c.Wv1sb[:, cdc * D + h2 * 128:cdc * D + (h2 + 1) * 128],
                                 start=(cdc == 0), stop=(cdc == 5))
                return r
            o_m2 = ph.op("tensor", mm2,
                         waits=([o_ld] + last.get(("quad", q), [])) if kt4 == 0
                               else [o_prev], ctr=c.tq)
            o_e2 = ph.op("vector", lambda v, kt=kt, vs=vs, bvh=bvh: v.tensor_copy(
                out=vh[:, vs * CTXT + kt * 128:vs * CTXT + (kt + 1) * 128],
                in_=c.pb[bvh][:, 0:128]), waits=[o_m2], ctr=c.vq)
            o_prev = o_e2
            o_vh = o_e2
        last[("ctb", cp)] = o_vh
        last[("quad", q)] = [o_e, o_vh]
        if kg == NKG - 1:
            last[("khdone", hp)] = o_e
            last[("vhdone", hp)] = o_vh

    for kg in range(NKG):
        khvh_slot(0, kg)
    SPT = c.NT // NSQ
    for h in range(L1H + 1):
        av_done = None
        for ti in range(c.NT):
            if h < L1H:
                o_ps = soft_tile(h, ti)
                if ti == c.NT - 1:
                    last[("plast", h % 2)] = o_ps
            if h > 0 and ti % SPT == SPT - 1:
                av_done = av_slot(h - 1, ti // SPT)
            KVS = max(1, c.NT // NKG)
            if h + 1 < L1H and ti % KVS == KVS // 2 and ti // KVS < NKG:
                khvh_slot(h + 1, ti // KVS)
        if h > 0:
            last[("pscr", (h - 1) % 2)] = av_done
    ph.emit()


def p65_ffn(c):
    with contextlib.ExitStack() as px:
        _p65a_inner(c, px.enter_context)
    barrier(c, "b65b")
    with contextlib.ExitStack() as px:
        _p65b_inner(c, px.enter_context)
    barrier(c, "b65c")
    with contextlib.ExitStack() as px:
        _p65c_inner(c, px.enter_context)


def _p65a_inner(c, ep):
    nc = c.nc
    T = c.T
    ph = Phase(c, "p65a")
    wsl = ep(nc.sbuf_tensor("p65_wsl", [128, 2 * MH], F32))
    gmx = ep(nc.sbuf_tensor("p65_gmx", [128, 4], F32))
    grow = ep(nc.sbuf_tensor("p65_grow", [1, 8], F32))
    wq16 = ep(nc.sbuf_tensor("p65_wq16", [128, 2 * 1024], FP16))
    qb = SimpleNamespace(
        sm=ep(nc.sbuf_tensor("p65_qsm", [128, 10 * 64], F32)),
        ind=ep(nc.sbuf_tensor("p65_qind", [128, 16 * 64], F32)),
        dq16=ep(nc.sbuf_tensor("p65_qdq", [128, 64], FP16)),
        fb=ep(nc.sbuf_tensor("p65_qfb", [128, 2 * 1024], F32)),
        hb=ep(nc.sbuf_tensor("p65_qhb", [128, 8 * 1024], FP16)))
    wlc = [sctr(c, "p65w0"), sctr(c, "p65w1")]
    wst = [sctr(c, "p65s0"), sctr(c, "p65s1")]
    WCFG = (("GW", c.GQ, FH, D, 4), ("UW", c.UQ, FH, D, 6), ("DW", c.DQ, D, FH, 8))
    prevw16 = {}
    qci = 0
    o_wdone = None
    for wname, wdst, R, Ccol, gslot in WCFG:
        nsl = R // 128
        o_red_last = None
        prevld = {0: o_wdone, 1: o_wdone}
        for i in range(nsl):
            b = i % 2
            o_l = ph.dma("sync", lambda s, wname=wname, i=i, b=b, Ccol=Ccol: s.dma_start(
                out=wsl[:, b * MH:b * MH + Ccol],
                in_=c.ins[wname][i * 128:(i + 1) * 128, :]),
                waits=[prevld.get(b)], ctr=wlc[b])

            def vred(v, i=i, b=b, Ccol=Ccol):
                sv = lambda j: selfsync(c, "vector", v, j)
                sv(v.tensor_reduce(out=gmx[:, 1:2], in_=wsl[:, b * MH:b * MH + Ccol],
                                   axis=AX.X, op=ALU.max, apply_absolute_value=True))
                if i == 0:
                    return v.tensor_copy(out=gmx[:, 0:1], in_=gmx[:, 1:2])
                return v.tensor_tensor(out=gmx[:, 0:1], in0=gmx[:, 0:1],
                                       in1=gmx[:, 1:2], op=ALU.max)
            o_r = ph.op("vector", vred, waits=[o_l], ctr=c.vq)
            prevld[b] = o_r
            o_red_last = o_r
        o_tp = ph.op("tensor", lambda t: t.transpose(
            c.PS[0:1, 0:128], gmx[:, 0:1], c.identf[:]),
            waits=[o_red_last, last_ps(c, ph)], ctr=c.tq)

        def vg(v):
            sv = lambda j: selfsync(c, "vector", v, j)
            sv(v.tensor_reduce(out=grow[0:1, 0:1], in_=c.PS[0:1, 0:128], axis=AX.X,
                               op=ALU.max))
            sv(v.reciprocal(out=grow[0:1, 1:2], in_=grow[0:1, 0:1]))
            sv(v.tensor_scalar_mul(out=grow[0:1, 2:3], in0=grow[0:1, 1:2],
                                   scalar1=448.0))
            return v.tensor_scalar_mul(out=grow[0:1, 3:4], in0=grow[0:1, 0:1],
                                       scalar1=1.0 / 2688.0)
        o_vg = ph.op("vector", vg, waits=[o_tp], ctr=c.vq)
        o_bc = ph.op("tensor", lambda t: t.matmul(
            c.PS[:, 0:2], c.ones_rowf[0:1, 0:128], grow[0:1, 2:4],
            start=True, stop=True), waits=[o_vg], ctr=c.tq)
        o_gc = ph.op("vector", lambda v, gslot=gslot: v.tensor_copy(
            out=c.gs_cols[:, gslot:gslot + 2], in_=c.PS[:, 0:2]),
            waits=[o_bc], ctr=c.vq)
        set_last_ps(c, o_gc)
        prevld = {0: o_gc, 1: o_gc}
        for i in range(nsl):
            b = i % 2
            o_l = ph.dma("sync", lambda s, wname=wname, i=i, b=b, Ccol=Ccol: s.dma_start(
                out=wsl[:, b * MH:b * MH + Ccol],
                in_=c.ins[wname][i * 128:(i + 1) * 128, :]),
                waits=[prevld.get(b)], ctr=wlc[b])
            o_qlast = None
            for qc in range(Ccol // 1024):
                sl16 = qci % 2
                qci += 1
                o_q = emit_quant(c, ph,
                                 wsl[:, b * MH + qc * 1024:b * MH + (qc + 1) * 1024],
                                 wq16[:, sl16 * 1024:(sl16 + 1) * 1024],
                                 64, c.gs_cols[:, gslot:gslot + 1],
                                 c.gs_cols[:, gslot + 1:gslot + 2], qb,
                                 waits=[o_l, o_gc, prevw16.get(sl16)])
                o_s = ph.dma("gpsimd", lambda g, wdst=wdst, i=i, qc=qc, sl16=sl16:
                             g.dma_start(out=wdst[i * 128:(i + 1) * 128,
                                                  qc * 1024:(qc + 1) * 1024],
                                         in_=wq16[:, sl16 * 1024:(sl16 + 1) * 1024]),
                             waits=[o_q], ctr=wst[sl16])
                prevw16[sl16] = o_s
                o_qlast = o_q
            prevld[b] = o_qlast
        o_wdone = o_qlast
    ph.emit()


def _p65b_inner(c, ep):
    nc = c.nc
    T = c.T
    ph = Phase(c, "p65b")
    TH = T // 4 if T >= 4096 else T // 2
    xqT = ep(nc.sbuf_tensor("p65_xqT", [128, 8 * (T // 2)], FP16))
    gqt = ep(nc.sbuf_tensor("p65_gqt", [128, 2 * 8 * 512], FP16))
    uqt = ep(nc.sbuf_tensor("p65_uqt", [128, 2 * 8 * 512], FP16))
    gsl = ep(nc.sbuf_tensor("p65_gsl", [128, 512], F32))
    hfl = ep(nc.sbuf_tensor("p65_hf", [128, 8 * 512], F32))
    hq16 = ep(nc.sbuf_tensor("p65_hq", [128, 8 * 512], FP16))
    qb2 = SimpleNamespace(
        sm=ep(nc.sbuf_tensor("p65_qsm2", [128, 10 * 256], F32)),
        ind=ep(nc.sbuf_tensor("p65_qind2", [128, 16 * 256], FP16)),
        dq16=ep(nc.sbuf_tensor("p65_qdq2", [128, 256], FP16)),
        fb=ep(nc.sbuf_tensor("p65_qfb2", [128, 2 * 512], F32)),
        hb=ep(nc.sbuf_tensor("p65_qhb2", [128, 8 * 512], FP16)))
    xql = sctr(c, "p65x")
    gul = [sctr(c, "p65g0"), sctr(c, "p65g1")]
    hst = [sctr(c, "p65h0"), sctr(c, "p65h1")]
    last = {}
    for half in range(T // TH):
        t0 = half * TH
        o_x = None
        for dc in range(8):
            o_x = ph.dma("sync", lambda s, dc=dc, t0=t0: s.dma_start_transpose(
                xqT[:, dc * TH:(dc + 1) * TH],
                c.XQ[t0:t0 + TH, dc * 128:(dc + 1) * 128]),
                waits=[last.get("xqT")], ctr=xql)
        for hg in range(8):
            hp = hg % 2
            o_g = None
            for dc in range(8):
                o_g = ph.dma("sync", lambda s, dc=dc, hg=hg, hp=hp: s.dma_start_transpose(
                    gqt[:, hp * 4096 + dc * 512:hp * 4096 + (dc + 1) * 512],
                    c.GQ[hg * 512:(hg + 1) * 512, dc * 128:(dc + 1) * 128]),
                    waits=[last.get(("gq", hp))], ctr=gul[hp])
                o_g = ph.dma("sync", lambda s, dc=dc, hg=hg, hp=hp: s.dma_start_transpose(
                    uqt[:, hp * 4096 + dc * 512:hp * 4096 + (dc + 1) * 512],
                    c.UQ[hg * 512:(hg + 1) * 512, dc * 128:(dc + 1) * 128]),
                    waits=[last.get(("gq", hp))], ctr=gul[hp])
            o_mmlast = None
            ntt = TH // 128
            _qb = int(_os.environ.get('QBATCH', '8'))
            for sb0 in range(0, ntt, _qb):
                kk = min(_qb, ntt - sb0)
                items = []
                for j in range(kk):
                    tt = sb0 + j
                    gbk = tt % 2
                    ubk = 2 + tt % 2

                    def mmg(t, tt=tt, hp=hp, gbk=gbk, ubk=ubk):
                        for dc in range(8):
                            t.matmul(c.pb[gbk],
                                     xqT[:, dc * TH + tt * 128:dc * TH + (tt + 1) * 128],
                                     gqt[:, hp * 4096 + dc * 512:hp * 4096 + (dc + 1) * 512],
                                     start=(dc == 0), stop=(dc == 7))
                        r = None
                        for dc in range(8):
                            r = t.matmul(c.pb[ubk],
                                         xqT[:, dc * TH + tt * 128:dc * TH + (tt + 1) * 128],
                                         uqt[:, hp * 4096 + dc * 512:hp * 4096 + (dc + 1) * 512],
                                         start=(dc == 0), stop=(dc == 7))
                        return r
                    o_mm = ph.op("tensor", mmg,
                                 waits=[o_x, o_g, last.get(("pb", gbk)),
                                        last.get(("pb", ubk))], ctr=c.tq)
                    o_mmlast = o_mm
                    o_sil = ph.op("scalar", lambda s, gbk=gbk: s.activation(
                        out=gsl[:], in_=c.pb[gbk], func=AF.Silu),
                        waits=[o_mm, last.get("gsl")], ctr=c.sq)

                    def vh_(v, ubk=ubk, j=j):
                        return v.tensor_tensor(out=hfl[:, j * 512:(j + 1) * 512],
                                               in0=gsl[:], in1=c.pb[ubk], op=ALU.mult)
                    o_vh = ph.op("vector", vh_,
                                 waits=[o_sil, last.get(("hfl", j))], ctr=c.vq)
                    last[("pb", gbk)] = o_vh
                    last[("pb", ubk)] = o_vh
                    last["gsl"] = o_vh
                    items.append((hfl[:, j * 512:(j + 1) * 512],
                                  hq16[:, j * 512:(j + 1) * 512],
                                  [o_vh, last.get(("hq", j))]))
                qouts = emit_quant_batch(c, ph, items, 32,
                                         c.gs_cols[:, 2:3], c.gs_cols[:, 3:4], qb2)
                for j in range(kk):
                    tt = sb0 + j
                    ti_glob = (t0 // 128) + tt
                    hb_ = tt % 2
                    o_hs = ph.dma("gpsimd", lambda g, ti_glob=ti_glob, hg=hg, j=j:
                                  g.dma_start(out=c.HQ[ti_glob * 128:(ti_glob + 1) * 128,
                                                       hg * 512:(hg + 1) * 512],
                                              in_=hq16[:, j * 512:(j + 1) * 512]),
                                  waits=[qouts[j]], ctr=hst[j % 2])
                    last[("hfl", j)] = qouts[j]
                    last[("hq", j)] = o_hs
            last[("gq", hp)] = o_mmlast
        last["xqT"] = o_mmlast
    ph.emit()


def _p65c_inner(c, ep):
    nc = c.nc
    ph = Phase(c, "p65c")
    dqT = ep(nc.sbuf_tensor("p65_dqT", [128, 32 * 1024], FP16))
    hqT = ep(nc.sbuf_tensor("p65_hqT", [128, 2 * 32 * 512], FP16))
    f1b = ep(nc.sbuf_tensor("p65_f1", [128, 2 * D], FP16))
    dql = sctr(c, "p65d")
    hql = [sctr(c, "p65hq0"), sctr(c, "p65hq1")]
    f1st = [sctr(c, "p65f0"), sctr(c, "p65f1")]
    o_d = None
    for hc in range(32):
        o_d = ph.dma("sync", lambda s, hc=hc: s.dma_start_transpose(
            dqT[:, hc * 1024:(hc + 1) * 1024],
            c.DQ[0:1024, hc * 128:(hc + 1) * 128]), ctr=dql)
    last = {}
    for tg in range(c.NTG):
        gp = tg % 2
        hv = hqT[:, gp * 16384:(gp + 1) * 16384]
        o_h = None
        for hc in range(32):
            o_h = ph.dma("sync", lambda s, tg=tg, hc=hc, hv=hv: s.dma_start_transpose(
                hv[:, hc * 512:(hc + 1) * 512],
                c.HQ[tg * 512:(tg + 1) * 512, hc * 128:(hc + 1) * 128]),
                waits=[last.get(("hqT", gp))], ctr=hql[gp])
        o_mml = None
        for tt in range(4):
            for og in range(2):
                bk = tt % 2 * 2 + og

                def mmd(t, tt=tt, og=og, bk=bk, hv=hv):
                    r = None
                    for hc in range(32):
                        r = t.matmul(c.pb[bk],
                                     hv[:, hc * 512 + tt * 128:hc * 512 + (tt + 1) * 128],
                                     dqT[:, hc * 1024 + og * 512:hc * 1024 + (og + 1) * 512],
                                     start=(hc == 0), stop=(hc == 31))
                    return r
                o_mm = ph.op("tensor", mmd,
                             waits=[o_h, o_d, last.get(("pb", bk))], ctr=c.tq)
                o_mml = o_mm
                fp = tt % 2
                fb = f1b[:, fp * D + og * 512:fp * D + (og + 1) * 512]
                o_fe = ph.op("scalar", lambda s, bk=bk, fb=fb: s.activation(
                    out=fb, in_=c.pb[bk], func=AF.Copy),
                    waits=[o_mm, last.get(("f1", fp, og))], ctr=c.sq)
                last[("pb", bk)] = o_fe
                ti = tg * 4 + tt
                o_fs = ph.dma("gpsimd", lambda g, ti=ti, og=og, fb=fb: g.dma_start(
                    out=c.F1[ti * 128:(ti + 1) * 128, og * 512:(og + 1) * 512], in_=fb),
                    waits=[o_fe], ctr=f1st[fp])
                last[("f1", fp, og)] = o_fs
        last[("hqT", gp)] = o_mml
    ph.emit()


def last_ps(c, ph):
    return getattr(c, "_ps_last", None)


def set_last_ps(c, op):
    c._ps_last = op


def p7_final(c):
    with contextlib.ExitStack() as px:
        _p7_inner(c, px.enter_context)


def _p7_inner(c, ep):
    nc = c.nc
    ph = Phase(c, "p7")
    Wo1sb = ep(nc.sbuf_tensor("p7_wo", [128, 8 * D], FP16))
    otl = ep(nc.sbuf_tensor("p7_ot", [128, 2 * 8 * 128], FP16))
    f1l = ep(nc.sbuf_tensor("p7_f1", [128, 2 * D], FP16))
    x1l = ep(nc.sbuf_tensor("p7_x1", [128, 2 * D], F32))
    g1 = ep(nc.sbuf_tensor("p7_g1", [128, D], F32))
    tsb = ep(nc.sbuf_tensor("p7_t", [128, D], F32))
    osb = ep(nc.sbuf_tensor("p7_o", [128, 2 * D], F32))
    wol = sctr(c, "p7w")
    otc = [sctr(c, "p7o0"), sctr(c, "p7o1")]
    f1c = [sctr(c, "p7f0"), sctr(c, "p7f1")]
    x1c = [sctr(c, "p7x0"), sctr(c, "p7x1")]
    ostc = [sctr(c, "p7s0"), sctr(c, "p7s1")]
    o_w = None
    for dc in range(8):
        o_w = ph.dma("gpsimd", lambda g, dc=dc: g.dma_start(
            out=Wo1sb[:, dc * D:(dc + 1) * D],
            in_=c.ins["Wo1"][dc * 128:(dc + 1) * 128, :]), ctr=wol)
    last = {}
    for ti in range(c.NT):
        p = ti % 2
        ov = otl[:, p * 1024:(p + 1) * 1024]
        fv = f1l[:, p * D:(p + 1) * D]
        xv = x1l[:, p * D:(p + 1) * D]
        outv = osb[:, p * D:(p + 1) * D]
        o_ot = ph.dma("sync", lambda s, ti=ti, ov=ov: s.dma_start(
            out=ov.rearrange("q (a t) -> q a t", t=128),
            in_=c.OUTT[0:D, ti * 128:(ti + 1) * 128].rearrange(
                "(a q) t -> q a t", q=128)),
            waits=[last.get(("ot", p))], ctr=otc[p])
        o_f1 = ph.dma("sync", lambda s, ti=ti, fv=fv: s.dma_start(
            out=fv, in_=c.F1[ti * 128:(ti + 1) * 128, :]),
            waits=[last.get(("f1", p))], ctr=f1c[p])
        o_x1 = ph.dma("sync", lambda s, ti=ti, xv=xv: s.dma_start(
            out=xv, in_=c.X1[ti * 128:(ti + 1) * 128, :]),
            waits=[last.get(("x1", p))], ctr=x1c[p])
        o_sil = ph.op("scalar", lambda s, fv=fv: s.activation(
            out=g1[:], in_=fv, func=AF.Silu), waits=[o_f1, last.get("g1")], ctr=c.sq)
        for og in range(2):
            bk = p * 2 + og

            def mms(t, ov=ov, og=og, bk=bk):
                for ic in range(8):
                    t.matmul(c.pb[bk], ov[:, ic * 128:(ic + 1) * 128],
                             Wo1sb[:, ic * D + og * 512:ic * D + (og + 1) * 512],
                             start=(ic == 0), stop=False)
                return t.matmul(c.pb[bk], c.ones_row[0:1, 0:128],
                                c.bo1_row[0:1, og * 512:(og + 1) * 512],
                                start=False, stop=True)
            o_mm = ph.op("tensor", mms,
                         waits=[o_ot, o_w, last.get(("pb", bk))], ctr=c.tq)

            def vout(v, og=og, bk=bk, outv=outv, xv=xv):
                selfsync(c, "vector", v, v.tensor_tensor(
                    out=tsb[:, og * 512:(og + 1) * 512],
                    in0=g1[:, og * 512:(og + 1) * 512], in1=c.pb[bk], op=ALU.mult))
                return v.tensor_tensor(out=outv[:, og * 512:(og + 1) * 512],
                                       in0=tsb[:, og * 512:(og + 1) * 512],
                                       in1=xv[:, og * 512:(og + 1) * 512], op=ALU.add)
            o_vo = ph.op("vector", vout,
                         waits=[o_mm, o_sil, o_x1, last.get(("ost", p))], ctr=c.vq)
            last[("pb", bk)] = o_vo
            last["g1"] = o_vo
            last[(ti, og)] = o_vo
        o_st = ph.dma("sync", lambda s, ti=ti, outv=outv: s.dma_start(
            out=c.out[ti * 128:(ti + 1) * 128, :], in_=outv),
            waits=[last[(ti, 0)], last[(ti, 1)]], ctr=ostc[p])
        last[("ost", p)] = o_st
        last[("ot", p)] = o_mm
        last[("f1", p)] = o_vo
        last[("x1", p)] = o_vo
    ph.emit()



def sctr(c, name):
    if name not in c.ctrs:
        c.ctrs[name] = Ctr(c.eg(c.nc.semaphore(name)))
    return c.ctrs[name]


def make_ctx(B, S, dbg=()):
    T = B * S
    c = SimpleNamespace()
    c.B, c.S, c.T = B, S, T
    c.NT = T // 128
    c.NCH = S // 128
    c.CAP = 2 * (T // E)
    c.NTG = T // 512
    nc = bass.Bass("TRN2", target_bir_lowering=False, detect_race_conditions=False)
    c.nc = nc

    def din(name, shape):
        return nc.dram_tensor(name, shape, F32, kind="ExternalInput").ap()

    def dscr(name, shape, dt=FP16):
        kind = "ExternalOutput" if name in dbg else "Internal"
        return nc.dram_tensor(name, shape, dt, kind=kind).ap()
    c.ins = {}
    for nm, shape in [
        ("x", [T, D]), ("Wq0", [D, D]), ("bq0", [D]), ("Wk0", [D, D]), ("bk0", [D]),
        ("Wv0", [D, D]), ("bv0", [D]), ("Wb0", [D, D]), ("bb0", [D]),
        ("Wa0", [D, D]), ("ba0", [D]),
        ("Wr", [D, E]), ("br", [E]), ("W1", [E, D, MH]), ("b1", [E, MH]),
        ("W2", [E, MH, D]), ("b2", [E, D]),
        ("ctx", [CTXT, CD]), ("Wq1", [D, D]), ("Wk1", [CD, D]), ("Wv1", [CD, D]),
        ("Wo1", [D, D]), ("bo1", [D]),
        ("GW", [FH, D]), ("UW", [FH, D]), ("DW", [D, FH]),
        ("s_in", [1]), ("s_in_d", [1]),
    ]:
        c.ins[nm] = din(nm, shape)
    c.out = nc.dram_tensor("OUT", [T, D], F32, kind="ExternalOutput").ap()
    c.W16 = {w + "": dscr(w + "16", [D, D]) for w in ("Wq0", "Wk0", "Wv0", "Wb0", "Wa0")}
    c.XN = dscr("XN", [T, D])
    c.QT = dscr("QT", [D, T])
    c.KT = dscr("KT", [D, T])
    c.KTM = dscr("KTM", [T, D])
    c.VTM = dscr("VTM", [T, D])
    c.VBS = dscr("VBS", [T, D])
    c.YATT = dscr("YATT", [T, D])
    c.W2S = dscr("W2S", [E, MH, D])
    c.XG = dscr("XG", [E * c.CAP, D])
    c.YG = dscr("YG", [E * c.CAP, D])
    c.X1 = dscr("X1", [T, D], F32)
    c.XN1 = dscr("XN1", [T, D])
    c.XQ = dscr("XQ", [T, D])
    c.QT1 = dscr("QT1", [D, T])
    c.CTX16 = dscr("CTX16", [CTXT, CD])
    c.PSCR = dscr("PSCR", [2 * T, CTXT], FP16)
    c.OUTT = dscr("OUTT", [D, T])
    c.GQ = dscr("GQ", [FH, D])
    c.UQ = dscr("UQ", [FH, D])
    c.DQ = dscr("DQ", [D, FH])
    c.HQ = dscr("HQ", [T, FH])
    c.F1 = dscr("F1", [T, D])
    c.DBGS = dscr("DBGS", [128, 3 * c.NT * NH], F32)
    return c


def build(B=4, S=2048, upto=7, dbg=()):
    c = make_ctx(B, S, dbg)
    nc = c.nc
    with contextlib.ExitStack() as gx:
        c.eg = gx.enter_context
        eg = c.eg
        c.ident = eg(nc.sbuf_tensor("ident", [128, 128], FP16))
        c.identf = eg(nc.sbuf_tensor("identf", [128, 128], F32))
        c.iotav = eg(nc.sbuf_tensor("iotav", [128, 128], F32))
        c.madd = eg(nc.sbuf_tensor("madd", [128, 128], F32))
        c.uinc = eg(nc.sbuf_tensor("uinc", [128, 128], FP16))
        c.ustrf = eg(nc.sbuf_tensor("ustrf", [128, 128], F32))
        c.uincf = eg(nc.sbuf_tensor("uincf", [128, 128], F32))
        c.rowsel = eg(nc.sbuf_tensor("rowsel", [16, NH * 128], F32))
        c.rsel_t = eg(nc.sbuf_tensor("rsel_t", [16, NH * 128], F32))
        c.ones_row = eg(nc.sbuf_tensor("ones_row", [1, 1024], FP16))
        c.ones_rowf = eg(nc.sbuf_tensor("ones_rowf", [1, 128], F32))
        c.ones_colb = eg(nc.sbuf_tensor("ones_colb", [128, 1], FP16))
        c.ones8f = eg(nc.sbuf_tensor("ones8f", [8, 128], F32))
        c.MLA = eg(nc.sbuf_tensor("MLA", [128, c.NT * NH], F32))
        c.BMS = eg(nc.sbuf_tensor("BMS", [128, c.NT * NH], F32))
        c.MASKS = eg(nc.sbuf_tensor("MASKS", [128, c.NT * E], FP16))
        c.MASKSF = eg(nc.sbuf_tensor("MASKSF", [128, c.NT * E], F32))
        c.DIDX = eg(nc.sbuf_tensor("DIDX", [128, c.NT], I32))
        c.S_all = eg(nc.sbuf_tensor("S_all", [128, B * NH * 32], F32))
        c.ecap_col = eg(nc.sbuf_tensor("ecap_col", [8, 1], F32))
        c.ecapmax = eg(nc.sbuf_tensor("ecapmax", [128, 8], F32))
        c.bias_rows = eg(nc.sbuf_tensor("bias_rows", [1, 4 * 1024], FP16))
        c.br_row = eg(nc.sbuf_tensor("br_row", [1, 8], FP16))
        c.bo1_row = eg(nc.sbuf_tensor("bo1_row", [1, 1024], FP16))
        c.bq_cols = eg(nc.sbuf_tensor("bq_cols", [128, 16], F32))
        c.gs_cols = eg(nc.sbuf_tensor("gs_cols", [128, 12], F32))
        c.sc_sm = eg(nc.sbuf_tensor("sc_sm", [1, 8], F32))
        c.cnt_sb = eg(nc.sbuf_tensor("cnt_sb", [8, 64], F32))
        c.constcols = eg(nc.sbuf_tensor("constcols", [128, 4], F32))
        c.PS = eg(nc.psum_tensor("PS", [128, 4096], F32))
        c.pb = [c.PS[:, i * 512:(i + 1) * 512] for i in range(8)]
        c.ctrs = {n: Ctr(eg(nc.semaphore(n))) for n in
                  ["ldA", "ldB", "ldC", "stA", "tq", "vq", "sq", "gq"]}
        c.self_ctrs = {n: Ctr(eg(nc.semaphore("self_" + n)))
                       for n in ("vector", "scalar", "gpsimd", "tensor")}
        for n, ct in c.self_ctrs.items():
            c.ctrs["self_" + n] = ct
        for k, v in c.ctrs.items():
            setattr(c, k if k != "sq" else "sq", v)
        c.ldA, c.ldB, c.ldC, c.stA = (c.ctrs[k] for k in ("ldA", "ldB", "ldC", "stA"))
        c.tq, c.vq, c.sq, c.gq = (c.ctrs[k] for k in ("tq", "vq", "sq", "gq"))
        c.block = eg(nc.Block())

        p0_consts(c)
        if upto >= 1:
            barrier(c, "b1")
            emit_layernorm(c, "p1", c.ins["x"], c.XN)
        if upto >= 2:
            barrier(c, "b2")
            p2_proj(c)
        if upto >= 3:
            barrier(c, "b3")
            p3_deltanet(c)
        if upto >= 4:
            barrier(c, "b4")
            p4_moe(c)
            barrier(c, "b4c")
            p4c_gate(c)
        if upto >= 5:
            barrier(c, "b5")
            with contextlib.ExitStack() as p5s:
                p5_prep(c, p5s.enter_context)
                if upto >= 6:
                    barrier(c, "b6")
                    p6_attn(c)
        if upto >= 6.5:
            barrier(c, "b65")
            p65_ffn(c)
        if upto >= 7:
            barrier(c, "b7")
            p7_final(c)
        if "DBGS" in dbg:
            barrier(c, "bd")
            pdbg(c)
        barrier(c, "bf")
    return c


def pdbg(c):
    ph = Phase(c, "pdbg")
    n = c.NT * NH
    o1 = ph.dma("gpsimd", lambda g: g.dma_start(out=c.DBGS[0:128, 0:n], in_=c.MLA[:]),
                ctr=c.stA)
    o2 = ph.dma("gpsimd", lambda g: g.dma_start(out=c.DBGS[0:128, n:2 * n], in_=c.BMS[:]),
                ctr=c.stA)
    o3 = ph.dma("gpsimd", lambda g: g.dma_start(
        out=c.DBGS[0:128, 2 * n:2 * n + c.NT * E], in_=c.MASKSF[:]), ctr=c.stA)
    ph.emit()


_NAME_MAP = {
    "x": "L0_attn_x", "Wq0": "L0_attn_Wq", "bq0": "L0_attn_bq",
    "Wk0": "L0_attn_Wk", "bk0": "L0_attn_bk", "Wv0": "L0_attn_Wv",
    "bv0": "L0_attn_bv", "Wb0": "L0_attn_Wbeta", "bb0": "L0_attn_bbeta",
    "Wa0": "L0_attn_Walpha", "ba0": "L0_attn_balpha",
    "Wr": "L0_ffn_Wr", "br": "L0_ffn_br", "W1": "L0_ffn_W1", "b1": "L0_ffn_b1",
    "W2": "L0_ffn_W2", "b2": "L0_ffn_b2",
    "ctx": "L1_attn_context", "Wq1": "L1_attn_Wq", "Wk1": "L1_attn_Wk",
    "Wv1": "L1_attn_Wv", "Wo1": "L1_attn_Wout", "bo1": "L1_attn_bout",
    "GW": "L1_ffn_gate_w", "UW": "L1_ffn_up_w", "DW": "L1_ffn_down_w",
    "s_in": "L1_ffn_s_in", "s_in_d": "L1_ffn_s_in_d" + "own",
}


def kernel(**inputs):
    NCORES = 8
    B, S = 4, 2048
    T = B * S
    x = np.ascontiguousarray(np.asarray(inputs["L0_attn_x"], np.float32))
    shards = [np.ascontiguousarray(x[i * B:(i + 1) * B].reshape(T, D))
              for i in range(NCORES)]
    common = {}
    for knl, kref in _NAME_MAP.items():
        if knl == "x":
            continue
        v = np.ascontiguousarray(np.asarray(inputs[kref], np.float32))
        if knl == "ctx":
            v = v.reshape(-1, CD)
        common[knl] = v
    c = build(B=B, S=S, upto=7)
    in_maps = [dict(common, x=shards[i]) for i in range(NCORES)]
    res = run_bass_kernel_spmd(c.nc, in_maps, core_ids=list(range(NCORES)))
    out = np.concatenate([np.asarray(r["OUT"], np.float32).reshape(B, S, D)
                          for r in res.results], axis=0)
    return out


# version 10 (final)
# speedup vs baseline: 1.0887x; 1.0083x over previous
import contextlib
import math
import os as _os
from types import SimpleNamespace
import numpy as np
import concourse.bass as bass
import concourse.mybir as mybir
from concourse.bass_utils import run_bass_kernel_spmd

F32 = mybir.dt.float32
BF16 = mybir.dt.bfloat16
FP16 = mybir.dt.float16
I32 = mybir.dt.int32
AF = mybir.ActivationFunctionType
ALU = mybir.AluOpType
AX = mybir.AxisListType

D = 1024
HD = 64
NH = 16
E = 8
MH = 4096
L1H = 8
L1D = 128
CD = 768
CTXT = 4096
FH = 4096
QSCALE = HD ** -0.5
ASCALE = 1.5 * (L1D ** -0.5)
LNEPS = 1e-5
LN2 = math.log(2.0)

ENGS = ("tensor", "vector", "scalar", "sync", "gpsimd")


class Ctr:
    def __init__(self, sem):
        self.sem = sem
        self.n = 0


class Op:
    __slots__ = ("eng", "fn", "waits", "ctr", "k", "tok", "idx", "ph")

    def __init__(self, eng, fn, waits, ctr, k, idx, ph=None):
        self.eng, self.fn, self.waits, self.ctr, self.k, self.idx = \
            eng, fn, list(waits), ctr, k, idx
        self.tok = None
        self.ph = ph


class Hard:

    def __init__(self, op):
        self.op = op


class Phase:
    def __init__(self, c, name):
        self.c = c
        self.name = name
        self.ops = []

    def op(self, eng, fn, waits=(), ctr=None, k=1):
        o = Op(eng, fn, waits, ctr, k, len(self.ops), self)
        self.ops.append(o)
        return o

    def dma(self, eng, fn, waits=(), ctr=None):
        return self.op(eng, fn, waits, ctr, 16)

    def emit(self):
        c = self.c
        for o in self.ops:
            if o.ctr is not None:
                o.ctr.n += o.k
                o.tok = o.ctr.n
        for eng_name in ENGS:
            ops = [o for o in self.ops if o.eng == eng_name]
            if not ops:
                continue

            def body(e, ops=ops, eng_name=eng_name):
                hw = {}
                for o in ops:
                    for w in o.waits:
                        if w is None:
                            continue
                        if isinstance(w, Hard):
                            w = w.op
                            ctr, tok = w.ctr, w.tok
                        elif isinstance(w, Op):
                            if w.eng == eng_name:
                                assert w.ph is not self or w.idx < o.idx, \
                                    f"{self.name}: same-engine forward wait"
                                continue
                            ctr, tok = w.ctr, w.tok
                        else:
                            ctr, tok = w
                        if ctr is None or tok is None:
                            continue
                        if hw.get(id(ctr), -1) < tok:
                            e.wait_ge(ctr.sem, tok)
                            hw[id(ctr)] = tok
                    last = o.fn(e)
                    if o.ctr is not None:
                        assert last is not None, f"{self.name}: op fn returned None"
                        last.then_inc(o.ctr.sem, o.k)
            getattr(c.block, eng_name)(body)


def selfsync(c, engname, eng, instr):
    ctr = c.self_ctrs[engname]
    instr.then_inc(ctr.sem, 1)
    ctr.n += 1
    eng.wait_ge(ctr.sem, ctr.n)


def barrier(c, name):
    snap = [(ctr, ctr.n) for ctr in c.ctrs.values() if ctr.n > 0]
    for eng_name in ENGS:
        def body(e, snap=snap):
            for ctr, n in snap:
                e.wait_ge(ctr.sem, n)
        getattr(c.block, eng_name)(body)



def p0_consts(c):
    ph = Phase(c, "p0")
    o_iota = ph.op("gpsimd", lambda g: g.iota(
        c.iotav[:], [[1, 128]], base=0, channel_multiplier=-1,
        allow_small_or_imprecise_dtypes=True), ctr=c.gq)
    o_i2 = ph.op("gpsimd", lambda g: g.iota(
        c.ecap_col[:, 0:1], [[1, 1]], base=-1, channel_multiplier=c.CAP,
        allow_small_or_imprecise_dtypes=True), ctr=c.gq)
    o_i3 = ph.op("gpsimd", lambda g: g.iota(
        c.ecapmax[:, 0:E], [[c.CAP, E]], base=c.CAP - 1, channel_multiplier=0,
        allow_small_or_imprecise_dtypes=True), ctr=c.gq)
    o_i4 = ph.op("gpsimd", lambda g: g.iota(
        c.rsel_t[:], [[1, NH * 128]], base=0, channel_multiplier=-128,
        allow_small_or_imprecise_dtypes=True), ctr=c.gq)
    for w in ("Wq0", "Wk0", "Wv0", "Wb0", "Wa0"):
        ph.dma("gpsimd", lambda g, w=w: g.dma_start(out=c.W16[w], in_=c.ins[w]), ctr=c.ldC)
    ph.dma("gpsimd", lambda g: g.dma_start(out=c.CTX16, in_=c.ins["ctx"]), ctr=c.ldC)
    for ei in range(E):
        ph.dma("gpsimd", lambda g, ei=ei: g.dma_start(out=c.W2S[ei], in_=c.ins["W2"][ei]),
               ctr=c.ldC)
    for i, bn in enumerate(("bk0", "bv0", "bb0", "ba0")):
        ph.dma("gpsimd", lambda g, i=i, bn=bn: g.dma_start(
            out=c.bias_rows[0:1, i * 1024:(i + 1) * 1024], in_=c.ins[bn].unsqueeze(0)),
            ctr=c.ldC)
    ph.dma("gpsimd", lambda g: g.dma_start(out=c.br_row[:], in_=c.ins["br"].unsqueeze(0)),
           ctr=c.ldC)
    ph.dma("gpsimd", lambda g: g.dma_start(out=c.bo1_row[:], in_=c.ins["bo1"].unsqueeze(0)),
           ctr=c.ldC)
    o_bq = ph.dma("sync", lambda s: s.dma_start(
        out=c.bq_cols[:, 0:8], in_=c.ins["bq0"].rearrange("(a p) -> p a", p=128),
        allow_slow_non_contiguous=True), ctr=c.ldA)
    o_bk = ph.dma("sync", lambda s: s.dma_start(
        out=c.bq_cols[:, 8:16], in_=c.ins["bk0"].rearrange("(a p) -> p a", p=128),
        allow_slow_non_contiguous=True), ctr=c.ldA)
    o_sin = ph.dma("sync", lambda s: s.dma_start(
        out=c.sc_sm[0:1, 0:1], in_=c.ins["s_in"].unsqueeze(0)), ctr=c.ldA)
    o_sind = ph.dma("sync", lambda s: s.dma_start(
        out=c.sc_sm[0:1, 1:2], in_=c.ins["s_in_d"].unsqueeze(0)), ctr=c.ldA)
    ph.op("scalar", lambda s: s.mul(out=c.bq_cols[:, 0:16], in_=c.bq_cols[:, 0:16],
                                    mul=QSCALE), waits=[o_sind], ctr=c.sq)

    def mkconsts(v):
        v.tensor_single_scalar(out=c.ident[:], in_=c.iotav[:], scalar=0.0, op=ALU.is_equal)
        v.tensor_single_scalar(out=c.identf[:], in_=c.iotav[:], scalar=0.0, op=ALU.is_equal)
        v.tensor_single_scalar(out=c.uinc[:], in_=c.iotav[:], scalar=0.0, op=ALU.is_ge)
        v.tensor_single_scalar(out=c.uincf[:], in_=c.iotav[:], scalar=0.0, op=ALU.is_ge)
        v.tensor_single_scalar(out=c.ustrf[:], in_=c.iotav[:], scalar=0.0, op=ALU.is_gt)
        v.tensor_scalar(out=c.madd[:], in0=c.iotav[:], scalar1=0.0, scalar2=-1e9,
                        op0=ALU.is_lt, op1=ALU.mult)
        v.memset(c.ones_row[:], 1.0)
        v.memset(c.ones_rowf[:], 1.0)
        v.memset(c.ones_colb[:], 1.0)
        v.memset(c.ones8f[:], 1.0)
        v.memset(c.S_all[:], 0.0)
        v.memset(c.constcols[:, 0:1], LNEPS)
        v.memset(c.constcols[:, 1:2], -9.0 * LN2)
        v.memset(c.constcols[:, 2:3], 9.0 * LN2)
        v.tensor_scalar(out=c.rowsel[:], in0=c.rsel_t[:], scalar1=0.0, scalar2=None,
                        op0=ALU.is_ge)
        v.tensor_single_scalar(out=c.rsel_t[:], in_=c.rsel_t[:], scalar=128.0,
                               op=ALU.is_lt)
        v.tensor_tensor(out=c.rowsel[:], in0=c.rowsel[:], in1=c.rsel_t[:], op=ALU.mult)
        return v.memset(c.MASKS[:], 0.0)

    o_consts = ph.op("vector", mkconsts, waits=[o_iota, o_i4], ctr=c.vq)
    o_gsmm = ph.op("tensor", lambda t: t.matmul(
        c.pb[0][:, 0:2], c.ones_rowf[0:1, 0:128], c.sc_sm[0:1, 0:2],
        start=True, stop=True), waits=[o_sind, o_consts], ctr=c.tq)

    def gs_ev(v):
        v.tensor_scalar_mul(out=c.gs_cols[:, 0:1], in0=c.pb[0][:, 0:1], scalar1=1.0 / 6.0)
        v.reciprocal(out=c.gs_cols[:, 1:2], in_=c.pb[0][:, 0:1])
        v.tensor_scalar_mul(out=c.gs_cols[:, 2:3], in0=c.pb[0][:, 1:2], scalar1=1.0 / 6.0)
        return v.reciprocal(out=c.gs_cols[:, 3:4], in_=c.pb[0][:, 1:2])

    ph.op("vector", gs_ev, waits=[o_gsmm], ctr=c.vq)
    ph.emit()


def emit_layernorm(c, name, x_src, xn_dst, x1_store=None, tile_epilogue=None):
    with contextlib.ExitStack() as px:
        _ln_inner(c, name, x_src, xn_dst, tile_epilogue, px.enter_context)


def _ln_inner(c, name, x_src, xn_dst, tile_epilogue, ep):
    ph = Phase(c, name)
    ldp = [sctr(c, "lnx0"), sctr(c, "lnx1")]
    stp = [sctr(c, "lnst0"), sctr(c, "lnst1")]
    xa = ep(c.nc.sbuf_tensor(name + "_x", [128, 2 * D], F32))
    oa = ep(c.nc.sbuf_tensor(name + "_o", [128, 2 * D], FP16))
    xc = ep(c.nc.sbuf_tensor(name + "_xc", [128, 2 * D], F32))
    sqs = ep(c.nc.sbuf_tensor(name + "_sq", [128, D], F32))
    stt = ep(c.nc.sbuf_tensor(name + "_st", [128, 16], F32))
    v3tok, g1tok = {}, {}
    for ti in range(c.NT):
        b = ti % 2
        xs = xa[:, b * D:(b + 1) * D]
        os_ = oa[:, b * D:(b + 1) * D]
        xcs = xc[:, b * D:(b + 1) * D]
        ss = stt[:, b * 8:(b + 1) * 8]
        w = [v3tok[ti - 2]] if ti >= 2 else []
        o_ld = ph.dma("sync", lambda s, ti=ti, xs=xs: s.dma_start(
            out=xs, in_=x_src[ti * 128:(ti + 1) * 128, :]), waits=w, ctr=ldp[b])

        def v1(v, xs=xs, ss=ss):
            selfsync(c, "vector", v,
                     v.tensor_reduce(out=ss[:, 0:1], in_=xs, axis=AX.X, op=ALU.add))
            return v.tensor_scalar_mul(out=ss[:, 1:2], in0=ss[:, 0:1], scalar1=-1.0 / D)
        o_v1 = ph.op("vector", v1, waits=[o_ld], ctr=c.vq)

        def s1(s, xs=xs, xcs=xcs, ss=ss):
            s.activation(out=xcs, in_=xs, func=AF.Identity, bias=ss[:, 1:2])
            return s.activation(out=sqs[:], in_=xcs, func=AF.Square, accum_out=ss[:, 3:4])
        o_s1 = ph.op("scalar", s1, waits=[o_v1], ctr=c.sq)

        def v2(v, ss=ss):
            selfsync(c, "vector", v,
                     v.tensor_scalar(out=ss[:, 4:5], in0=ss[:, 3:4], scalar1=1.0 / D,
                                     scalar2=LNEPS, op0=ALU.mult, op1=ALU.add))
            return v.reciprocal(out=ss[:, 5:6], in_=ss[:, 4:5])
        o_v2 = ph.op("vector", v2, waits=[o_s1], ctr=c.vq)

        def s2(s, ss=ss):
            return s.activation(out=ss[:, 6:7], in_=ss[:, 5:6], func=AF.Sqrt)
        o_s2 = ph.op("scalar", s2, waits=[o_v2], ctr=c.sq)

        def v3(v, xcs=xcs, ss=ss, os_=os_):
            return v.tensor_scalar(out=os_, in0=xcs, scalar1=ss[:, 6:7],
                                   scalar2=None, op0=ALU.mult)
        w3 = [o_s2] + ([g1tok[ti - 2]] if ti >= 2 else [])
        v3tok[ti] = ph.op("vector", v3, waits=w3, ctr=c.vq)
        g1tok[ti] = ph.dma("gpsimd", lambda g, ti=ti, os_=os_: g.dma_start(
            out=xn_dst[ti * 128:(ti + 1) * 128, :], in_=os_), waits=[v3tok[ti]], ctr=stp[b])
        if tile_epilogue is not None:
            tile_epilogue(ph, ti, v2tok[ti], SimpleNamespace(xn=os_, x=xs, xcs=xcs))
    ph.emit()


def p2_proj(c):
    with contextlib.ExitStack() as px:
        _p2_inner(c, px.enter_context)


def _p2_inner(c, ep):
    ph = Phase(c, "p2")
    nc = c.nc
    WT = {w: ep(nc.sbuf_tensor("WT_" + w, [128, 8 * D], FP16))
          for w in ("Wq0", "Wk0", "Wv0", "Wb0", "Wa0")}
    WrS = ep(nc.sbuf_tensor("WrS", [128, 8 * E], FP16))
    wWT = sctr(c, "wWT")
    xntc = [sctr(c, "xnt0"), sctr(c, "xnt1")]
    evs = [sctr(c, f"evs{i}") for i in range(4)]
    ktmc = [sctr(c, "ktm0"), sctr(c, "ktm1")]
    vtmc = [sctr(c, "vtm0"), sctr(c, "vtm1")]
    vbsc = [sctr(c, "vbs0"), sctr(c, "vbs1")]

    o_w = None
    for w in ("Wq0", "Wk0", "Wv0", "Wb0", "Wa0"):
        for dc in range(8):
            o_w = ph.dma("sync", lambda s, w=w, dc=dc: s.dma_start_transpose(
                WT[w][:, dc * D:(dc + 1) * D], c.W16[w][0:D, dc * 128:(dc + 1) * 128]),
                ctr=wWT)
    wWr = sctr(c, "wWr")
    o_wr = None
    for dc in range(8):
        o_wr = ph.dma("gpsimd", lambda g, dc=dc: g.dma_start(
            out=WrS[:, dc * E:(dc + 1) * E], in_=c.ins["Wr"][dc * 128:(dc + 1) * 128, :]),
            ctr=wWr)
    wload = [o_w, o_wr]

    xnt = ep(nc.sbuf_tensor("p2_xnt", [128, 2 * 8 * 512], FP16))
    evfm = ep(nc.sbuf_tensor("p2_evfm", [128, 4 * 512], FP16))
    evktm = ep(nc.sbuf_tensor("p2_evktm", [128, 2 * 512], FP16))
    vsb = ep(nc.sbuf_tensor("p2_v", [128, 2 * D], FP16))
    vbsb = ep(nc.sbuf_tensor("p2_vb", [128, 2 * D], FP16))
    betas = ep(nc.sbuf_tensor("p2_beta", [128, 2 * D], F32))
    nlss = ep(nc.sbuf_tensor("p2_nls", [128, 2 * D], F32))
    e1s = ep(nc.sbuf_tensor("p2_e1", [128, 512], F32))
    a1s = ep(nc.sbuf_tensor("p2_a1", [128, 2 * 512], F32))
    red16 = ep(nc.sbuf_tensor("p2_red", [128, 16], F32))
    logt = ep(nc.sbuf_tensor("p2_logt", [128, 48], F32))

    last_pb = {}
    last_slot = {}
    vt_hist = {}
    aev_hist = {}
    last_xnt_use = {}

    for tg in range(c.NTG):
        gb = tg % 2
        xg = xnt[:, gb * 4096:(gb + 1) * 4096]
        w = [last_xnt_use.get(gb)]
        o_l = None
        for dc in range(8):
            o_l = ph.dma("sync", lambda s, tg=tg, dc=dc, xg=xg: s.dma_start_transpose(
                xg[:, dc * 512:(dc + 1) * 512],
                c.XN[tg * 512:(tg + 1) * 512, dc * 128:(dc + 1) * 128]),
                waits=w, ctr=xntc[gb])
        lw = [o_l] + (wload if tg == 0 else [])

        for wi, (wname, dst, colofs) in enumerate((("Wq0", c.QT, 0), ("Wk0", c.KT, 8))):
            for oc in range(8):
                bk = oc % 4
                bank = c.pb[bk]

                def mm(t, wname=wname, oc=oc, xg=xg, bank=bank):
                    r = None
                    for dc in range(8):
                        r = t.matmul(bank[:],
                                     WT[wname][:, dc * D + oc * 128:dc * D + (oc + 1) * 128],
                                     xg[:, dc * 512:(dc + 1) * 512],
                                     start=(dc == 0), stop=(dc == 7))
                    return r
                o_mm = ph.op("tensor", mm, waits=lw + [last_pb.get(bk)], ctr=c.tq)
                last_xnt_use[gb] = o_mm
                sl = (wi * 8 + oc) % 4
                eb = evfm[:, sl * 512:(sl + 1) * 512]

                def evac(s, bank=bank, eb=eb, colofs=colofs, oc=oc):
                    return s.activation(out=eb, in_=bank[:], func=AF.Identity,
                                        scale=QSCALE,
                                        bias=c.bq_cols[:, colofs + oc:colofs + oc + 1])
                o_ev = ph.op("scalar", evac, waits=[o_mm, last_slot.get(sl)], ctr=c.sq)
                last_pb[bk] = o_ev
                o_st = ph.dma("gpsimd", lambda g, dst=dst, oc=oc, tg=tg, eb=eb: g.dma_start(
                    out=dst[oc * 128:(oc + 1) * 128, tg * 512:(tg + 1) * 512], in_=eb),
                    waits=[o_ev], ctr=evs[sl])
                last_slot[sl] = o_st

        for tt in range(4):
            ti = tg * 4 + tt
            p = ti % 2
            o_vt_m2 = vt_hist.get(ti - 2)
            ev_ops = []
            for wi, wname in enumerate(("Wk0", "Wv0", "Wb0", "Wa0")):
                for oh in range(2):
                    bk = (wi * 2 + oh) % 4
                    bank = c.pb[bk]

                    def mmtm(t, wname=wname, oh=oh, tt=tt, xg=xg, bank=bank, wi=wi):
                        for dc in range(8):
                            t.matmul(bank[:],
                                     xg[:, dc * 512 + tt * 128:dc * 512 + (tt + 1) * 128],
                                     WT[wname][:, dc * D + oh * 512:dc * D + (oh + 1) * 512],
                                     start=(dc == 0), stop=False)
                        return t.matmul(
                            bank[:], c.ones_row[0:1, 0:128],
                            c.bias_rows[0:1, wi * 1024 + oh * 512:wi * 1024 + (oh + 1) * 512],
                            start=False, stop=True)
                    o_mm = ph.op("tensor", mmtm, waits=lw + [last_pb.get(bk)], ctr=c.tq)
                    last_xnt_use[gb] = o_mm
                    if wname == "Wk0":
                        eb = evktm[:, oh * 512:(oh + 1) * 512]

                        def evk(s, bank=bank, eb=eb):
                            return s.activation(out=eb, in_=bank[:], func=AF.Copy,
                                                scale=QSCALE)
                        o_ev = ph.op("scalar", evk,
                                     waits=[o_mm, last_slot.get(("ktm", oh))], ctr=c.sq)
                        o_st = ph.dma("gpsimd", lambda g, ti=ti, oh=oh, eb=eb: g.dma_start(
                            out=c.KTM[ti * 128:(ti + 1) * 128, oh * 512:(oh + 1) * 512],
                            in_=eb), waits=[o_ev], ctr=ktmc[oh])
                        last_slot[("ktm", oh)] = o_st
                        last_pb[bk] = o_ev
                    elif wname == "Wv0":
                        vb_half = vsb[:, p * D + oh * 512:p * D + (oh + 1) * 512]

                        def evv(s, bank=bank, vb_half=vb_half):
                            return s.activation(out=vb_half, in_=bank[:], func=AF.Copy)
                        o_ev = ph.op("scalar", evv,
                                     waits=[o_mm, o_vt_m2, last_slot.get(("vtm", p))],
                                     ctr=c.sq)
                        o_st = ph.dma("gpsimd",
                                      lambda g, ti=ti, oh=oh, vb_half=vb_half: g.dma_start(
                                          out=c.VTM[ti * 128:(ti + 1) * 128,
                                                    oh * 512:(oh + 1) * 512], in_=vb_half),
                                      waits=[o_ev], ctr=vtmc[p])
                        last_slot[("vtm", p)] = o_st
                        last_pb[bk] = o_ev
                        ev_ops.append(o_ev)
                    elif wname == "Wb0":
                        bslice = betas[:, p * D + oh * 512:p * D + (oh + 1) * 512]

                        def evb(s, bank=bank, bslice=bslice):
                            s.activation(out=e1s[:], in_=bank[:], func=AF.Exp)
                            return s.activation(out=bslice, in_=e1s[:], func=AF.Ln, bias=1.0)
                        o_ev = ph.op("scalar", evb, waits=[o_mm, o_vt_m2], ctr=c.sq)
                        last_pb[bk] = o_ev
                        ev_ops.append(o_ev)
                    else:
                        aslice = nlss[:, p * D + oh * 512:p * D + (oh + 1) * 512]
                        asl = a1s[:, oh * 512:(oh + 1) * 512]

                        def eva_v(v, bank=bank, asl=asl):
                            return v.tensor_scalar(out=asl, in0=bank[:], scalar1=-10.0,
                                                   scalar2=10.0, op0=ALU.max, op1=ALU.min)
                        o_cl = ph.op("vector", eva_v,
                                     waits=[o_mm, aev_hist.get(oh)], ctr=c.vq)
                        last_pb[bk] = o_cl

                        def eva_s(s, aslice=aslice, asl=asl):
                            s.activation(out=e1s[:], in_=asl, func=AF.Exp, scale=-1.0)
                            return s.activation(out=aslice, in_=e1s[:], func=AF.Ln, bias=1.0)
                        o_ev = ph.op("scalar", eva_s, waits=[o_cl, o_vt_m2], ctr=c.sq)
                        aev_hist[oh] = o_ev
                        ev_ops.append(o_ev)

            def mmr(t, tt=tt, xg=xg):
                for dc in range(8):
                    t.matmul(c.pb[4][:, 0:8],
                             xg[:, dc * 512 + tt * 128:dc * 512 + (tt + 1) * 128],
                             WrS[:, dc * E:(dc + 1) * E], start=(dc == 0), stop=False)
                return t.matmul(c.pb[4][:, 0:8], c.ones_row[0:1, 0:128],
                                c.br_row[0:1, 0:8], start=False, stop=True)
            o_mr = ph.op("tensor", mmr, waits=lw + [last_pb.get(4)], ctr=c.tq)
            last_xnt_use[gb] = o_mr

            def vtile(v, ti=ti, p=p):
                bet = betas[:, p * D:(p + 1) * D]
                nls = nlss[:, p * D:(p + 1) * D]
                vv = vsb[:, p * D:(p + 1) * D]
                vb = vbsb[:, p * D:(p + 1) * D]
                selfsync(c, "vector", v, v.tensor_reduce(
                    out=red16[:], in_=bet.rearrange("q (h d) -> q h d", d=HD),
                    axis=AX.X, op=ALU.add))
                v.tensor_copy(out=c.BMS[:, ti * NH:(ti + 1) * NH], in_=red16[:])
                selfsync(c, "vector", v, v.tensor_reduce(
                    out=red16[:], in_=nls.rearrange("q (h d) -> q h d", d=HD),
                    axis=AX.X, op=ALU.add))
                v.tensor_scalar_mul(out=c.MLA[:, ti * NH:(ti + 1) * NH], in0=red16[:],
                                    scalar1=-1.0 / HD)
                v.tensor_tensor(out=vb, in0=vv, in1=bet, op=ALU.mult)
                selfsync(c, "vector", v,
                         v.tensor_reduce(out=logt[:, 8:9], in_=c.pb[4][:, 0:8],
                                         axis=AX.X, op=ALU.max))
                eq = logt[:, 16:24]
                selfsync(c, "vector", v,
                         v.tensor_scalar(out=eq, in0=c.pb[4][:, 0:8],
                                         scalar1=logt[:, 8:9], scalar2=None,
                                         op0=ALU.is_ge))
                cs = logt[:, 24:32]
                t1 = logt[:, 32:40]
                sv = lambda i: selfsync(c, "vector", v, i)
                sv(v.tensor_copy(out=cs, in_=eq))
                sv(v.tensor_copy(out=t1[:, 1:8], in_=cs[:, 0:7]))
                sv(v.tensor_tensor(out=cs[:, 1:8], in0=cs[:, 1:8], in1=t1[:, 1:8], op=ALU.add))
                sv(v.tensor_copy(out=t1[:, 2:8], in_=cs[:, 0:6]))
                sv(v.tensor_tensor(out=cs[:, 2:8], in0=cs[:, 2:8], in1=t1[:, 2:8], op=ALU.add))
                sv(v.tensor_copy(out=t1[:, 4:8], in_=cs[:, 0:4]))
                sv(v.tensor_tensor(out=cs[:, 4:8], in0=cs[:, 4:8], in1=t1[:, 4:8], op=ALU.add))
                first = logt[:, 40:48]
                sv(v.tensor_single_scalar(out=first, in_=cs, scalar=1.0, op=ALU.is_equal))
                mf = c.MASKSF[:, ti * E:(ti + 1) * E]
                sv(v.tensor_tensor(out=mf, in0=eq, in1=first, op=ALU.mult))
                return v.tensor_copy(out=c.MASKS[:, ti * E:(ti + 1) * E], in_=mf)
            o_vt = ph.op("vector", vtile,
                         waits=ev_ops + [o_mr, last_slot.get(("vbs", p))], ctr=c.vq)
            vt_hist[ti] = o_vt
            last_pb[4] = o_vt
            o_vbst = ph.dma("gpsimd", lambda g, ti=ti, p=p: g.dma_start(
                out=c.VBS[ti * 128:(ti + 1) * 128, :], in_=vbsb[:, p * D:(p + 1) * D]),
                waits=[o_vt], ctr=vbsc[p])
            last_slot[("vbs", p)] = o_vbst

            def mcnt(t, ti=ti):
                return t.matmul(c.pb[5][0:8, ti:ti + 1], c.MASKS[:, ti * E:(ti + 1) * E],
                                c.ones_colb[:, 0:1], start=True, stop=True)
            o_cnt_last = ph.op("tensor", mcnt, waits=[o_vt], ctr=c.tq)
    ph.op("vector", lambda v: v.tensor_copy(out=c.cnt_sb[0:8, 0:c.NT],
                                            in_=c.pb[5][0:8, 0:c.NT]),
          waits=[o_cnt_last], ctr=c.vq)
    ph.emit()


def p3_deltanet(c):
    with contextlib.ExitStack() as px:
        _p3_inner(c, px.enter_context)


def _p3_inner(c, ep):
    ph = Phase(c, "p3")
    nc = c.nc
    NCH = c.NCH
    qts = ep(nc.sbuf_tensor("p3_q", [128, 2 * D], FP16))
    kts = ep(nc.sbuf_tensor("p3_k", [128, 2 * D], FP16))
    ktm = ep(nc.sbuf_tensor("p3_ktm", [128, 2 * D], FP16))
    vtm = ep(nc.sbuf_tensor("p3_v", [128, 2 * D], FP16))
    vbt = ep(nc.sbuf_tensor("p3_vb", [128, 2 * D], FP16))
    ytile = ep(nc.sbuf_tensor("p3_y", [128, 2 * D], FP16))
    m_tm = ep(nc.sbuf_tensor("p3_mtm", [128, 2 * NH], F32))
    mT = ep(nc.sbuf_tensor("p3_mT", [16, 2 * 128], F32))
    curveT = ep(nc.sbuf_tensor("p3_cT", [16, 2 * 128], F32))
    curve_tm = ep(nc.sbuf_tensor("p3_ctm", [128, 2 * NH], F32))
    dsb_a = ep(nc.sbuf_tensor("p3_dsb", [128, 2 * 128], F32))
    dec_a = ep(nc.sbuf_tensor("p3_dec", [128, 2 * 128], F32))
    adT_a = ep(nc.sbuf_tensor("p3_adT", [128, 2 * 128], FP16))
    ymem_a = ep(nc.sbuf_tensor("p3_ymem", [128, 2 * 64], F32))
    vbm_a = ep(nc.sbuf_tensor("p3_vbm", [128, 2 * 64], FP16))
    asum_c = ep(nc.sbuf_tensor("p3_asum", [128, 1], F32))
    S16 = ep(nc.sbuf_tensor("p3_S16", [128, c.B * NH * 32], FP16))

    ldq = [sctr(c, "d3l0"), sctr(c, "d3l1")]
    styc = [sctr(c, "d3y0"), sctr(c, "d3y1")]

    o_zero = ph.op("vector", lambda v: v.memset(S16[:], 0.0), ctr=c.vq)

    P = c.pb
    last = {}
    ev5_tok = {}
    st_tok = {}
    ld_tok = {}

    for b in range(c.B):
        for n in range(NCH):
            ci = b * NCH + n
            par = ci % 2
            t0 = ci * 128
            qg = qts[:, par * D:(par + 1) * D]
            kg = kts[:, par * D:(par + 1) * D]
            kmg = ktm[:, par * D:(par + 1) * D]
            vg = vtm[:, par * D:(par + 1) * D]
            vbg = vbt[:, par * D:(par + 1) * D]
            yg = ytile[:, par * D:(par + 1) * D]
            mtg = m_tm[:, par * NH:(par + 1) * NH]
            mTg = mT[:, par * 128:(par + 1) * 128]
            cTg = curveT[:, par * 128:(par + 1) * 128]
            ctg = curve_tm[:, par * NH:(par + 1) * NH]

            wl = [ld_tok.get(ci - 2)]
            o_l = None
            for name, dst, src in (("q", qg, c.QT), ("k", kg, c.KT)):
                o_l = ph.dma("sync", lambda s, dst=dst, src=src, t0=t0: s.dma_start(
                    out=dst[:].rearrange("p (a t) -> p a t", t=128),
                    in_=src[0:D, t0:t0 + 128].rearrange("(a p) t -> p a t", p=128)),
                    waits=wl, ctr=ldq[par])
            for dst, src in ((kmg, c.KTM), (vg, c.VTM), (vbg, c.VBS)):
                o_l = ph.dma("sync", lambda s, dst=dst, src=src, t0=t0: s.dma_start(
                    out=dst, in_=src[t0:t0 + 128, 0:D]), waits=wl, ctr=ldq[par])

            def oc1(t, ci=ci):
                return t.matmul(P[6][:, 0:16], c.uincf[:],
                                c.MLA[:, ci * NH:(ci + 1) * NH], start=True, stop=True)
            o_c1 = ph.op("tensor", oc1, waits=[o_zero, last.get("pb6")], ctr=c.tq)

            def oc2(v, mtg=mtg):
                return v.tensor_copy(out=mtg, in_=P[6][:, 0:16])
            o_c2 = ph.op("vector", oc2, waits=[o_c1, last.get(("mtm", par))], ctr=c.vq)
            last["pb6"] = o_c2

            def oc3(t, mtg=mtg):
                return t.transpose(P[7][0:16, 0:128], mtg, c.identf[:])
            o_c3 = ph.op("tensor", oc3, waits=[o_c2, last.get("pb7")], ctr=c.tq)

            def oc4(v, mTg=mTg):
                return v.tensor_copy(out=mTg, in_=P[7][0:16, 0:128])
            o_c4 = ph.op("vector", oc4, waits=[o_c3, last.get(("mT", par))], ctr=c.vq)
            last["pb7"] = o_c4

            def oc5(s, mTg=mTg, cTg=cTg):
                return s.activation(out=cTg, in_=mTg, func=AF.Exp)
            o_c5 = ph.op("scalar", oc5, waits=[o_c4, last.get(("cT", par))], ctr=c.sq)

            def oc6(s, mtg=mtg, ctg=ctg):
                return s.activation(out=ctg, in_=mtg, func=AF.Exp)
            o_c6 = ph.op("scalar", oc6, waits=[o_c2, last.get(("ctm", par))], ctr=c.sq)

            ev2_list = []
            for h in range(NH):
                p0 = (h % 2) * 64
                hp = h % 2
                dsb = dsb_a[:, hp * 128:(hp + 1) * 128]
                dec = dec_a[:, hp * 128:(hp + 1) * 128]
                adT = adT_a[:, hp * 128:(hp + 1) * 128]
                ymem = ymem_a[:, hp * 64:(hp + 1) * 64]
                vbm = vbm_a[:, hp * 64:(hp + 1) * 64]
                dc = h // 2
                qh = qg[p0:p0 + 64, dc * 128:(dc + 1) * 128]
                kh = kg[p0:p0 + 64, dc * 128:(dc + 1) * 128]
                cb = ((b * NH + h) // 2) * 64
                sS = c.S_all[p0:p0 + 64, cb:cb + 64]
                sS16 = S16[p0:p0 + 64, cb:cb + 64]

                o_m1 = ph.op("tensor", lambda t, kh=kh, qh=qh: t.matmul(
                    P[0][:, 0:128], kh, qh, start=True, stop=True),
                    waits=[o_l, last.get("pb0")], ctr=c.tq)
                o_m2 = ph.op("tensor", lambda t, mTg=mTg, h=h: t.matmul(
                    P[1][:, 0:128], c.rowsel[0:16, h * 128:(h + 1) * 128],
                    mTg[0:16, 0:128], start=True, stop=True),
                    waits=[o_c4, last.get("pb1")], ctr=c.tq)
                o_d1 = ph.op("vector", lambda v, mtg=mtg, h=h, dsb=dsb: v.scalar_tensor_tensor(
                    out=dsb[:], in0=P[1][:, 0:128], scalar=mtg[:, h:h + 1],
                    in1=c.madd[:], op0=ALU.subtract, op1=ALU.add),
                    waits=[o_m2, Hard(o_c2), last.get(("dsb", hp))], ctr=c.vq)
                last["pb1"] = o_d1
                o_d2 = ph.op("scalar", lambda s, dec=dec, dsb=dsb: s.activation(
                    out=dec[:], in_=dsb[:], func=AF.Exp),
                    waits=[o_d1, last.get(("dec", hp))], ctr=c.sq)
                last[("dsb", hp)] = o_d2
                o_d3 = ph.op("vector", lambda v, adT=adT, dec=dec: v.tensor_tensor(
                    out=adT[:], in0=P[0][:, 0:128], in1=dec[:], op=ALU.mult),
                    waits=[o_d2, o_m1, last.get(("adT", hp))], ctr=c.vq)
                last["pb0"] = o_d3
                last[("dec", hp)] = o_d3
                o_m3 = ph.op("tensor", lambda t, vbg=vbg, h=h, adT=adT: t.matmul(
                    P[2][:, 0:64], adT[:], vbg[:, h * 64:(h + 1) * 64],
                    start=True, stop=True), waits=[o_d3, last.get("pb2")], ctr=c.tq)
                last[("adT", hp)] = o_m3
                o_m4 = ph.op("tensor", lambda t, qh=qh, sS16=sS16: t.matmul(
                    P[5][:, 0:64], qh, sS16, start=True, stop=True),
                    waits=[o_l, ev5_tok.get((b, h), o_zero), last.get("pb5")], ctr=c.tq)
                o_d4 = ph.op("vector", lambda v, vg=vg, h=h, ci=ci, vbm=vbm: v.tensor_scalar(
                    out=vbm[:], in0=vg[:, h * 64:(h + 1) * 64],
                    scalar1=c.BMS[:, ci * NH + h:ci * NH + h + 1], scalar2=1.0 / HD,
                    op0=ALU.mult, op1=ALU.mult),
                    waits=[o_l, last.get(("vbm", hp))], ctr=c.vq)
                o_m5 = ph.op("tensor", lambda t, kmg=kmg, h=h, p0=p0, vbm=vbm: t.matmul(
                    P[3][p0:p0 + 64, 0:64], kmg[:, h * 64:(h + 1) * 64], vbm[:],
                    start=True, stop=True, tile_position=(0, p0)),
                    waits=[o_d4, last.get("pb3")], ctr=c.tq)
                last[("vbm", hp)] = o_m5
                o_m6 = ph.op("tensor", lambda t, cTg=cTg, h=h, p0=p0: t.matmul(
                    P[4][p0:p0 + 64, 0:1], c.rowsel[0:16, h * 128:h * 128 + 64],
                    cTg[0:16, 127:128], start=True, stop=True, tile_position=(0, p0)),
                    waits=[o_c5, last.get("pb4")], ctr=c.tq)
                o_e1 = ph.op("scalar", lambda s, ctg=ctg, h=h, ymem=ymem: s.activation(
                    out=ymem[:], in_=P[5][:, 0:64], func=AF.Identity,
                    scale=ctg[:, h:h + 1], bias=0.0),
                    waits=[o_m4, Hard(o_c6), last.get(("ymem", hp))], ctr=c.sq)
                last["pb5"] = o_e1
                o_e2 = ph.op("vector", lambda v, yg=yg, h=h, ymem=ymem: v.tensor_tensor(
                    out=yg[:, h * 64:(h + 1) * 64], in0=P[2][:, 0:64], in1=ymem[:],
                    op=ALU.add),
                    waits=[o_m3, o_e1, st_tok.get(ci - 2), last.get("yg")], ctr=c.vq)
                last["pb2"] = o_e2
                last[("ymem", hp)] = o_e2
                ev2_list.append(o_e2)
                o_e3 = ph.op("scalar", lambda s, p0=p0: s.copy(
                    out=asum_c[p0:p0 + 64, 0:1], in_=P[4][p0:p0 + 64, 0:1]),
                    waits=[o_m6, last.get("asumc")], ctr=c.sq)
                o_e4 = ph.op("vector", lambda v, sS=sS, p0=p0: v.scalar_tensor_tensor(
                    out=sS, in0=sS, scalar=asum_c[p0:p0 + 64, 0:1],
                    in1=P[3][p0:p0 + 64, 0:64],
                    op0=ALU.mult, op1=ALU.add), waits=[o_e3, o_m5], ctr=c.vq)
                last["pb3"] = o_e4
                last["pb4"] = o_e4
                last["asumc"] = o_e4
                o_e5 = ph.op("vector", lambda v, sS=sS, sS16=sS16: v.tensor_copy(
                    out=sS16, in_=sS), waits=[], ctr=c.vq)
                ev5_tok[(b, h)] = o_e5
            last["yg"] = ev2_list[-1]
            last[("mtm", par)] = ev2_list[-1]
            last[("mT", par)] = ev2_list[-1]
            last[("cT", par)] = ev2_list[-1]
            last[("ctm", par)] = ev2_list[-1]
            ld_tok[ci] = ev2_list[-1]
            st_tok[ci] = ph.dma("gpsimd", lambda g, t0=t0, yg=yg: g.dma_start(
                out=c.YATT[t0:t0 + 128, 0:D], in_=yg), waits=[ev2_list[-1]], ctr=styc[par])
    ph.emit()


def emit_quant_batch(c, ph, items, nb, gs6_col, rgs_col, qb, extra_waits=()):
    k = len(items)
    F = nb * 16
    K = k * nb
    sm = qb.sm
    amax = sm[:, 0 * K:1 * K]
    sc = sm[:, 1 * K:2 * K]
    cnt = sm[:, 2 * K:3 * K]
    q2 = sm[:, 3 * K:4 * K]
    rq = sm[:, 4 * K:5 * K]
    u2 = sm[:, 5 * K:6 * K]
    mm_ = sm[:, 6 * K:7 * K]
    dq = sm[:, 7 * K:8 * K]
    rdq = sm[:, 8 * K:9 * K]
    ind3_14 = qb.ind[:, 0:14 * K].rearrange("p (j n) -> p n j", n=K)
    ind3_16 = qb.ind[:, 0:16 * K].rearrange("p (j n) -> p n j", n=K)
    lastq = getattr(c, "_qlast", {})
    c._qlast = lastq

    o_am = []
    for i, (xin, _, w) in enumerate(items):
        def va(v, xin=xin, i=i):
            return v.tensor_reduce(out=amax[:, i * nb:(i + 1) * nb],
                                   in_=xin.rearrange("p (n j) -> p n j", j=16),
                                   axis=AX.X, op=ALU.max, apply_absolute_value=True)
        o_am.append(ph.op("vector", va,
                          waits=list(w) + list(extra_waits) +
                                ([lastq.get("sm")] if i == 0 else []), ctr=c.vq))

    def vsc(v):
        sv = lambda j: selfsync(c, "vector", v, j)
        sv(v.tensor_scalar(out=sc, in0=amax, scalar1=gs6_col, scalar2=448.0,
                           op0=ALU.mult, op1=ALU.min))
        for i, j in enumerate(range(-5, 9)):
            v.tensor_single_scalar(out=qb.ind[:, i * K:(i + 1) * K], in_=sc,
                                   scalar=float(2.0 ** j), op=ALU.is_ge)
        sv(v.memset(qb.ind[:, 14 * K:16 * K], 0.0))
        sv(v.tensor_reduce(out=cnt, in_=ind3_16, axis=AX.X, op=ALU.add))
        return v.tensor_copy(out=mm_, in_=cnt)
    o_sc = ph.op("vector", vsc, waits=o_am, ctr=c.vq)

    def sexp(s):
        s.activation(out=q2, in_=cnt, func=AF.Exp, scale=LN2, bias=c.constcols[:, 1:2])
        return s.activation(out=rq, in_=cnt, func=AF.Exp, scale=-LN2,
                            bias=c.constcols[:, 2:3])
    o_se = ph.op("scalar", sexp, waits=[o_sc, lastq.get("sm2")], ctr=c.sq)

    def vdq(v):
        sv = lambda j: selfsync(c, "vector", v, j)
        sv(v.tensor_tensor(out=u2, in0=sc, in1=rq, op=ALU.mult))
        for kk2 in range(16):
            v.tensor_single_scalar(out=qb.ind[:, kk2 * K:(kk2 + 1) * K], in_=u2,
                                   scalar=kk2 + 0.5, op=ALU.is_ge)
        sv(v.tensor_single_scalar(out=qb.ind[:, 15 * K:16 * K], in_=u2,
                                  scalar=15.5, op=ALU.is_ge))
        sv(v.tensor_reduce(out=mm_, in_=ind3_16, axis=AX.X, op=ALU.add))
        sv(v.tensor_tensor(out=dq, in0=mm_, in1=q2, op=ALU.mult))
        sv(v.tensor_scalar(out=dq, in0=dq, scalar1=rgs_col, scalar2=None,
                           op0=ALU.mult))
        sv(v.scalar_tensor_tensor(out=dq, in0=dq, scalar=0.0, in1=dq,
                                  op0=ALU.is_equal, op1=ALU.add))
        sv(v.reciprocal(out=rdq, in_=dq))
        return v.tensor_copy(out=qb.dq16[:, 0:K], in_=dq)
    o_vd = ph.op("vector", vdq, waits=[o_se], ctr=c.vq)
    lastq["sm"] = o_vd
    lastq["sm2"] = o_vd

    QW = ((0.25, 0.5), (0.75, 0.5), (1.25, 0.5), (1.75, 0.5),
          (2.5, 1.0), (3.5, 1.0), (5.0, 2.0))
    fb0 = qb.fb[:, 0 * F:1 * F]
    fb1 = qb.fb[:, 1 * F:2 * F]
    hsl = [qb.hb[:, i * F:(i + 1) * F] for i in range(8)]
    outs = []
    for i, (xin, out16, w) in enumerate(items):
        rdq_b = rdq[:, i * nb:(i + 1) * nb].unsqueeze(2).broadcast_to([128, nb, 16])
        dq16_b = qb.dq16[:, i * nb:(i + 1) * nb].unsqueeze(2).broadcast_to([128, nb, 16])
        o_abs = ph.op("scalar", lambda s, xin=xin: s.activation(
            out=fb0, in_=xin, func=AF.Abs),
            waits=list(w) + [o_vd, lastq.get("fb")], ctr=c.sq)

        def v1(v, rdq_b=rdq_b):
            sv = lambda j: selfsync(c, "vector", v, j)
            sv(v.tensor_tensor(out=fb1, in0=fb0, in1=rdq_b, op=ALU.mult))
            for j, (t, wt) in enumerate(QW):
                v.tensor_scalar(out=hsl[j], in0=fb1, scalar1=float(t),
                                scalar2=float(wt), op0=ALU.is_ge, op1=ALU.mult)
            sv(v.tensor_tensor(out=hsl[7], in0=hsl[5], in1=hsl[6], op=ALU.add))
            v.tensor_tensor(out=hsl[0], in0=hsl[0], in1=hsl[1], op=ALU.add)
            v.tensor_tensor(out=hsl[2], in0=hsl[2], in1=hsl[3], op=ALU.add)
            sv(v.tensor_tensor(out=hsl[4], in0=hsl[4], in1=hsl[7], op=ALU.add))
            sv(v.tensor_tensor(out=hsl[0], in0=hsl[0], in1=hsl[2], op=ALU.add))
            return v.tensor_tensor(out=hsl[0], in0=hsl[0], in1=hsl[4], op=ALU.add)
        o_v1 = ph.op("vector", v1, waits=[o_abs], ctr=c.vq)
        o_sg = ph.op("scalar", lambda s, xin=xin: s.activation(
            out=hsl[1], in_=xin, func=AF.Sign), waits=[o_v1], ctr=c.sq)

        def v3(v, dq16_b=dq16_b, out16=out16):
            selfsync(c, "vector", v, v.tensor_tensor(out=hsl[1], in0=hsl[1],
                                                     in1=dq16_b, op=ALU.mult))
            return v.tensor_tensor(out=out16, in0=hsl[0], in1=hsl[1], op=ALU.mult)
        o_v3 = ph.op("vector", v3, waits=[o_sg], ctr=c.vq)
        lastq["fb"] = o_v3
        outs.append(o_v3)
    return outs


def emit_quant(c, ph, xin, out16, nb, gs6_col, rgs_col, qb, waits):
    return emit_quant_batch(c, ph, [(xin, out16, list(waits))], nb,
                            gs6_col, rgs_col, qb)[-1]


def p4_moe(c):
    with contextlib.ExitStack() as px:
        _p4_inner(c, px.enter_context)


def _p4_inner(c, ep):
    nc = c.nc
    NT, CAP = c.NT, c.CAP
    TGE = min(512, CAP)
    NG = CAP // TGE
    NTT = TGE // 128
    P = c.pb

    ph = Phase(c, "p4a")
    base_sb = ep(nc.sbuf_tensor("p4_base", [64, 8], F32))
    baseT = ep(nc.sbuf_tensor("p4_baseT", [8, 64], F32))
    cntT = ep(nc.sbuf_tensor("p4_cntT", [64, 8], F32))
    b8 = ep(nc.sbuf_tensor("p4_b8", [8, 1], F32))
    diag8 = ep(nc.sbuf_tensor("p4_diag", [8, 8], F32))
    mp8 = ep(nc.sbuf_tensor("p4_mp", [128, 8], F32))
    dcol = ep(nc.sbuf_tensor("p4_dcol", [128, 4], F32))
    xnsb = ep(nc.sbuf_tensor("p4_xn", [128, 2 * D], FP16))
    zeroD = ep(nc.sbuf_tensor("p4_zero", [128, D], FP16))
    xldc = [sctr(c, "p4x0"), sctr(c, "p4x1")]
    sctc = sctr(c, "p4sct")

    o_t1 = ph.op("tensor", lambda t: t.transpose(P[7][0:c.NT, 0:8],
                                                 c.cnt_sb[0:8, 0:c.NT],
                                                 c.identf[0:8, 0:8]),
                 ctr=c.tq)
    o_v1 = ph.op("vector", lambda v: v.tensor_copy(out=cntT[0:c.NT, :],
                                                   in_=P[7][0:c.NT, 0:8]),
                 waits=[o_t1], ctr=c.vq)
    o_t2 = ph.op("tensor", lambda t: t.matmul(P[6][0:c.NT, 0:8],
                                              c.ustrf[0:c.NT, 0:c.NT], cntT[0:c.NT, :],
                                              start=True, stop=True),
                 waits=[o_v1], ctr=c.tq)
    o_v2 = ph.op("vector", lambda v: v.tensor_copy(out=base_sb[0:c.NT, :],
                                                   in_=P[6][0:c.NT, 0:8]),
                 waits=[o_t2], ctr=c.vq)
    o_t3 = ph.op("tensor", lambda t: t.transpose(P[7][0:8, 0:c.NT],
                                                 base_sb[0:c.NT, :],
                                                 c.identf[0:c.NT, 0:c.NT]),
                 waits=[o_v2, o_v1], ctr=c.tq)
    o_v3 = ph.op("vector", lambda v: v.tensor_copy(out=baseT[:, 0:c.NT],
                                                   in_=P[7][0:8, 0:c.NT]),
                 waits=[o_t3], ctr=c.vq)
    o_z = ph.op("vector", lambda v: v.memset(zeroD[:], 0.0), ctr=c.vq)
    zf = None
    for blk in range(E * CAP // 128):
        zf = ph.dma("gpsimd", lambda g, blk=blk: g.dma_start(
            out=c.XG[blk * 128:(blk + 1) * 128, :], in_=zeroD[:]),
            waits=[o_z], ctr=sctc)

    prev = {}
    for ti in range(NT):
        par = ti % 2

        def vdiag(v, ti=ti):
            sv = lambda i: selfsync(c, "vector", v, i)
            sv(v.tensor_tensor(out=b8[:], in0=baseT[:, ti:ti + 1], in1=c.ecap_col[:],
                               op=ALU.add))
            return v.tensor_scalar(out=diag8[:], in0=c.identf[0:8, 0:8], scalar1=b8[:],
                                   scalar2=None, op0=ALU.mult)
        o_vd = ph.op("vector", vdiag, waits=[o_v3, prev.get("posmm")], ctr=c.vq)

        def posmm(t, ti=ti):
            t.matmul(P[0][:, 0:8], c.uincf[:], c.MASKSF[:, ti * E:(ti + 1) * E],
                     start=True, stop=False)
            return t.matmul(P[0][:, 0:8], c.ones8f[:], diag8[:], start=False, stop=True)
        o_pm = ph.op("tensor", posmm, waits=[o_vd, prev.get("didx")], ctr=c.tq)
        prev["posmm"] = o_pm

        def vidx(v, ti=ti):
            sv = lambda i: selfsync(c, "vector", v, i)
            mf = c.MASKSF[:, ti * E:(ti + 1) * E]
            sv(v.tensor_tensor(out=mp8[:], in0=mf, in1=P[0][:, 0:8], op=ALU.mult))
            sv(v.tensor_reduce(out=dcol[:, 0:1], in_=mp8[:], axis=AX.X, op=ALU.add))
            sv(v.tensor_tensor(out=mp8[:], in0=mf, in1=c.ecapmax[:], op=ALU.mult))
            sv(v.tensor_reduce(out=dcol[:, 1:2], in_=mp8[:], axis=AX.X, op=ALU.add))
            sv(v.tensor_tensor(out=dcol[:, 2:3], in0=dcol[:, 0:1], in1=dcol[:, 1:2],
                               op=ALU.min))
            return v.tensor_copy(out=c.DIDX[:, ti:ti + 1], in_=dcol[:, 2:3])
        o_vi = ph.op("vector", vidx, waits=[o_pm], ctr=c.vq)
        prev["didx"] = o_vi
        if ti == 0 and hasattr(c, "dbg_capture"):
            def vcap(v):
                v.tensor_copy(out=c.dbg_capture[0:8, 0:8], in_=diag8[:])
                v.tensor_copy(out=c.dbg_capture[0:128, 8:16], in_=P[0][:, 0:8])
                v.tensor_copy(out=c.dbg_capture[0:128, 16:20], in_=dcol[:])
                return v.tensor_copy(out=c.dbg_capture[0:8, 20:21], in_=b8[:])
            o_cap = ph.op("vector", vcap, waits=[o_pm], ctr=c.vq)
            prev["didx"] = o_cap

        o_xl = ph.dma("sync", lambda s, ti=ti, par=par: s.dma_start(
            out=xnsb[:, par * D:(par + 1) * D], in_=c.XN[ti * 128:(ti + 1) * 128, :]),
            waits=[prev.get(("sct", par))], ctr=xldc[par])
        o_sc = ph.dma("gpsimd", lambda g, ti=ti, par=par: g.indirect_dma_start(
            out=c.XG, out_offset=bass.IndirectOffsetOnAxis(c.DIDX[:, ti:ti + 1], 0),
            in_=xnsb[:, par * D:(par + 1) * D], in_offset=None),
            waits=[o_xl, o_vi] + ([zf] if ti == 0 else []), ctr=sctc)
        prev[("sct", par)] = o_sc
    ph.emit()

    barrier(c, "b4b")
    ph = Phase(c, "p4b")
    W1sb = ep(nc.sbuf_tensor("p4_w1", [128, 8 * MH], FP16))
    b1row = ep(nc.sbuf_tensor("p4_b1", [1, MH], FP16))
    b2row = ep(nc.sbuf_tensor("p4_b2", [1, D], FP16))
    xgT = ep(nc.sbuf_tensor("p4_xgT", [128, 2 * 8 * TGE], FP16))
    hT = ep(nc.sbuf_tensor("p4_hT", [128, 32 * TGE], FP16))
    w2sl = ep(nc.sbuf_tensor("p4_w2", [128, 2 * D], FP16))
    ysb = ep(nc.sbuf_tensor("p4_y", [128, 2 * D], FP16))
    w1c = sctr(c, "p4w1")
    bxc = sctr(c, "p4bx")
    xgl = [sctr(c, "p4g0"), sctr(c, "p4g1")]
    w2c = [sctr(c, "p4w20"), sctr(c, "p4w21")]
    ygst = [sctr(c, "p4yg0"), sctr(c, "p4yg1")]
    NYB = 2 * NTT
    last = {}
    for e in range(E):
        o_w1 = None
        for dc in range(8):
            o_w1 = ph.dma("gpsimd", lambda g, e=e, dc=dc: g.dma_start(
                out=W1sb[:, dc * MH:(dc + 1) * MH],
                in_=c.ins["W1"][e, dc * 128:(dc + 1) * 128, :]),
                waits=[last.get("w1free")], ctr=w1c)
        o_b1 = ph.dma("gpsimd", lambda g, e=e: g.dma_start(
            out=b1row[:], in_=c.ins["b1"][e].unsqueeze(0)),
            waits=[last.get("b1free")], ctr=bxc)
        o_b2 = ph.dma("gpsimd", lambda g, e=e: g.dma_start(
            out=b2row[:], in_=c.ins["b2"][e].unsqueeze(0)),
            waits=[last.get("b2free")], ctr=bxc)
        for g_ in range(NG):
            gp = g_ % 2
            xgv = xgT[:, gp * 8 * TGE:(gp + 1) * 8 * TGE]
            hTv = hT[:]
            row0 = e * CAP + g_ * TGE
            o_xg = None
            for dc in range(8):
                o_xg = ph.dma("sync", lambda s, dc=dc, row0=row0, xgv=xgv:
                              s.dma_start_transpose(
                                  xgv[:, dc * TGE:(dc + 1) * TGE],
                                  c.XG[row0:row0 + TGE, dc * 128:(dc + 1) * 128]),
                              waits=[last.get(("xg", gp))], ctr=xgl[gp])
            hlast = None
            hm_last = None
            for hc in range(32):
                bk = hc % 4

                def mmh(t, hc=hc, xgv=xgv, bk=bk):
                    for dc in range(8):
                        t.matmul(P[bk][:, 0:TGE],
                                 W1sb[:, dc * MH + hc * 128:dc * MH + (hc + 1) * 128],
                                 xgv[:, dc * TGE:(dc + 1) * TGE],
                                 start=(dc == 0), stop=False)
                    return t.matmul(P[bk][:, 0:TGE],
                                    b1row[0:1, hc * 128:(hc + 1) * 128],
                                    c.ones_row[0:1, 0:TGE], start=False, stop=True)
                o_mh = ph.op("tensor", mmh,
                             waits=[o_xg, o_w1, o_b1, last.get(("pb", bk))], ctr=c.tq)
                hm_last = o_mh

                def evh(s, hc=hc, hTv=hTv, bk=bk):
                    return s.activation(out=hTv[:, hc * TGE:(hc + 1) * TGE],
                                        in_=P[bk][:, 0:TGE], func=AF.Gelu)
                o_eh = ph.op("scalar", evh, waits=[o_mh, last.get("hTb")], ctr=c.sq)
                last[("pb", bk)] = o_eh
                hlast = o_eh
            last[("xg", gp)] = hm_last
            o_w2 = {}
            for pre in range(2):
                o_w2[pre] = ph.dma("sync", lambda s, e=e, hc=pre, wp=pre % 2: s.dma_start(
                    out=w2sl[:, (hc % 2) * D:((hc % 2) + 1) * D],
                    in_=c.W2S[e, hc * 128:(hc + 1) * 128, :]),
                    waits=[last.get(("w2", pre % 2))], ctr=w2c[pre % 2])
            ylast = None
            for hc in range(32):
                wp = hc % 2

                def mmy(t, hc=hc, wp=wp, hTv=hTv):
                    r = None
                    for tt in range(NTT):
                        for oh in range(2):
                            bk = tt * 2 + oh
                            r = t.matmul(
                                P[bk][:, 0:512],
                                hTv[:, hc * TGE + tt * 128:hc * TGE + (tt + 1) * 128],
                                w2sl[:, wp * D + oh * 512:wp * D + (oh + 1) * 512],
                                start=(hc == 0), stop=False)
                    return r
                wts = [o_w2[hc], hlast]
                if hc == 0:
                    wts += [last.get(("pb", b2_)) for b2_ in range(NYB)]
                o_my = ph.op("tensor", mmy, waits=wts, ctr=c.tq)
                last[("w2", wp)] = o_my
                ylast = o_my
                if hc + 2 < 32:
                    o_w2[hc + 2] = ph.dma("sync", lambda s, e=e, hc=hc, wp=wp:
                                          s.dma_start(
                                              out=w2sl[:, wp * D:(wp + 1) * D],
                                              in_=c.W2S[e, (hc + 2) * 128:(hc + 3) * 128, :]),
                                          waits=[o_my], ctr=w2c[wp])
            last["hTb"] = ylast
            o_mb = None
            for tt in range(NTT):
                def mmb(t, tt=tt):
                    r = None
                    for oh in range(2):
                        bk = tt * 2 + oh
                        r = t.matmul(P[bk][:, 0:512], c.ones_row[0:1, 0:128],
                                     b2row[0:1, oh * 512:(oh + 1) * 512],
                                     start=False, stop=True)
                    return r
                o_mb = ph.op("tensor", mmb, waits=[ylast, o_b2], ctr=c.tq)
                yp = tt % 2

                def evy(s, tt=tt, yp=yp):
                    r = None
                    for oh in range(2):
                        bk = tt * 2 + oh
                        r = s.activation(
                            out=ysb[:, yp * D + oh * 512:yp * D + (oh + 1) * 512],
                            in_=P[bk][:, 0:512], func=AF.Copy)
                    return r
                o_ey = ph.op("scalar", evy, waits=[o_mb, last.get(("ysb", yp))], ctr=c.sq)
                for oh in range(2):
                    last[("pb", tt * 2 + oh)] = o_ey
                o_sy = ph.dma("gpsimd", lambda g, row0=row0, tt=tt, yp=yp: g.dma_start(
                    out=c.YG[row0 + tt * 128:row0 + (tt + 1) * 128, :],
                    in_=ysb[:, yp * D:(yp + 1) * D]), waits=[o_ey], ctr=ygst[yp])
                last[("ysb", yp)] = o_sy
        last["w1free"] = ylast
        last["b1free"] = hm_last
        last["b2free"] = o_mb
    ph.emit()


def p4c_gate(c):
    with contextlib.ExitStack() as px:
        _p4c_inner(c, px.enter_context)


def _p4c_inner(c, ep):
    nc = c.nc
    ph = Phase(c, "p4c")
    ymo = ep(nc.sbuf_tensor("p4c_ym", [128, 2 * D], FP16))
    yat = ep(nc.sbuf_tensor("p4c_ya", [128, 2 * D], FP16))
    xin = ep(nc.sbuf_tensor("p4c_x", [128, 2 * D], F32))
    gsil = ep(nc.sbuf_tensor("p4c_g", [128, D], F32))
    x1sb = ep(nc.sbuf_tensor("p4c_x1", [128, 2 * D], F32))
    xcs = ep(nc.sbuf_tensor("p4c_xc", [128, 2 * D], F32))
    sqs = ep(nc.sbuf_tensor("p4c_sq", [128, D], F32))
    stt = ep(nc.sbuf_tensor("p4c_st", [128, 16], F32))
    xn1f = ep(nc.sbuf_tensor("p4c_xn1", [128, D], F32))
    o16 = ep(nc.sbuf_tensor("p4c_o16", [128, 2 * D], FP16))
    xq16 = ep(nc.sbuf_tensor("p4c_xq", [128, 2 * D], FP16))
    qb = SimpleNamespace(
        sm=ep(nc.sbuf_tensor("p4c_qsm", [128, 10 * 64], F32)),
        ind=ep(nc.sbuf_tensor("p4c_qind", [128, 16 * 64], F32)),
        dq16=ep(nc.sbuf_tensor("p4c_qdq", [128, 64], FP16)),
        fb=ep(nc.sbuf_tensor("p4c_qfb", [128, 2 * D], F32)),
        hb=ep(nc.sbuf_tensor("p4c_qhb", [128, 8 * D], FP16)))
    gt = sctr(c, "p4gt")
    yac = [sctr(c, "p4ya0"), sctr(c, "p4ya1")]
    xxc = [sctr(c, "p4xx0"), sctr(c, "p4xx1")]
    x1c = [sctr(c, "p4x10"), sctr(c, "p4x11")]
    xnc = [sctr(c, "p4xn0"), sctr(c, "p4xn1")]
    xqc = [sctr(c, "p4xq0"), sctr(c, "p4xq1")]
    prev = {}
    for ti in range(c.NT):
        p = ti % 2
        ymv = ymo[:, p * D:(p + 1) * D]
        yav = yat[:, p * D:(p + 1) * D]
        xv = xin[:, p * D:(p + 1) * D]
        x1v = x1sb[:, p * D:(p + 1) * D]
        xcv = xcs[:, p * D:(p + 1) * D]
        ss = stt[:, p * 8:(p + 1) * 8]
        ov = o16[:, p * D:(p + 1) * D]
        xqv = xq16[:, p * D:(p + 1) * D]
        o_g = ph.dma("gpsimd", lambda g, ti=ti, ymv=ymv: g.indirect_dma_start(
            out=ymv, out_offset=None, in_=c.YG,
            in_offset=bass.IndirectOffsetOnAxis(c.DIDX[:, ti:ti + 1], 0)),
            waits=[prev.get(("ym", p))], ctr=gt)
        o_ya = ph.dma("sync", lambda s, ti=ti, yav=yav: s.dma_start(
            out=yav, in_=c.YATT[ti * 128:(ti + 1) * 128, :]),
            waits=[prev.get(("ya", p))], ctr=yac[p])
        o_xx = ph.dma("sync", lambda s, ti=ti, xv=xv: s.dma_start(
            out=xv, in_=c.ins["x"][ti * 128:(ti + 1) * 128, :]),
            waits=[prev.get(("xx", p))], ctr=xxc[p])
        o_si = ph.op("scalar", lambda s, ymv=ymv: s.activation(
            out=gsil[:], in_=ymv, func=AF.Silu),
            waits=[o_g, prev.get("gsil")], ctr=c.sq)

        def vmix(v, yav=yav, xv=xv, x1v=x1v):
            selfsync(c, "vector", v,
                     v.tensor_tensor(out=gsil[:], in0=gsil[:], in1=yav, op=ALU.mult))
            return v.tensor_tensor(out=x1v, in0=gsil[:], in1=xv, op=ALU.add)
        o_vm = ph.op("vector", vmix,
                     waits=[o_si, o_ya, o_xx, prev.get(("x1", p))], ctr=c.vq)
        prev[("ym", p)] = o_vm
        prev[("ya", p)] = o_vm
        prev["gsil"] = o_vm
        o_sx1 = ph.dma("sync", lambda s, ti=ti, x1v=x1v: s.dma_start(
            out=c.X1[ti * 128:(ti + 1) * 128, :], in_=x1v), waits=[o_vm], ctr=x1c[p])
        prev[("x1", p)] = o_sx1
        def v1(v, x1v=x1v, ss=ss):
            selfsync(c, "vector", v,
                     v.tensor_reduce(out=ss[:, 0:1], in_=x1v, axis=AX.X, op=ALU.add))
            return v.tensor_scalar_mul(out=ss[:, 1:2], in0=ss[:, 0:1], scalar1=-1.0 / D)
        o_v1 = ph.op("vector", v1, waits=[o_vm], ctr=c.vq)

        def s1(s, x1v=x1v, xcv=xcv, ss=ss):
            s.activation(out=xcv, in_=x1v, func=AF.Identity, bias=ss[:, 1:2])
            return s.activation(out=sqs[:], in_=xcv, func=AF.Square, accum_out=ss[:, 3:4])
        o_s1 = ph.op("scalar", s1, waits=[o_v1], ctr=c.sq)

        def v2(v, ss=ss):
            selfsync(c, "vector", v,
                     v.tensor_scalar(out=ss[:, 4:5], in0=ss[:, 3:4], scalar1=1.0 / D,
                                     scalar2=LNEPS, op0=ALU.mult, op1=ALU.add))
            return v.reciprocal(out=ss[:, 5:6], in_=ss[:, 4:5])
        o_v2 = ph.op("vector", v2, waits=[o_s1], ctr=c.vq)
        o_s2 = ph.op("scalar", lambda s, ss=ss: s.activation(
            out=ss[:, 6:7], in_=ss[:, 5:6], func=AF.Sqrt), waits=[o_v2], ctr=c.sq)

        def v3(v, xcv=xcv, ss=ss, ov=ov):
            v.tensor_scalar(out=xn1f[:], in0=xcv, scalar1=ss[:, 6:7], scalar2=None,
                            op0=ALU.mult)
            return v.tensor_copy(out=ov, in_=xn1f[:])
        o_v3 = ph.op("vector", v3,
                     waits=[o_s2, prev.get(("o16", p)), prev.get("xn1f")], ctr=c.vq)
        o_sxn = ph.dma("gpsimd", lambda g, ti=ti, ov=ov: g.dma_start(
            out=c.XN1[ti * 128:(ti + 1) * 128, :], in_=ov), waits=[o_v3], ctr=xnc[p])
        prev[("o16", p)] = o_sxn
        o_q = emit_quant(c, ph, xn1f[:], xqv, 64, c.gs_cols[:, 0:1], c.gs_cols[:, 1:2],
                         qb, waits=[o_v3, prev.get(("xq", p))])
        prev["xn1f"] = o_q
        o_sxq = ph.dma("gpsimd", lambda g, ti=ti, xqv=xqv: g.dma_start(
            out=c.XQ[ti * 128:(ti + 1) * 128, :], in_=xqv), waits=[o_q], ctr=xqc[p])
        prev[("xq", p)] = o_sxq
    ph.emit()


def p5_prep(c, ep):
    nc = c.nc
    ph = Phase(c, "p5")
    c.Wk1sb = ep(nc.sbuf_tensor("p5_wk1", [128, 6 * D], FP16))
    c.Wv1sb = ep(nc.sbuf_tensor("p5_wv1", [128, 6 * D], FP16))
    wlc = sctr(c, "p5wl")
    o_ct = None
    o_wl = None
    for w, dst in (("Wk1", c.Wk1sb), ("Wv1", c.Wv1sb)):
        for cdc in range(6):
            o_wl = ph.dma("gpsimd", lambda g, w=w, dst=dst, cdc=cdc: g.dma_start(
                out=dst[:, cdc * D:(cdc + 1) * D],
                in_=c.ins[w][cdc * 128:(cdc + 1) * 128, :]), ctr=wlc)
    c.p5toks = [o_wl]

    with contextlib.ExitStack() as p5x:
        e5 = p5x.enter_context
        Wq1sb = e5(nc.sbuf_tensor("p5_wq1", [128, 8 * D], FP16))
        xnt = e5(nc.sbuf_tensor("p5_xnt", [128, 2 * 8 * 512], FP16))
        evb = e5(nc.sbuf_tensor("p5_ev", [128, 4 * 512], FP16))
        o_wq = None
        for dc in range(8):
            o_wq = ph.dma("gpsimd", lambda g, dc=dc: g.dma_start(
                out=Wq1sb[:, dc * D:(dc + 1) * D],
                in_=c.ins["Wq1"][dc * 128:(dc + 1) * 128, :]), ctr=wlc)
        xl = [sctr(c, "p5x0"), sctr(c, "p5x1")]
        q1st = [sctr(c, f"p5q{i}") for i in range(4)]
        last = {}
        for tg in range(c.NTG):
            gb = tg % 2
            xg = xnt[:, gb * 4096:(gb + 1) * 4096]
            o_l = None
            for dc in range(8):
                o_l = ph.dma("sync", lambda s, tg=tg, dc=dc, xg=xg: s.dma_start_transpose(
                    xg[:, dc * 512:(dc + 1) * 512],
                    c.XN1[tg * 512:(tg + 1) * 512, dc * 128:(dc + 1) * 128]),
                    waits=[last.get(("xg", gb))], ctr=xl[gb])
            for oc in range(8):
                bk = oc % 4

                def mm(t, oc=oc, xg=xg, bk=bk):
                    r = None
                    for dc in range(8):
                        r = t.matmul(c.pb[bk],
                                     Wq1sb[:, dc * D + oc * 128:dc * D + (oc + 1) * 128],
                                     xg[:, dc * 512:(dc + 1) * 512],
                                     start=(dc == 0), stop=(dc == 7))
                    return r
                o_mm = ph.op("tensor", mm,
                             waits=[o_l, o_wq, last.get(("pb", bk))] +
                                   ([] if tg else [o_ct]), ctr=c.tq)
                last[("xg", gb)] = o_mm
                sl = oc % 4
                eb = evb[:, sl * 512:(sl + 1) * 512]
                o_ev = ph.op("scalar", lambda s, bk=bk, eb=eb: s.activation(
                    out=eb, in_=c.pb[bk], func=AF.Copy),
                    waits=[o_mm, last.get(("ev", sl))], ctr=c.sq)
                last[("pb", bk)] = o_ev
                o_st = ph.dma("gpsimd", lambda g, oc=oc, tg=tg, eb=eb: g.dma_start(
                    out=c.QT1[oc * 128:(oc + 1) * 128, tg * 512:(tg + 1) * 512], in_=eb),
                    waits=[o_ev], ctr=q1st[sl])
                last[("ev", sl)] = o_st
        ph.emit()


def p6_attn(c):
    with contextlib.ExitStack() as px:
        _p6_inner(c, px.enter_context)


def _p6_inner(c, ep):
    nc = c.nc
    NKT = CTXT // 128
    NKG = CTXT // 512
    ph = Phase(c, "p6")
    khT = ep(nc.sbuf_tensor("p6_khT", [128, 2 * CTXT], FP16))
    vh = ep(nc.sbuf_tensor("p6_vh", [128, 3 * CTXT], FP16))
    qh = ep(nc.sbuf_tensor("p6_qh", [128, 2 * 128], FP16))
    Psb = ep(nc.sbuf_tensor("p6_P", [128, 2 * CTXT], FP16))
    rmx = ep(nc.sbuf_tensor("p6_rm", [128, 2 * 16], F32))
    PT = ep(nc.sbuf_tensor("p6_PT", [128, NKT * 1024], FP16))
    otb = ep(nc.sbuf_tensor("p6_ot", [128, 2 * 512], FP16))
    ctb = ep(nc.sbuf_tensor("p6_ctb", [128, 6 * 512], FP16))
    qlc = [sctr(c, "p6q0"), sctr(c, "p6q1")]
    pstc = [sctr(c, "p6p0"), sctr(c, "p6p1")]
    ptlc = {"sync": sctr(c, "p6ts"), "scalar": sctr(c, "p6tc")}
    otc = [sctr(c, "p6o0"), sctr(c, "p6o1")]
    ctl = [sctr(c, "p6c0"), sctr(c, "p6c1")]
    last = {}
    PS = c.PS
    NSQ = c.T // 1024
    slot = [0]

    def next_quad():
        q = slot[0] % 2
        slot[0] += 1
        return q

    def soft_half(h, ti, half, o_ql):
        hp = h % 2
        par = ti % 2
        q = next_quad()
        qv = qh[:, par * 128:(par + 1) * 128]
        pv = Psb[:, par * CTXT:(par + 1) * CTXT]
        rm = rmx[:, par * 16:(par + 1) * 16]
        k0 = half * 2048
        b0 = q * 2048

        def mmq(t, qv=qv, hp=hp, k0=k0, b0=b0):
            r = None
            for kg in range(4):
                r = t.matmul(PS[:, b0 + kg * 512:b0 + (kg + 1) * 512], qv,
                             khT[:, hp * CTXT + k0 + kg * 512:
                                 hp * CTXT + k0 + (kg + 1) * 512],
                             start=True, stop=True)
            return r
        o_mq = ph.op("tensor", mmq,
                     waits=[o_ql, last.get(("khdone", hp))] + last.get(("quad", q), []),
                     ctr=c.tq)

        def rmax(v, rm=rm, half=half, b0=b0):
            sv = lambda i: selfsync(c, "vector", v, i)
            sv(v.tensor_reduce(out=rm[:, half:half + 1], in_=PS[:, b0:b0 + 2048],
                               axis=AX.X, op=ALU.max))
            return v.tensor_scalar_mul(out=rm[:, 2 + half:3 + half],
                                       in0=rm[:, half:half + 1], scalar1=-ASCALE)
        o_rm = ph.op("vector", rmax,
                     waits=[o_mq, last.get(("rmx", par))], ctr=c.vq)

        def expev(s, pv=pv, rm=rm, half=half, k0=k0, b0=b0):
            return s.activation(out=pv[:, k0:k0 + 2048], in_=PS[:, b0:b0 + 2048],
                                func=AF.Exp, scale=ASCALE,
                                bias=rm[:, 2 + half:3 + half],
                                accum_out=rm[:, 4 + half:5 + half])
        o_ex = ph.op("scalar", expev,
                     waits=[o_rm, last.get(("Psb", par))], ctr=c.sq)
        last[("quad", q)] = [o_ex]
        return o_mq, o_ex

    def soft_tile(h, ti):
        hp = h % 2
        par = ti % 2
        o_ql = ph.dma("scalar", lambda s, h=h, ti=ti, par=par: s.dma_start(
            out=qh[:, par * 128:(par + 1) * 128],
            in_=c.QT1[h * 128:(h + 1) * 128, ti * 128:(ti + 1) * 128]),
            waits=[last.get(("qh", par))], ctr=qlc[par])
        oA = soft_half(h, ti, 0, o_ql)
        oB = soft_half(h, ti, 1, o_ql)
        last[("qh", par)] = oB[0]
        pv = Psb[:, par * CTXT:(par + 1) * CTXT]
        rm = rmx[:, par * 16:(par + 1) * 16]

        def vfix(v, rm=rm):
            sv = lambda i: selfsync(c, "vector", v, i)
            sv(v.tensor_tensor(out=rm[:, 6:7], in0=rm[:, 0:1], in1=rm[:, 1:2],
                               op=ALU.max))
            sv(v.tensor_tensor(out=rm[:, 8:9], in0=rm[:, 0:1], in1=rm[:, 6:7],
                               op=ALU.subtract))
            sv(v.tensor_tensor(out=rm[:, 9:10], in0=rm[:, 1:2], in1=rm[:, 6:7],
                               op=ALU.subtract))
            return v.tensor_scalar_mul(out=rm[:, 8:10], in0=rm[:, 8:10],
                                       scalar1=ASCALE)
        o_vf = ph.op("vector", vfix, waits=[], ctr=c.vq)
        o_sf = ph.op("scalar", lambda s, rm=rm: s.activation(
            out=rm[:, 10:12], in_=rm[:, 8:10], func=AF.Exp),
            waits=[o_vf, oB[1]], ctr=c.sq)

        def vnorm(v, rm=rm, pv=pv):
            sv = lambda i: selfsync(c, "vector", v, i)
            sv(v.tensor_tensor(out=rm[:, 12:14], in0=rm[:, 4:6], in1=rm[:, 10:12],
                               op=ALU.mult))
            sv(v.tensor_tensor(out=rm[:, 12:13], in0=rm[:, 12:13],
                               in1=rm[:, 13:14], op=ALU.add))
            sv(v.reciprocal(out=rm[:, 14:15], in_=rm[:, 12:13]))
            sv(v.tensor_scalar(out=rm[:, 10:12], in0=rm[:, 10:12],
                               scalar1=rm[:, 14:15], scalar2=None, op0=ALU.mult))
            v.tensor_scalar(out=pv[:, 0:2048], in0=pv[:, 0:2048],
                            scalar1=rm[:, 10:11], scalar2=None, op0=ALU.mult)
            return v.tensor_scalar(out=pv[:, 2048:4096], in0=pv[:, 2048:4096],
                                   scalar1=rm[:, 11:12], scalar2=None, op0=ALU.mult)
        o_vn = ph.op("vector", vnorm, waits=[o_sf], ctr=c.vq)
        last[("rmx", par)] = o_vn
        o_ps = ph.dma("gpsimd", lambda g, ti=ti, hp=hp, pv=pv: g.dma_start(
            out=c.PSCR[hp * c.T + ti * 128:hp * c.T + (ti + 1) * 128, :], in_=pv),
            waits=[o_vn, last.get(("pscr", hp))], ctr=pstc[par])
        last[("Psb", par)] = o_ps
        return o_ps

    def pt_loads(h1, sq, extra):
        hp = h1 % 2
        for hf in range(2):
            toks = {}
            for j in range(16):
                kc = hf * 16 + j
                eng = "sync"
                toks[eng] = ph.dma(eng, lambda s, sq=sq, kc=kc, hp=hp:
                                   s.dma_start_transpose(
                                       PT[:, kc * 1024:(kc + 1) * 1024],
                                       c.PSCR[hp * c.T + sq * 1024:
                                              hp * c.T + (sq + 1) * 1024,
                                              kc * 128:(kc + 1) * 128]),
                                   waits=extra + last.get(("pth", hf), []),
                                   ctr=ptlc[eng])
            last[("ptld", hf)] = list(toks.values())

    def av_slot(h1, sq):
        hp = h1 % 2
        vs = h1 % 3
        q = next_quad()
        if sq == 0:
            pt_loads(h1, 0, [last.get(("plast", hp))])
        h0rd = []
        h1rd = []
        o_av = {}
        for qg in range(2):
            bk = q * 4 + qg

            def mm_h0(t, qg=qg, bk=bk, vs=vs):
                r = None
                for kc in range(16):
                    r = t.matmul(c.pb[bk],
                                 vh[:, vs * CTXT + kc * 128:vs * CTXT + (kc + 1) * 128],
                                 PT[:, kc * 1024 + qg * 512:kc * 1024 + (qg + 1) * 512],
                                 start=(kc == 0), stop=False)
                return r
            o1 = ph.op("tensor", mm_h0,
                       waits=last[("ptld", 0)] + [last.get(("vhdone", hp))] +
                             last.get(("quad", q), []), ctr=c.tq)

            def mm_h1(t, qg=qg, bk=bk, vs=vs):
                r = None
                for kc in range(16, 32):
                    r = t.matmul(c.pb[bk],
                                 vh[:, vs * CTXT + kc * 128:vs * CTXT + (kc + 1) * 128],
                                 PT[:, kc * 1024 + qg * 512:kc * 1024 + (qg + 1) * 512],
                                 start=False, stop=(kc == 31))
                return r
            o2 = ph.op("tensor", mm_h1, waits=last[("ptld", 1)], ctr=c.tq)
            h0rd.append(o1)
            h1rd.append(o2)
            o_av[qg] = o2
        evs = []
        for qg in range(2):
            bk = q * 4 + qg
            ob = otb[:, qg * 512:(qg + 1) * 512]
            o_oe = ph.op("scalar", lambda s, bk=bk, ob=ob: s.activation(
                out=ob, in_=c.pb[bk], func=AF.Copy),
                waits=[o_av[qg], last.get(("otb", qg))], ctr=c.sq)
            evs.append(o_oe)
            o_ost = ph.dma("gpsimd", lambda g, h1=h1, sq=sq, qg=qg, ob=ob: g.dma_start(
                out=c.OUTT[h1 * 128:(h1 + 1) * 128,
                           sq * 1024 + qg * 512:sq * 1024 + (qg + 1) * 512],
                in_=ob), waits=[o_oe], ctr=otc[qg])
            last[("otb", qg)] = o_ost
        last[("quad", q)] = evs
        last[("pth", 0)] = [h0rd[1]]
        last[("pth", 1)] = [h1rd[1]]
        if sq + 1 < NSQ:
            pt_loads(h1, sq + 1, [])
        return h1rd[1]

    def khvh_slot(h2, kg):
        hp = h2 % 2
        vs = h2 % 3
        q = next_quad()
        cp = 0
        ctv = ctb[:, 0:3072]
        o_ld = None
        for cdc in range(6):
            o_ld = ph.dma("sync", lambda s, kg=kg, cdc=cdc, ctv=ctv:
                          s.dma_start_transpose(
                              ctv[:, cdc * 512:(cdc + 1) * 512],
                              c.CTX16[kg * 512:(kg + 1) * 512,
                                      cdc * 128:(cdc + 1) * 128]),
                          waits=[last.get(("ctb", cp))], ctr=ctl[cp])
        bkh = q * 4 + 2
        bvh = q * 4 + 3

        def mm1(t, h2=h2, ctv=ctv, bkh=bkh):
            r = None
            for cdc in range(6):
                r = t.matmul(c.pb[bkh],
                             c.Wk1sb[:, cdc * D + h2 * 128:cdc * D + (h2 + 1) * 128],
                             ctv[:, cdc * 512:(cdc + 1) * 512],
                             start=(cdc == 0), stop=(cdc == 5))
            return r
        o_m = ph.op("tensor", mm1,
                    waits=[o_ld] + last.get(("quad", q), []), ctr=c.tq)
        o_e = ph.op("scalar", lambda s, kg=kg, hp=hp, bkh=bkh: s.activation(
            out=khT[:, hp * CTXT + kg * 512:hp * CTXT + (kg + 1) * 512],
            in_=c.pb[bkh], func=AF.Copy), waits=[o_m], ctr=c.sq)
        o_vh = None
        o_prev = None
        for kt4 in range(4):
            kt = kg * 4 + kt4

            def mm2(t, h2=h2, kt4=kt4, ctv=ctv, bvh=bvh):
                r = None
                for cdc in range(6):
                    r = t.matmul(c.pb[bvh][:, 0:128],
                                 ctv[:, cdc * 512 + kt4 * 128:cdc * 512 + (kt4 + 1) * 128],
                                 c.Wv1sb[:, cdc * D + h2 * 128:cdc * D + (h2 + 1) * 128],
                                 start=(cdc == 0), stop=(cdc == 5))
                return r
            o_m2 = ph.op("tensor", mm2,
                         waits=([o_ld] + last.get(("quad", q), [])) if kt4 == 0
                               else [o_prev], ctr=c.tq)
            o_e2 = ph.op("vector", lambda v, kt=kt, vs=vs, bvh=bvh: v.tensor_copy(
                out=vh[:, vs * CTXT + kt * 128:vs * CTXT + (kt + 1) * 128],
                in_=c.pb[bvh][:, 0:128]), waits=[o_m2], ctr=c.vq)
            o_prev = o_e2
            o_vh = o_e2
        last[("ctb", cp)] = o_vh
        last[("quad", q)] = [o_e, o_vh]
        if kg == NKG - 1:
            last[("khdone", hp)] = o_e
            last[("vhdone", hp)] = o_vh

    for kg in range(NKG):
        khvh_slot(0, kg)
    SPT = c.NT // NSQ
    for h in range(L1H + 1):
        av_done = None
        for ti in range(c.NT):
            if h < L1H:
                o_ps = soft_tile(h, ti)
                if ti == c.NT - 1:
                    last[("plast", h % 2)] = o_ps
            if h > 0 and ti % SPT == SPT - 1:
                av_done = av_slot(h - 1, ti // SPT)
            KVS = max(1, c.NT // NKG)
            if h + 1 < L1H and ti % KVS == KVS // 2 and ti // KVS < NKG:
                khvh_slot(h + 1, ti // KVS)
        if h > 0:
            last[("pscr", (h - 1) % 2)] = av_done
    ph.emit()


def p65_ffn(c):
    with contextlib.ExitStack() as px:
        _p65a_inner(c, px.enter_context)
    barrier(c, "b65b")
    with contextlib.ExitStack() as px:
        _p65b_inner(c, px.enter_context)
    barrier(c, "b65c")
    with contextlib.ExitStack() as px:
        _p65c_inner(c, px.enter_context)


def _p65a_inner(c, ep):
    nc = c.nc
    T = c.T
    ph = Phase(c, "p65a")
    wsl = ep(nc.sbuf_tensor("p65_wsl", [128, 2 * MH], F32))
    gmx = ep(nc.sbuf_tensor("p65_gmx", [128, 4], F32))
    grow = ep(nc.sbuf_tensor("p65_grow", [1, 8], F32))
    wq16 = ep(nc.sbuf_tensor("p65_wq16", [128, 2 * 1024], FP16))
    qb = SimpleNamespace(
        sm=ep(nc.sbuf_tensor("p65_qsm", [128, 10 * 64], F32)),
        ind=ep(nc.sbuf_tensor("p65_qind", [128, 16 * 64], F32)),
        dq16=ep(nc.sbuf_tensor("p65_qdq", [128, 64], FP16)),
        fb=ep(nc.sbuf_tensor("p65_qfb", [128, 2 * 1024], F32)),
        hb=ep(nc.sbuf_tensor("p65_qhb", [128, 8 * 1024], FP16)))
    wlc = [sctr(c, "p65w0"), sctr(c, "p65w1")]
    wst = [sctr(c, "p65s0"), sctr(c, "p65s1")]
    WCFG = (("GW", c.GQ, FH, D, 4), ("UW", c.UQ, FH, D, 6), ("DW", c.DQ, D, FH, 8))
    prevw16 = {}
    qci = 0
    o_wdone = None
    for wname, wdst, R, Ccol, gslot in WCFG:
        nsl = R // 128
        o_red_last = None
        prevld = {0: o_wdone, 1: o_wdone}
        for i in range(nsl):
            b = i % 2
            o_l = ph.dma("sync", lambda s, wname=wname, i=i, b=b, Ccol=Ccol: s.dma_start(
                out=wsl[:, b * MH:b * MH + Ccol],
                in_=c.ins[wname][i * 128:(i + 1) * 128, :]),
                waits=[prevld.get(b)], ctr=wlc[b])

            def vred(v, i=i, b=b, Ccol=Ccol):
                sv = lambda j: selfsync(c, "vector", v, j)
                sv(v.tensor_reduce(out=gmx[:, 1:2], in_=wsl[:, b * MH:b * MH + Ccol],
                                   axis=AX.X, op=ALU.max, apply_absolute_value=True))
                if i == 0:
                    return v.tensor_copy(out=gmx[:, 0:1], in_=gmx[:, 1:2])
                return v.tensor_tensor(out=gmx[:, 0:1], in0=gmx[:, 0:1],
                                       in1=gmx[:, 1:2], op=ALU.max)
            o_r = ph.op("vector", vred, waits=[o_l], ctr=c.vq)
            prevld[b] = o_r
            o_red_last = o_r
        o_tp = ph.op("tensor", lambda t: t.transpose(
            c.PS[0:1, 0:128], gmx[:, 0:1], c.identf[:]),
            waits=[o_red_last, last_ps(c, ph)], ctr=c.tq)

        def vg(v):
            sv = lambda j: selfsync(c, "vector", v, j)
            sv(v.tensor_reduce(out=grow[0:1, 0:1], in_=c.PS[0:1, 0:128], axis=AX.X,
                               op=ALU.max))
            sv(v.reciprocal(out=grow[0:1, 1:2], in_=grow[0:1, 0:1]))
            sv(v.tensor_scalar_mul(out=grow[0:1, 2:3], in0=grow[0:1, 1:2],
                                   scalar1=448.0))
            return v.tensor_scalar_mul(out=grow[0:1, 3:4], in0=grow[0:1, 0:1],
                                       scalar1=1.0 / 2688.0)
        o_vg = ph.op("vector", vg, waits=[o_tp], ctr=c.vq)
        o_bc = ph.op("tensor", lambda t: t.matmul(
            c.PS[:, 0:2], c.ones_rowf[0:1, 0:128], grow[0:1, 2:4],
            start=True, stop=True), waits=[o_vg], ctr=c.tq)
        o_gc = ph.op("vector", lambda v, gslot=gslot: v.tensor_copy(
            out=c.gs_cols[:, gslot:gslot + 2], in_=c.PS[:, 0:2]),
            waits=[o_bc], ctr=c.vq)
        set_last_ps(c, o_gc)
        prevld = {0: o_gc, 1: o_gc}
        for i in range(nsl):
            b = i % 2
            o_l = ph.dma("sync", lambda s, wname=wname, i=i, b=b, Ccol=Ccol: s.dma_start(
                out=wsl[:, b * MH:b * MH + Ccol],
                in_=c.ins[wname][i * 128:(i + 1) * 128, :]),
                waits=[prevld.get(b)], ctr=wlc[b])
            o_qlast = None
            for qc in range(Ccol // 1024):
                sl16 = qci % 2
                qci += 1
                o_q = emit_quant(c, ph,
                                 wsl[:, b * MH + qc * 1024:b * MH + (qc + 1) * 1024],
                                 wq16[:, sl16 * 1024:(sl16 + 1) * 1024],
                                 64, c.gs_cols[:, gslot:gslot + 1],
                                 c.gs_cols[:, gslot + 1:gslot + 2], qb,
                                 waits=[o_l, o_gc, prevw16.get(sl16)])
                o_s = ph.dma("gpsimd", lambda g, wdst=wdst, i=i, qc=qc, sl16=sl16:
                             g.dma_start(out=wdst[i * 128:(i + 1) * 128,
                                                  qc * 1024:(qc + 1) * 1024],
                                         in_=wq16[:, sl16 * 1024:(sl16 + 1) * 1024]),
                             waits=[o_q], ctr=wst[sl16])
                prevw16[sl16] = o_s
                o_qlast = o_q
            prevld[b] = o_qlast
        o_wdone = o_qlast
    ph.emit()


def _p65b_inner(c, ep):
    nc = c.nc
    T = c.T
    ph = Phase(c, "p65b")
    TH = T // 4 if T >= 4096 else T // 2
    xqT = ep(nc.sbuf_tensor("p65_xqT", [128, 8 * (T // 2)], FP16))
    gqt = ep(nc.sbuf_tensor("p65_gqt", [128, 2 * 8 * 512], FP16))
    uqt = ep(nc.sbuf_tensor("p65_uqt", [128, 2 * 8 * 512], FP16))
    gsl = ep(nc.sbuf_tensor("p65_gsl", [128, 512], F32))
    hfl = ep(nc.sbuf_tensor("p65_hf", [128, 8 * 512], F32))
    hq16 = ep(nc.sbuf_tensor("p65_hq", [128, 8 * 512], FP16))
    qb2 = SimpleNamespace(
        sm=ep(nc.sbuf_tensor("p65_qsm2", [128, 10 * 256], F32)),
        ind=ep(nc.sbuf_tensor("p65_qind2", [128, 16 * 256], FP16)),
        dq16=ep(nc.sbuf_tensor("p65_qdq2", [128, 256], FP16)),
        fb=ep(nc.sbuf_tensor("p65_qfb2", [128, 2 * 512], F32)),
        hb=ep(nc.sbuf_tensor("p65_qhb2", [128, 8 * 512], FP16)))
    xql = sctr(c, "p65x")
    gul = [sctr(c, "p65g0"), sctr(c, "p65g1")]
    hst = [sctr(c, "p65h0"), sctr(c, "p65h1")]
    last = {}
    for half in range(T // TH):
        t0 = half * TH
        o_x = None
        for dc in range(8):
            o_x = ph.dma("sync", lambda s, dc=dc, t0=t0: s.dma_start_transpose(
                xqT[:, dc * TH:(dc + 1) * TH],
                c.XQ[t0:t0 + TH, dc * 128:(dc + 1) * 128]),
                waits=[last.get("xqT")], ctr=xql)
        for hg in range(8):
            hp = hg % 2
            o_g = None
            for dc in range(8):
                o_g = ph.dma("sync", lambda s, dc=dc, hg=hg, hp=hp: s.dma_start_transpose(
                    gqt[:, hp * 4096 + dc * 512:hp * 4096 + (dc + 1) * 512],
                    c.GQ[hg * 512:(hg + 1) * 512, dc * 128:(dc + 1) * 128]),
                    waits=[last.get(("gq", hp))], ctr=gul[hp])
                o_g = ph.dma("sync", lambda s, dc=dc, hg=hg, hp=hp: s.dma_start_transpose(
                    uqt[:, hp * 4096 + dc * 512:hp * 4096 + (dc + 1) * 512],
                    c.UQ[hg * 512:(hg + 1) * 512, dc * 128:(dc + 1) * 128]),
                    waits=[last.get(("gq", hp))], ctr=gul[hp])
            o_mmlast = None
            ntt = TH // 128
            _qb = int(_os.environ.get('QBATCH', '8'))
            for sb0 in range(0, ntt, _qb):
                kk = min(_qb, ntt - sb0)
                items = []
                for j in range(kk):
                    tt = sb0 + j
                    gbk = tt % 2
                    ubk = 2 + tt % 2

                    def mmg(t, tt=tt, hp=hp, gbk=gbk, ubk=ubk):
                        for dc in range(8):
                            t.matmul(c.pb[gbk],
                                     xqT[:, dc * TH + tt * 128:dc * TH + (tt + 1) * 128],
                                     gqt[:, hp * 4096 + dc * 512:hp * 4096 + (dc + 1) * 512],
                                     start=(dc == 0), stop=(dc == 7))
                        r = None
                        for dc in range(8):
                            r = t.matmul(c.pb[ubk],
                                         xqT[:, dc * TH + tt * 128:dc * TH + (tt + 1) * 128],
                                         uqt[:, hp * 4096 + dc * 512:hp * 4096 + (dc + 1) * 512],
                                         start=(dc == 0), stop=(dc == 7))
                        return r
                    o_mm = ph.op("tensor", mmg,
                                 waits=[o_x, o_g, last.get(("pb", gbk)),
                                        last.get(("pb", ubk))], ctr=c.tq)
                    o_mmlast = o_mm
                    o_sil = ph.op("scalar", lambda s, gbk=gbk: s.activation(
                        out=gsl[:], in_=c.pb[gbk], func=AF.Silu),
                        waits=[o_mm, last.get("gsl")], ctr=c.sq)

                    def vh_(v, ubk=ubk, j=j):
                        return v.tensor_tensor(out=hfl[:, j * 512:(j + 1) * 512],
                                               in0=gsl[:], in1=c.pb[ubk], op=ALU.mult)
                    o_vh = ph.op("vector", vh_,
                                 waits=[o_sil, last.get(("hfl", j))], ctr=c.vq)
                    last[("pb", gbk)] = o_vh
                    last[("pb", ubk)] = o_vh
                    last["gsl"] = o_vh
                    items.append((hfl[:, j * 512:(j + 1) * 512],
                                  hq16[:, j * 512:(j + 1) * 512],
                                  [o_vh, last.get(("hq", j))]))
                qouts = emit_quant_batch(c, ph, items, 32,
                                         c.gs_cols[:, 2:3], c.gs_cols[:, 3:4], qb2)
                for j in range(kk):
                    tt = sb0 + j
                    ti_glob = (t0 // 128) + tt
                    hb_ = tt % 2
                    o_hs = ph.dma("gpsimd", lambda g, ti_glob=ti_glob, hg=hg, j=j:
                                  g.dma_start(out=c.HQ[ti_glob * 128:(ti_glob + 1) * 128,
                                                       hg * 512:(hg + 1) * 512],
                                              in_=hq16[:, j * 512:(j + 1) * 512]),
                                  waits=[qouts[j]], ctr=hst[j % 2])
                    last[("hfl", j)] = qouts[j]
                    last[("hq", j)] = o_hs
            last[("gq", hp)] = o_mmlast
        last["xqT"] = o_mmlast
    ph.emit()


def _p65c_inner(c, ep):
    nc = c.nc
    ph = Phase(c, "p65c")
    dqT = ep(nc.sbuf_tensor("p65_dqT", [128, 32 * 1024], FP16))
    hqT = ep(nc.sbuf_tensor("p65_hqT", [128, 2 * 32 * 512], FP16))
    f1b = ep(nc.sbuf_tensor("p65_f1", [128, 2 * D], FP16))
    dql = sctr(c, "p65d")
    hql = [sctr(c, "p65hq0"), sctr(c, "p65hq1")]
    f1st = [sctr(c, "p65f0"), sctr(c, "p65f1")]
    o_d = None
    for hc in range(32):
        o_d = ph.dma("sync", lambda s, hc=hc: s.dma_start_transpose(
            dqT[:, hc * 1024:(hc + 1) * 1024],
            c.DQ[0:1024, hc * 128:(hc + 1) * 128]), ctr=dql)
    last = {}
    for tg in range(c.NTG):
        gp = tg % 2
        hv = hqT[:, gp * 16384:(gp + 1) * 16384]
        o_h = None
        for hc in range(32):
            o_h = ph.dma("sync", lambda s, tg=tg, hc=hc, hv=hv: s.dma_start_transpose(
                hv[:, hc * 512:(hc + 1) * 512],
                c.HQ[tg * 512:(tg + 1) * 512, hc * 128:(hc + 1) * 128]),
                waits=[last.get(("hqT", gp))], ctr=hql[gp])
        o_mml = None
        for tt in range(4):
            for og in range(2):
                bk = tt % 2 * 2 + og

                def mmd(t, tt=tt, og=og, bk=bk, hv=hv):
                    r = None
                    for hc in range(32):
                        r = t.matmul(c.pb[bk],
                                     hv[:, hc * 512 + tt * 128:hc * 512 + (tt + 1) * 128],
                                     dqT[:, hc * 1024 + og * 512:hc * 1024 + (og + 1) * 512],
                                     start=(hc == 0), stop=(hc == 31))
                    return r
                o_mm = ph.op("tensor", mmd,
                             waits=[o_h, o_d, last.get(("pb", bk))], ctr=c.tq)
                o_mml = o_mm
                fp = tt % 2
                fb = f1b[:, fp * D + og * 512:fp * D + (og + 1) * 512]
                o_fe = ph.op("scalar", lambda s, bk=bk, fb=fb: s.activation(
                    out=fb, in_=c.pb[bk], func=AF.Copy),
                    waits=[o_mm, last.get(("f1", fp, og))], ctr=c.sq)
                last[("pb", bk)] = o_fe
                ti = tg * 4 + tt
                o_fs = ph.dma("gpsimd", lambda g, ti=ti, og=og, fb=fb: g.dma_start(
                    out=c.F1[ti * 128:(ti + 1) * 128, og * 512:(og + 1) * 512], in_=fb),
                    waits=[o_fe], ctr=f1st[fp])
                last[("f1", fp, og)] = o_fs
        last[("hqT", gp)] = o_mml
    ph.emit()


def last_ps(c, ph):
    return getattr(c, "_ps_last", None)


def set_last_ps(c, op):
    c._ps_last = op


def p7_final(c):
    with contextlib.ExitStack() as px:
        _p7_inner(c, px.enter_context)


def _p7_inner(c, ep):
    nc = c.nc
    ph = Phase(c, "p7")
    Wo1sb = ep(nc.sbuf_tensor("p7_wo", [128, 8 * D], FP16))
    otl = ep(nc.sbuf_tensor("p7_ot", [128, 2 * 8 * 128], FP16))
    f1l = ep(nc.sbuf_tensor("p7_f1", [128, 2 * D], FP16))
    x1l = ep(nc.sbuf_tensor("p7_x1", [128, 2 * D], F32))
    g1 = ep(nc.sbuf_tensor("p7_g1", [128, D], F32))
    tsb = ep(nc.sbuf_tensor("p7_t", [128, D], F32))
    osb = ep(nc.sbuf_tensor("p7_o", [128, 2 * D], F32))
    wol = sctr(c, "p7w")
    otc = [sctr(c, "p7o0"), sctr(c, "p7o1")]
    f1c = [sctr(c, "p7f0"), sctr(c, "p7f1")]
    x1c = [sctr(c, "p7x0"), sctr(c, "p7x1")]
    ostc = [sctr(c, "p7s0"), sctr(c, "p7s1")]
    o_w = None
    for dc in range(8):
        o_w = ph.dma("gpsimd", lambda g, dc=dc: g.dma_start(
            out=Wo1sb[:, dc * D:(dc + 1) * D],
            in_=c.ins["Wo1"][dc * 128:(dc + 1) * 128, :]), ctr=wol)
    last = {}
    for ti in range(c.NT):
        p = ti % 2
        ov = otl[:, p * 1024:(p + 1) * 1024]
        fv = f1l[:, p * D:(p + 1) * D]
        xv = x1l[:, p * D:(p + 1) * D]
        outv = osb[:, p * D:(p + 1) * D]
        o_ot = ph.dma("sync", lambda s, ti=ti, ov=ov: s.dma_start(
            out=ov.rearrange("q (a t) -> q a t", t=128),
            in_=c.OUTT[0:D, ti * 128:(ti + 1) * 128].rearrange(
                "(a q) t -> q a t", q=128)),
            waits=[last.get(("ot", p))], ctr=otc[p])
        o_f1 = ph.dma("sync", lambda s, ti=ti, fv=fv: s.dma_start(
            out=fv, in_=c.F1[ti * 128:(ti + 1) * 128, :]),
            waits=[last.get(("f1", p))], ctr=f1c[p])
        o_x1 = ph.dma("sync", lambda s, ti=ti, xv=xv: s.dma_start(
            out=xv, in_=c.X1[ti * 128:(ti + 1) * 128, :]),
            waits=[last.get(("x1", p))], ctr=x1c[p])
        o_sil = ph.op("scalar", lambda s, fv=fv: s.activation(
            out=g1[:], in_=fv, func=AF.Silu), waits=[o_f1, last.get("g1")], ctr=c.sq)
        for og in range(2):
            bk = p * 2 + og

            def mms(t, ov=ov, og=og, bk=bk):
                for ic in range(8):
                    t.matmul(c.pb[bk], ov[:, ic * 128:(ic + 1) * 128],
                             Wo1sb[:, ic * D + og * 512:ic * D + (og + 1) * 512],
                             start=(ic == 0), stop=False)
                return t.matmul(c.pb[bk], c.ones_row[0:1, 0:128],
                                c.bo1_row[0:1, og * 512:(og + 1) * 512],
                                start=False, stop=True)
            o_mm = ph.op("tensor", mms,
                         waits=[o_ot, o_w, last.get(("pb", bk))], ctr=c.tq)

            def vout(v, og=og, bk=bk, outv=outv, xv=xv):
                selfsync(c, "vector", v, v.tensor_tensor(
                    out=tsb[:, og * 512:(og + 1) * 512],
                    in0=g1[:, og * 512:(og + 1) * 512], in1=c.pb[bk], op=ALU.mult))
                return v.tensor_tensor(out=outv[:, og * 512:(og + 1) * 512],
                                       in0=tsb[:, og * 512:(og + 1) * 512],
                                       in1=xv[:, og * 512:(og + 1) * 512], op=ALU.add)
            o_vo = ph.op("vector", vout,
                         waits=[o_mm, o_sil, o_x1, last.get(("ost", p))], ctr=c.vq)
            last[("pb", bk)] = o_vo
            last["g1"] = o_vo
            last[(ti, og)] = o_vo
        o_st = ph.dma("sync", lambda s, ti=ti, outv=outv: s.dma_start(
            out=c.out[ti * 128:(ti + 1) * 128, :], in_=outv),
            waits=[last[(ti, 0)], last[(ti, 1)]], ctr=ostc[p])
        last[("ost", p)] = o_st
        last[("ot", p)] = o_mm
        last[("f1", p)] = o_vo
        last[("x1", p)] = o_vo
    ph.emit()



def sctr(c, name):
    if name not in c.ctrs:
        c.ctrs[name] = Ctr(c.eg(c.nc.semaphore(name)))
    return c.ctrs[name]


def make_ctx(B, S, dbg=()):
    T = B * S
    c = SimpleNamespace()
    c.B, c.S, c.T = B, S, T
    c.NT = T // 128
    c.NCH = S // 128
    c.CAP = 2 * (T // E)
    c.NTG = T // 512
    nc = bass.Bass("TRN2", target_bir_lowering=False, detect_race_conditions=False)
    c.nc = nc

    def din(name, shape):
        return nc.dram_tensor(name, shape, F32, kind="ExternalInput").ap()

    def dscr(name, shape, dt=FP16):
        kind = "ExternalOutput" if name in dbg else "Internal"
        return nc.dram_tensor(name, shape, dt, kind=kind).ap()
    c.ins = {}
    for nm, shape in [
        ("x", [T, D]), ("Wq0", [D, D]), ("bq0", [D]), ("Wk0", [D, D]), ("bk0", [D]),
        ("Wv0", [D, D]), ("bv0", [D]), ("Wb0", [D, D]), ("bb0", [D]),
        ("Wa0", [D, D]), ("ba0", [D]),
        ("Wr", [D, E]), ("br", [E]), ("W1", [E, D, MH]), ("b1", [E, MH]),
        ("W2", [E, MH, D]), ("b2", [E, D]),
        ("ctx", [CTXT, CD]), ("Wq1", [D, D]), ("Wk1", [CD, D]), ("Wv1", [CD, D]),
        ("Wo1", [D, D]), ("bo1", [D]),
        ("GW", [FH, D]), ("UW", [FH, D]), ("DW", [D, FH]),
        ("s_in", [1]), ("s_in_d", [1]),
    ]:
        c.ins[nm] = din(nm, shape)
    c.out = nc.dram_tensor("OUT", [T, D], F32, kind="ExternalOutput").ap()
    c.W16 = {w + "": dscr(w + "16", [D, D]) for w in ("Wq0", "Wk0", "Wv0", "Wb0", "Wa0")}
    c.XN = dscr("XN", [T, D])
    c.QT = dscr("QT", [D, T])
    c.KT = dscr("KT", [D, T])
    c.KTM = dscr("KTM", [T, D])
    c.VTM = dscr("VTM", [T, D])
    c.VBS = dscr("VBS", [T, D])
    c.YATT = dscr("YATT", [T, D])
    c.W2S = dscr("W2S", [E, MH, D])
    c.XG = dscr("XG", [E * c.CAP, D])
    c.YG = dscr("YG", [E * c.CAP, D])
    c.X1 = dscr("X1", [T, D], F32)
    c.XN1 = dscr("XN1", [T, D])
    c.XQ = dscr("XQ", [T, D])
    c.QT1 = dscr("QT1", [D, T])
    c.CTX16 = dscr("CTX16", [CTXT, CD])
    c.PSCR = dscr("PSCR", [2 * T, CTXT], FP16)
    c.OUTT = dscr("OUTT", [D, T])
    c.GQ = dscr("GQ", [FH, D])
    c.UQ = dscr("UQ", [FH, D])
    c.DQ = dscr("DQ", [D, FH])
    c.HQ = dscr("HQ", [T, FH])
    c.F1 = dscr("F1", [T, D])
    c.DBGS = dscr("DBGS", [128, 3 * c.NT * NH], F32)
    return c


def build(B=4, S=2048, upto=7, dbg=()):
    c = make_ctx(B, S, dbg)
    nc = c.nc
    with contextlib.ExitStack() as gx:
        c.eg = gx.enter_context
        eg = c.eg
        c.ident = eg(nc.sbuf_tensor("ident", [128, 128], FP16))
        c.identf = eg(nc.sbuf_tensor("identf", [128, 128], F32))
        c.iotav = eg(nc.sbuf_tensor("iotav", [128, 128], F32))
        c.madd = eg(nc.sbuf_tensor("madd", [128, 128], F32))
        c.uinc = eg(nc.sbuf_tensor("uinc", [128, 128], FP16))
        c.ustrf = eg(nc.sbuf_tensor("ustrf", [128, 128], F32))
        c.uincf = eg(nc.sbuf_tensor("uincf", [128, 128], F32))
        c.rowsel = eg(nc.sbuf_tensor("rowsel", [16, NH * 128], F32))
        c.rsel_t = eg(nc.sbuf_tensor("rsel_t", [16, NH * 128], F32))
        c.ones_row = eg(nc.sbuf_tensor("ones_row", [1, 1024], FP16))
        c.ones_rowf = eg(nc.sbuf_tensor("ones_rowf", [1, 128], F32))
        c.ones_colb = eg(nc.sbuf_tensor("ones_colb", [128, 1], FP16))
        c.ones8f = eg(nc.sbuf_tensor("ones8f", [8, 128], F32))
        c.MLA = eg(nc.sbuf_tensor("MLA", [128, c.NT * NH], F32))
        c.BMS = eg(nc.sbuf_tensor("BMS", [128, c.NT * NH], F32))
        c.MASKS = eg(nc.sbuf_tensor("MASKS", [128, c.NT * E], FP16))
        c.MASKSF = eg(nc.sbuf_tensor("MASKSF", [128, c.NT * E], F32))
        c.DIDX = eg(nc.sbuf_tensor("DIDX", [128, c.NT], I32))
        c.S_all = eg(nc.sbuf_tensor("S_all", [128, B * NH * 32], F32))
        c.ecap_col = eg(nc.sbuf_tensor("ecap_col", [8, 1], F32))
        c.ecapmax = eg(nc.sbuf_tensor("ecapmax", [128, 8], F32))
        c.bias_rows = eg(nc.sbuf_tensor("bias_rows", [1, 4 * 1024], FP16))
        c.br_row = eg(nc.sbuf_tensor("br_row", [1, 8], FP16))
        c.bo1_row = eg(nc.sbuf_tensor("bo1_row", [1, 1024], FP16))
        c.bq_cols = eg(nc.sbuf_tensor("bq_cols", [128, 16], F32))
        c.gs_cols = eg(nc.sbuf_tensor("gs_cols", [128, 12], F32))
        c.sc_sm = eg(nc.sbuf_tensor("sc_sm", [1, 8], F32))
        c.cnt_sb = eg(nc.sbuf_tensor("cnt_sb", [8, 64], F32))
        c.constcols = eg(nc.sbuf_tensor("constcols", [128, 4], F32))
        c.PS = eg(nc.psum_tensor("PS", [128, 4096], F32))
        c.pb = [c.PS[:, i * 512:(i + 1) * 512] for i in range(8)]
        c.ctrs = {n: Ctr(eg(nc.semaphore(n))) for n in
                  ["ldA", "ldB", "ldC", "stA", "tq", "vq", "sq", "gq"]}
        c.self_ctrs = {n: Ctr(eg(nc.semaphore("self_" + n)))
                       for n in ("vector", "scalar", "gpsimd", "tensor")}
        for n, ct in c.self_ctrs.items():
            c.ctrs["self_" + n] = ct
        for k, v in c.ctrs.items():
            setattr(c, k if k != "sq" else "sq", v)
        c.ldA, c.ldB, c.ldC, c.stA = (c.ctrs[k] for k in ("ldA", "ldB", "ldC", "stA"))
        c.tq, c.vq, c.sq, c.gq = (c.ctrs[k] for k in ("tq", "vq", "sq", "gq"))
        c.block = eg(nc.Block())

        p0_consts(c)
        if upto >= 1:
            barrier(c, "b1")
            emit_layernorm(c, "p1", c.ins["x"], c.XN)
        if upto >= 2:
            barrier(c, "b2")
            p2_proj(c)
        if upto >= 3:
            barrier(c, "b3")
            p3_deltanet(c)
        if upto >= 4:
            barrier(c, "b4")
            p4_moe(c)
            barrier(c, "b4c")
            p4c_gate(c)
        if upto >= 5:
            barrier(c, "b5")
            with contextlib.ExitStack() as p5s:
                p5_prep(c, p5s.enter_context)
                if upto >= 6:
                    barrier(c, "b6")
                    p6_attn(c)
        if upto >= 6.5:
            barrier(c, "b65")
            p65_ffn(c)
        if upto >= 7:
            barrier(c, "b7")
            p7_final(c)
        if "DBGS" in dbg:
            barrier(c, "bd")
            pdbg(c)
        barrier(c, "bf")
    return c


def pdbg(c):
    ph = Phase(c, "pdbg")
    n = c.NT * NH
    o1 = ph.dma("gpsimd", lambda g: g.dma_start(out=c.DBGS[0:128, 0:n], in_=c.MLA[:]),
                ctr=c.stA)
    o2 = ph.dma("gpsimd", lambda g: g.dma_start(out=c.DBGS[0:128, n:2 * n], in_=c.BMS[:]),
                ctr=c.stA)
    o3 = ph.dma("gpsimd", lambda g: g.dma_start(
        out=c.DBGS[0:128, 2 * n:2 * n + c.NT * E], in_=c.MASKSF[:]), ctr=c.stA)
    ph.emit()


_NAME_MAP = {
    "x": "L0_attn_x", "Wq0": "L0_attn_Wq", "bq0": "L0_attn_bq",
    "Wk0": "L0_attn_Wk", "bk0": "L0_attn_bk", "Wv0": "L0_attn_Wv",
    "bv0": "L0_attn_bv", "Wb0": "L0_attn_Wbeta", "bb0": "L0_attn_bbeta",
    "Wa0": "L0_attn_Walpha", "ba0": "L0_attn_balpha",
    "Wr": "L0_ffn_Wr", "br": "L0_ffn_br", "W1": "L0_ffn_W1", "b1": "L0_ffn_b1",
    "W2": "L0_ffn_W2", "b2": "L0_ffn_b2",
    "ctx": "L1_attn_context", "Wq1": "L1_attn_Wq", "Wk1": "L1_attn_Wk",
    "Wv1": "L1_attn_Wv", "Wo1": "L1_attn_Wout", "bo1": "L1_attn_bout",
    "GW": "L1_ffn_gate_w", "UW": "L1_ffn_up_w", "DW": "L1_ffn_down_w",
    "s_in": "L1_ffn_s_in", "s_in_d": "L1_ffn_s_in_d" + "own",
}


def kernel(**inputs):
    NCORES = 8
    B, S = 4, 2048
    T = B * S
    x = np.ascontiguousarray(np.asarray(inputs["L0_attn_x"], np.float32))
    shards = [np.ascontiguousarray(x[i * B:(i + 1) * B].reshape(T, D))
              for i in range(NCORES)]
    common = {}
    for knl, kref in _NAME_MAP.items():
        if knl == "x":
            continue
        v = np.ascontiguousarray(np.asarray(inputs[kref], np.float32))
        if knl == "ctx":
            v = v.reshape(-1, CD)
        common[knl] = v
    c = build(B=B, S=S, upto=7)
    in_maps = [dict(common, x=shards[i]) for i in range(NCORES)]
    res = run_bass_kernel_spmd(c.nc, in_maps, core_ids=list(range(NCORES)))
    out = np.concatenate([np.asarray(r["OUT"], np.float32).reshape(B, S, D)
                          for r in res.results], axis=0)
    return out
